# version 06
# baseline (speedup 1.0000x reference)
import numpy as np
import concourse.bass as bass
import concourse.mybir as mybir
from concourse.bass_utils import run_bass_kernel_spmd

F32 = mybir.dt.float32
BF16 = mybir.dt.bfloat16
AF = mybir.ActivationFunctionType
ALU = mybir.AluOpType
AX = mybir.AxisListType

D, T, H, HD = 1024, 2048, 16, 64
NCH, NTT = 8, 4
NE, HID, SSD, CONVK = 8, 4096, 32, 4
NEX = 9

STAGES = ("attn", "ffn1", "mamba1", "mamba2", "conv", "ffn2")

CN_INVD, CN_EPS, CN_BNS, CN_NEG1, CN_HALF, CN_ONE, CN_NK, CN_NINF = range(8)


def _rope_perm():
    perm = []
    for h in range(H):
        perm += [h * HD + 2 * j for j in range(HD // 2)]
        perm += [h * HD + 2 * j + 1 for j in range(HD // 2)]
    return np.array(perm)


def _cos_sin():
    freqs = 1.0 / 1000.0 ** (np.arange(0, HD, 2, dtype=np.float64) / HD)
    ang = np.arange(T, dtype=np.float64)[:, None] * freqs[None, :]
    cos = np.tile(np.cos(ang).T, (4, 1)).astype(np.float32)
    sin = np.tile(np.sin(ang).T, (4, 1)).astype(np.float32)
    sgn = np.ones((128, 1), np.float32)
    sgn[32:64] = -1.0
    sgn[96:128] = -1.0
    return cos, sgn * sin


def _bias_tile(b):
    n = b.shape[0]
    return np.ascontiguousarray(b.reshape(n // 128, 128).T).astype(np.float32)


def host_prep(inp):
    import ml_dtypes
    bf = ml_dtypes.bfloat16
    g = {}
    perm = _rope_perm()
    g["wqT"] = np.ascontiguousarray(inp["L0_attn_a_wq"].T[:, perm])
    g["wkp"] = np.ascontiguousarray(inp["L0_attn_a_wk"][:, perm[:HD]])
    g["wv"] = np.ascontiguousarray(inp["L0_attn_a_wv"])
    g["woT"] = np.ascontiguousarray(inp["L0_attn_a_wo"].T)

    w1 = [np.ascontiguousarray(inp["L0_ffn_shared_w1"].T)]
    w2 = [np.ascontiguousarray(inp["L0_ffn_shared_w2"].T)]
    for e in range(NE):
        w1.append(np.ascontiguousarray(inp["L0_ffn_experts_w1"][e].T))
        w2.append(np.ascontiguousarray(inp["L0_ffn_experts_w2"][e].T))
    g["ffn_w1"] = np.stack(w1)
    g["ffn_w2"] = np.stack(w2)
    gw = inp["L0_ffn_gate_w"].T
    g["gateT"] = np.ascontiguousarray(
        gw.reshape(NCH, 128, NE).transpose(1, 0, 2).reshape(128, NCH * NE))

    order = []
    for c in range(NCH):
        order += list(range(128 * c, 128 * (c + 1)))
        order += list(range(D + 128 * c, D + 128 * (c + 1)))
    for p in ("m1", "m2"):
        pre = f"L0_attn_b_{p}_"
        g[p + "_inw"] = np.ascontiguousarray(inp[pre + "in_w"][:, order])
        g[p + "_inb"] = _bias_tile(inp[pre + "in_b"][order])
        g[p + "_convw"] = np.ascontiguousarray(
            inp[pre + "conv_w"][:, 0, :].reshape(NCH, 128, CONVK)
            .transpose(1, 0, 2).reshape(128, NCH * CONVK))
        g[p + "_convb"] = _bias_tile(inp[pre + "conv_b"])
        g[p + "_ssmw"] = np.ascontiguousarray(inp[pre + "ssm_w"])
        g[p + "_ssmb"] = np.ascontiguousarray(
            inp[pre + "ssm_b"].reshape(SSD, 1)).astype(np.float32)
        g[p + "_A"] = np.ascontiguousarray(inp[pre + "A"])
        g[p + "_AT"] = np.ascontiguousarray(inp[pre + "A"].T)
        g[p + "_B"] = np.ascontiguousarray(inp[pre + "B"].reshape(SSD, 1))
        g[p + "_C"] = np.ascontiguousarray(inp[pre + "C"].reshape(SSD, 1))
        g[p + "_D"] = _bias_tile(inp[pre + "D"])
        g[p + "_outw"] = np.ascontiguousarray(inp[pre + "out_w"])
        g[p + "_outb"] = _bias_tile(inp[pre + "out_b"])

    g["c1wT"] = np.ascontiguousarray(inp["L0_attn_b_c1_w"].transpose(2, 1, 0))
    g["c1b"] = np.ascontiguousarray(
        inp["L0_attn_b_c1_b"].reshape(4, 128).T).astype(np.float32)
    g["c4wT"] = np.ascontiguousarray(inp["L0_attn_b_c4_w"][:, :, 0].T)
    g["c5wT"] = np.ascontiguousarray(inp["L0_attn_b_c5_w"].transpose(2, 1, 0))

    cos, s2 = _cos_sin()
    g["cosq"] = (cos / 8.0).astype(bf)
    g["s2q"] = (s2 / 8.0).astype(bf)
    g["cosk"] = cos.astype(bf)
    g["s2k"] = s2.astype(bf)
    g["ident_f"] = np.eye(128, dtype=np.float32)
    g["ident_bf"] = np.eye(128, dtype=np.float32).astype(bf)
    sel = np.zeros((128, 128), np.float32)
    for b in range(32):
        for js in range(4):
            sel[32 * js + b, 4 * b + js] = 1.0
    g["sel"] = sel
    g["ones_col_f"] = np.ones((128, 1), np.float32)
    g["ones_col_bf"] = np.ones((128, 1), np.float32).astype(bf)
    g["ones_row_bf"] = np.ones((1, 128), np.float32).astype(bf)
    cvals = [1.0 / D, 1e-5, float(1.0 / np.sqrt(np.float32(1.00001))), -1.0,
             0.5, 1.0, -1.02, -30000.0]
    g["CN"] = np.tile(np.array(cvals, np.float32)[None, :], (128, 1))
    return g


SHARED_NAMES = (
    ["wqT", "wkp", "wv", "woT", "ffn_w1", "ffn_w2", "gateT",
     "c1wT", "c1b", "c4wT", "c5wT"]
    + [f"{p}_{s}" for p in ("m1", "m2") for s in
       ("inw", "inb", "convw", "convb", "ssmw", "ssmb", "A", "AT", "B", "C",
        "D", "outw", "outb")])
CONST_NAMES = ("cosq", "s2q", "cosk", "s2k", "ident_f", "ident_bf", "sel",
               "ones_col_f", "ones_col_bf", "ones_row_bf", "CN")
CONST_BF = {"cosq", "s2q", "cosk", "s2k", "ident_bf", "ones_col_bf",
            "ones_row_bf"}


class Prog:
    SEMS = ("dma", "dgp", "pe", "ve", "se", "po", "ys")

    def __init__(self):
        self.ops = []
        self.c = {s: 0 for s in self.SEMS}
        self.phase = "init"

    ESEM = {"vector": "ve", "scalar": "se", "gpsimd": "po"}

    def op(self, eng, fn, wait=(), inc=()):
        waits = []
        for w in wait:
            if isinstance(w, tuple):
                if w[1] > 0:
                    waits.append(w)
            elif self.c[w] > 0:
                waits.append((w, self.c[w]))
        es = self.ESEM.get(eng)
        es_prior = self.c[es] if es is not None else 0
        incs = []
        fence = None
        isdma = False
        for i in inc:
            s, n = i if isinstance(i, tuple) else (i, 1)
            if s in ("dma", "dgp"):
                isdma = True
                if s == "dma" and eng == "sync":
                    fence = ("dma", "ys")
                if s == "dgp" and eng == "gpsimd":
                    fence = ("dgp", "po")
            self.c[s] += n
            incs.append((s, n))
        if es is not None and not isdma:
            if es_prior > 0:
                waits.append((es, es_prior))
            if not any(s == es for s, _ in incs):
                self.c[es] += 1
                incs.append((es, 1))
        self.ops.append((eng, waits, fn, incs, self.phase))
        if fence is not None:
            fs, ms = fence
            selfw = [(fs, self.c[fs])]
            if ms == "po" and self.c["po"] > 0:
                selfw.append(("po", self.c["po"]))
            self.c[ms] += 1
            self.ops.append((eng, selfw, lambda e: e.nop(), [(ms, 1)],
                             self.phase))

    def val(self, sem):
        return self.c[sem]

    def all_waits(self):
        return [(s, self.c[s]) for s in self.SEMS]


def _ncols(ap):
    if len(ap.shape) == 3:
        return ap.shape[1] * ap.shape[2]
    return ap.shape[1]


def build_program(stages=STAGES, taps=()):
    nc = bass.Bass("TRN2", target_bir_lowering=False)

    def din(name, shape, dtype=F32):
        return nc.dram_tensor(name, list(shape), dtype, kind="ExternalInput").ap()

    x_in = din("x", (T, D))
    out = nc.dram_tensor("out", [T, D], F32, kind="ExternalOutput").ap()
    tap_outs = {tp: nc.dram_tensor("tap_" + tp, [128, 32256 + NCH * T], F32,
                                   kind="ExternalOutput").ap() for tp in taps}

    wqT = din("wqT", (D, D))
    wkp = din("wkp", (D, HD))
    wv_d = din("wv", (D, HD))
    woT = din("woT", (D, D))
    ffn_w1 = din("ffn_w1", (NEX, D, HID))
    ffn_w2 = din("ffn_w2", (NEX, HID, D))
    gateT = din("gateT", (128, NCH * NE))
    mbp = {}
    for p in ("m1", "m2"):
        mbp[p] = {
            "inw": din(p + "_inw", (D, 2 * D)),
            "inb": din(p + "_inb", (128, 16)),
            "convw": din(p + "_convw", (128, NCH * CONVK)),
            "convb": din(p + "_convb", (128, NCH)),
            "ssmw": din(p + "_ssmw", (D, SSD)),
            "ssmb": din(p + "_ssmb", (SSD, 1)),
            "A": din(p + "_A", (SSD, SSD)),
            "AT": din(p + "_AT", (SSD, SSD)),
            "B": din(p + "_B", (SSD, 1)),
            "C": din(p + "_C", (SSD, 1)),
            "D": din(p + "_D", (128, NCH)),
            "outw": din(p + "_outw", (D, D)),
            "outb": din(p + "_outb", (128, NCH)),
        }
    c1wT = din("c1wT", (3, D, 512))
    c1b_d = din("c1b", (128, 4))
    c4wT = din("c4wT", (D, 512))
    c5wT = din("c5wT", (3, D, D))
    cshape = {"cosq": (128, T), "s2q": (128, T), "cosk": (128, T),
              "s2k": (128, T), "ident_f": (128, 128), "ident_bf": (128, 128),
              "sel": (128, 128), "ones_col_f": (128, 1),
              "ones_col_bf": (128, 1), "ones_row_bf": (1, 128),
              "CN": (128, 8)}
    cdram = {n: din("c_" + n, cshape[n], BF16 if n in CONST_BF else F32)
             for n in cshape}

    dscan = nc.dram_tensor("dscan", [2 * 33024], F32, kind="Internal").ap()

    XB = nc.alloc_sbuf_tensor("XB", [128, NCH * T], F32)

    def X(c):
        return XB.ap()[:, T * c: T * (c + 1)]

    CONST = {n: nc.alloc_sbuf_tensor("sb_" + n, list(cshape[n]),
                                     BF16 if n in CONST_BF else F32)
             for n in cshape}

    def C_(n):
        return CONST[n].ap()

    def CNs(i):
        return C_("CN")[:, i:i + 1]

    def CN1(i):
        return C_("CN")[0:1, i:i + 1]

    AR_BYTES = 129024
    ARH = nc.alloc_sbuf_tensor("AR", [128, AR_BYTES // 4], F32)

    def arf(off, cols, parts=128, p0=0):
        assert off % 4 == 0 and off + 4 * cols <= AR_BYTES, (off, cols)
        return ARH.ap()[p0:p0 + parts, off // 4: off // 4 + cols]

    def arb(off, cols, parts=128, p0=0):
        assert off % 4 == 0 and off + 2 * cols <= AR_BYTES + 2, (off, cols)
        w = (2 * cols + 3) // 4
        return ARH.ap()[p0:p0 + parts, off // 4: off // 4 + w] \
            .bitcast(BF16)[:, 0:cols]

    PSH = nc.alloc_psum_tensor("PS", [128, 4096], F32)
    PSAP = PSH.ap()

    def psn(bank, nbank=1, row0=0, nrow=128):
        return PSAP[row0:row0 + nrow, 512 * bank: 512 * (bank + nbank)]

    def psbf(bank, cols, row0=0, nrow=128):
        return PSAP[row0:row0 + nrow, 512 * bank: 512 * (bank + 1)] \
            .bitcast(BF16)[:, 0:cols]

    P = Prog()

    A_O, B_O, C_O = 0, 24576, 57344

    def xn_t(c):
        return arb(B_O + 4096 * c, T)

    def rowf(i, cols=T):
        return arf(A_O + 8192 * i, cols, parts=1)

    def p0():
        for n in cdram:
            P.op("sync", lambda e, a=CONST[n].ap(), b=cdram[n]:
                 e.dma_start(out=a, in_=b), inc=[("dma", 16)])
        P.op("vector", lambda e: e.memset(ARH.ap(), 0.0), inc=["ve"])
        mz = P.val("ve")
        for i in range(2):
            P.op("gpsimd", lambda e, i=i: e.dma_start(
                out=dscan[33024 * i: 33024 * (i + 1)]
                .rearrange("(p f) -> p f", p=128, f=258),
                in_=arf(C_O, 258)), wait=(["ve"] if i == 0 else []),
                inc=[("dgp", 16)])
        stg = [arf(C_O + 8192 + i * 4096, 1024) for i in range(2)]
        vml = {}
        for tc in range(16):
            b = tc % 2
            P.op("sync", lambda e, b=b, tc=tc: e.dma_start(
                out=stg[b], in_=x_in[128 * tc: 128 * (tc + 1), :]),
                wait=([("ve", vml.get(tc - 2, mz))]), inc=[("dma", 16)])
            for c in range(NCH):
                P.op("tensor", lambda e, b=b, c=c: e.transpose(
                    psn(c).bitcast(F32)[:, 0:128],
                    stg[b][:, 128 * c: 128 * (c + 1)], C_("ident_f")),
                    wait=(["ys", ("ve", vml.get(tc - 1, 0))] if c == 0 else []),
                    inc=(["pe"] if c == NCH - 1 else []))
            for c in range(NCH):
                P.op("vector", lambda e, c=c, tc=tc: e.tensor_copy(
                    X(c)[:, 128 * tc: 128 * (tc + 1)],
                    psn(c).bitcast(F32)[:, 0:128]),
                    wait=(["pe"] if c == 0 else []),
                    inc=(["ve"] if c == NCH - 1 else []))
            vml[tc] = P.val("ve")

    class WS:
        def __init__(self, stage_off, stage_cols, bf_off, bf_cols,
                     cast_eng="vector"):
            self.stage = arf(C_O + stage_off, stage_cols)
            step = ((2 * bf_cols + 63) // 64) * 64
            self.bfo = [arb(C_O + bf_off + i * step, bf_cols) for i in range(2)]
            self.cast_eng = cast_eng
            self.cs = {"vector": "ve", "scalar": "se", "gpsimd": "po"}[cast_eng]
            self.ready = [0, 0]
            self.freepe = [P.val("pe"), P.val("pe")]
            self.stage_free = P.val(self.cs)

        def load(self, src_ap, buf):
            n = _ncols(src_ap)
            three = len(src_ap.shape) == 3
            a = src_ap.shape[1] if three else 1
            P.op("sync", lambda e, s=src_ap, n=n, a=a, th=three: e.dma_start(
                out=(self.stage[:, 0:n].rearrange("p (a b) -> p a b", a=a)
                     if th else self.stage[:, 0:n]), in_=s),
                wait=[(self.cs, self.stage_free), ("pe", self.freepe[buf])],
                inc=[("dma", 16)])
            P.op(self.cast_eng, lambda e, b=buf, n=n: e.tensor_copy(
                self.bfo[b][:, 0:n], self.stage[:, 0:n]),
                wait=["ys"], inc=[self.cs])
            self.ready[buf] = P.val(self.cs)
            self.stage_free = P.val(self.cs)

        def bf(self, buf):
            return self.bfo[buf]

        def wait_use(self, buf):
            return [(self.cs, self.ready[buf])]

        def mark_done(self, buf):
            self.freepe[buf] = P.val("pe")

    def wsrc(dram2d, colsl):
        return dram2d[:, colsl].rearrange("(k p) c -> p k c", p=128)

    def ln(src, ready=None):
        sq = [arf(C_O + i * 8192, T) for i in range(2)]
        entry = P.all_waits()
        pe_marks = {}
        for c in range(NCH):
            w = ([("pe", pe_marks.get(c - 2, 0))] if c >= 2 else
                 ([] if ready else list(entry)))
            if ready:
                w = w + list(ready(c))
            P.op("scalar", lambda e, c=c: e.activation(
                sq[c % 2], src(c), AF.Square), wait=w, inc=["se"])
            for tt in range(NTT):
                P.op("tensor", lambda e, c=c, tt=tt: e.matmul(
                    psn(tt, 1, 0, 1), C_("ones_col_f"),
                    src(c)[:, 512 * tt: 512 * (tt + 1)],
                    start=(c == 0), stop=(c == NCH - 1)),
                    wait=((P.all_waits() if not ready else list(ready(c)) +
                           [("pe", P.val("pe"))])
                          if (c == 0 and tt == 0) else
                          (list(ready(c)) if ready and tt == 0 else [])))
            for tt in range(NTT):
                P.op("tensor", lambda e, c=c, tt=tt: e.matmul(
                    psn(4 + tt, 1, 0, 1), C_("ones_col_f"),
                    sq[c % 2][:, 512 * tt: 512 * (tt + 1)],
                    start=(c == 0), stop=(c == NCH - 1)),
                    wait=(["se"] if tt == 0 else []),
                    inc=(["pe"] if tt == NTT - 1 else []))
            pe_marks[c] = P.val("pe")
        r_m, r_s2, r_r = rowf(0), rowf(1), rowf(2)
        P.op("scalar", lambda e: e.activation(
            r_m, PSAP[0:1, 0:2048], AF.Copy, scale=CN1(CN_INVD)), wait=["pe"])
        P.op("scalar", lambda e: e.activation(
            r_s2, PSAP[0:1, 2048:4096], AF.Copy, scale=CN1(CN_INVD)),
            inc=["se"])
        P.op("vector", lambda e: e.tensor_mul(r_r, r_m, r_m), wait=["se"])
        P.op("vector", lambda e: e.tensor_sub(r_s2, r_s2, r_r), inc=["ve"])
        P.op("scalar", lambda e: e.activation(
            r_s2, r_s2, AF.Sqrt, bias=CN1(CN_EPS)), wait=["ve"], inc=["se"])
        P.op("vector", lambda e: e.reciprocal(r_r, r_s2), wait=["se"])
        P.op("vector", lambda e: e.tensor_mul(r_m, r_m, r_r), inc=["ve"])
        rb_r = arb(A_O + 8192, T, parts=1)
        rb_m = arb(A_O + 8192 + 4096, T, parts=1)
        P.op("scalar", lambda e: e.copy(rb_r, r_r), wait=["ve"])
        P.op("scalar", lambda e: e.copy(rb_m, r_m), inc=["se"])
        for tt in range(NTT):
            P.op("tensor", lambda e, tt=tt: e.matmul(
                psn(tt), C_("ones_row_bf"), rb_r[:, 512 * tt: 512 * (tt + 1)],
                start=True, stop=True), wait=(["se"] if tt == 0 else []))
            P.op("tensor", lambda e, tt=tt: e.matmul(
                psn(4 + tt), C_("ones_row_bf"), rb_m[:, 512 * tt: 512 * (tt + 1)],
                start=True, stop=True),
                inc=(["pe"] if tt == NTT - 1 else []))
        tmp = arf(C_O, T)
        for c in range(NCH):
            P.op("vector", lambda e, c=c: e.tensor_mul(
                tmp, src(c), PSAP[:, 0:2048]), wait=(["pe"] if c == 0 else []))
            P.op("vector", lambda e, c=c: e.tensor_sub(
                xn_t(c), tmp, PSAP[:, 2048:4096]),
                inc=(["ve"] if c == NCH - 1 else []))

    def attn():
        kaug = arb(C_O + 0, T, parts=65)
        va = arb(C_O + 4224, 16 * 66)
        scr = [arb(C_O + 6400 + i * 2048, 1024) for i in range(3)]
        WSO = 12544
        qs = [arb(C_O + 20736 + i * 4160, T, parts=65) for i in range(4)]
        ao = [arb(C_O + 37376 + i * 4096, T) for i in range(8)]
        av_sb = arb(C_O + 70144, 512, parts=64)
        ptil = [arb(A_O + 8192 + i * 2048, 1024) for i in range(2)]
        sqscr = arb(A_O + 12288, T, parts=64)
        negk = arf(A_O + 0, 1, parts=1)

        def qst(pp, hh):
            return qs[2 * (pp % 2) + hh]

        wkv = WS(WSO, 1024, WSO + 4096, 1024)
        wkv.load(wsrc(wkp, slice(0, HD)), 0)
        wkv.load(wsrc(wv_d, slice(0, HD)), 1)
        for tt in range(NTT):
            for kk in range(NCH):
                P.op("tensor", lambda e, kk=kk, tt=tt: e.matmul(
                    psn(tt, 1, 0, 64), wkv.bf(0)[:, 64 * kk: 64 * (kk + 1)],
                    xn_t(kk)[:, 512 * tt: 512 * (tt + 1)],
                    start=(kk == 0), stop=(kk == NCH - 1)),
                    wait=(wkv.wait_use(0) + P.all_waits()
                          if kk == 0 and tt == 0 else []),
                    inc=(["pe"] if kk == NCH - 1 and tt == NTT - 1 else []))
        for hf in range(2):
            sl = slice(1024 * hf, 1024 * (hf + 1))
            P.op("vector", lambda e, sl=sl: e.tensor_mul(
                scr[0][0:64, :], PSAP[0:64, 0:2048][:, sl], C_("cosk")[0:64, sl]),
                wait=(["pe"] if hf == 0 else []))
            P.op("vector", lambda e, sl=sl: e.tensor_mul(
                scr[1][0:64, :], PSAP[0:64, 0:2048][:, sl], C_("s2k")[0:64, sl]),
                inc=["ve"])
            P.op("gpsimd", lambda e: e.dma_start(
                out=scr[2][0:32, :], in_=scr[1][32:64, :]),
                wait=["ve"], inc=[("dgp", 16)])
            P.op("gpsimd", lambda e: e.dma_start(
                out=scr[2][32:64, :], in_=scr[1][0:32, :]), inc=[("dgp", 16)])
            P.op("vector", lambda e, sl=sl: e.tensor_add(
                kaug[0:64, sl], scr[0][0:64, :], scr[2][0:64, :]),
                wait=["po"], inc=["ve"])
        P.op("vector", lambda e: e.memset(rowf(0), 1.0), inc=["ve"])
        P.op("scalar", lambda e: e.copy(
            arb(A_O, T, parts=1), rowf(0)), wait=["ve"], inc=["se"])
        P.op("gpsimd", lambda e: e.dma_start(
            out=kaug[64:65, :], in_=arb(A_O, T, parts=1)), wait=["se"],
            inc=[("dgp", 16)])
        for tt in range(NTT):
            for kk in range(NCH):
                P.op("tensor", lambda e, kk=kk, tt=tt: e.matmul(
                    psn(4 + tt, 1, 0, 64), wkv.bf(1)[:, 64 * kk: 64 * (kk + 1)],
                    xn_t(kk)[:, 512 * tt: 512 * (tt + 1)],
                    start=(kk == 0), stop=(kk == NCH - 1)),
                    wait=(wkv.wait_use(1) if kk == 0 and tt == 0 else []),
                    inc=(["pe"] if kk == NCH - 1 and tt == NTT - 1 else []))
        wkv.mark_done(0)
        wkv.mark_done(1)
        P.op("scalar", lambda e: e.copy(sqscr, PSAP[0:64, 2048:4096]),
             wait=["pe"], inc=["se"])
        for kt in range(16):
            P.op("tensor", lambda e, kt=kt: e.transpose(
                psbf(0, 64), sqscr[:, 128 * kt: 128 * (kt + 1)],
                C_("ident_bf")[0:64, 0:64]),
                wait=(["se", "ve"] if kt == 0 else ["ve"]), inc=["pe"])
            P.op("vector", lambda e, kt=kt: e.tensor_copy(
                va[:, 66 * kt: 66 * kt + 64], psbf(0, 64)), wait=["pe"])
            P.op("vector", lambda e, kt=kt: e.memset(
                va[:, 66 * kt + 64: 66 * kt + 65], 1.0), inc=["ve"])
        P.op("scalar", lambda e: e.activation(
            sqscr, kaug[0:64, :], AF.Square), wait=["ve"], inc=["se"])
        for tt in range(NTT):
            P.op("tensor", lambda e, tt=tt: e.matmul(
                psn(tt, 1, 0, 1), C_("ones_col_bf")[0:64, :],
                sqscr[:, 512 * tt: 512 * (tt + 1)], start=True, stop=True),
                wait=(["se"] if tt == 0 else []),
                inc=(["pe"] if tt == NTT - 1 else []))
        P.op("scalar", lambda e: e.activation(
            rowf(0), PSAP[0:1, 0:2048], AF.Sqrt),
            wait=["pe", ("po", P.val("po"))], inc=["se"])
        P.op("vector", lambda e: e.reduce_max(
            rowf(2, 1), rowf(0), axis=AX.X), wait=["se"])
        P.op("vector", lambda e: e.tensor_scalar(
            negk, rowf(2, 1), CN1(CN_NK), None, ALU.mult), inc=["ve"])

        wq = WS(WSO, 1024, WSO + 4096, 1024)
        exp_mark = {}
        avsb_mark = 0
        bcfree_mark = 0
        pair_done_pe = {}
        rope_done_ve = {}
        qrdy = {}

        def qproj(pp):
            buf = pp % 2
            wq.load(wsrc(wqT, slice(128 * pp, 128 * (pp + 1))), buf)
            for tt in range(NTT):
                for kk in range(NCH):
                    P.op("tensor", lambda e, kk=kk, tt=tt, buf=buf: e.matmul(
                        psn(tt), wq.bf(buf)[:, 128 * kk: 128 * (kk + 1)],
                        xn_t(kk)[:, 512 * tt: 512 * (tt + 1)],
                        start=(kk == 0), stop=(kk == NCH - 1)),
                        wait=(wq.wait_use(buf)
                              + [("ve", rope_done_ve.get(pp - 1, 0))]
                              if kk == 0 and tt == 0 else []),
                        inc=(["pe"] if kk == NCH - 1 and tt == NTT - 1 else []))
            wq.mark_done(buf)
            return P.val("pe")

        def rope_q(pp, qpe):
            for hf in range(2):
                sl = slice(1024 * hf, 1024 * (hf + 1))
                P.op("vector", lambda e, sl=sl: e.tensor_mul(
                    scr[0], PSAP[:, 0:2048][:, sl], C_("cosq")[:, sl]),
                    wait=([("pe", qpe)] if hf == 0 else [])
                    + [("po", P.val("po"))])
                P.op("vector", lambda e, sl=sl: e.tensor_mul(
                    scr[1], PSAP[:, 0:2048][:, sl], C_("s2q")[:, sl]),
                    inc=["ve"])
                for blk in range(4):
                    so, do = [32, 0, 96, 64][blk], [0, 32, 64, 96][blk]
                    P.op("gpsimd", lambda e, so=so, do=do: e.dma_start(
                        out=scr[2][do:do + 32, :], in_=scr[1][so:so + 32, :]),
                        wait=(["ve"] if blk == 0 else []), inc=[("dgp", 16)])
                P.op("vector", lambda e: e.tensor_add(
                    scr[1], scr[0], scr[2]), wait=["po"], inc=["ve"])
                for hh in range(2):
                    P.op("gpsimd", lambda e, hh=hh, sl=sl, pp=pp: e.dma_start(
                        out=qst(pp, hh)[0:64, sl],
                        in_=scr[1][64 * hh: 64 * hh + 64, :]),
                        wait=["ve", ("pe", pair_done_pe.get(pp - 2, 0))],
                        inc=[("dgp", 16)])
            rope_done_ve[pp] = P.val("ve")

        def bounds(pp):
            for hh in range(2):
                P.op("scalar", lambda e, pp=pp, hh=hh: e.activation(
                    sqscr, qst(pp, hh)[0:64, :], AF.Square),
                    wait=["po"], inc=["se"])
                for tt in range(NTT):
                    P.op("tensor", lambda e, tt=tt: e.matmul(
                        psn(tt, 1, 0, 1), C_("ones_col_bf")[0:64, :],
                        sqscr[:, 512 * tt: 512 * (tt + 1)], start=True,
                        stop=True), wait=(["se"] if tt == 0 else []),
                        inc=(["pe"] if tt == NTT - 1 else []))
                P.op("scalar", lambda e: e.activation(
                    rowf(2), PSAP[0:1, 0:2048], AF.Sqrt),
                    wait=["pe", ("ve", P.val("ve"))], inc=["se"])
                P.op("vector", lambda e, pp=pp, hh=hh: e.tensor_scalar(
                    qst(pp, hh)[64:65, :], rowf(2), negk, CN1(CN_HALF),
                    ALU.mult, ALU.subtract), wait=["se"], inc=["ve"])
            qrdy[pp] = P.val("ve")

        def st_av(pp):
            nonlocal avsb_mark, bcfree_mark
            for hh in range(2):
                for qt in range(NTT):
                    nkt = 4 * (qt + 1)
                    for kt2 in range(nkt // 2):
                        for half in range(2):
                            kt = 2 * kt2 + half
                            P.op("tensor", lambda e, kt=kt, qt=qt, pp=pp,
                                 hh=hh, half=half: e.matmul(
                                     psn(4 + half),
                                     kaug[:, 128 * kt: 128 * (kt + 1)],
                                     qst(pp, hh)[:, 512 * qt: 512 * (qt + 1)],
                                     start=True, stop=True),
                                 wait=[("ve", qrdy[pp]),
                                       ("se", exp_mark.get("e", 0))],
                                 inc=["pe"])
                        st_pe = P.val("pe")
                        pt = ptil[kt2 % 2]
                        P.op("scalar", lambda e, pt=pt: e.activation(
                            pt, PSAP[:, 2048:3072], AF.Exp),
                            wait=[("pe", st_pe),
                                  ("po", exp_mark.get(("a", kt2 % 2), 0)),
                                  ("pe", exp_mark.get(("av", kt2 % 2), 0))],
                            inc=["se"])
                        exp_mark["e"] = P.val("se")
                        for half in range(2):
                            kt = 2 * kt2 + half
                            diag = kt >= 4 * qt
                            if diag:
                                P.op("gpsimd", lambda e, kt=kt, qt=qt, pt=pt,
                                     half=half: e.affine_select(
                                         pt[:, 512 * half: 512 * (half + 1)],
                                         pt[:, 512 * half: 512 * (half + 1)],
                                         pattern=[[1, 512]],
                                         compare_op=ALU.is_ge, fill=0.0,
                                         base=512 * qt - 128 * kt,
                                         channel_multiplier=-1),
                                     wait=["se"], inc=["po"])
                        exp_mark[("a", kt2 % 2)] = P.val("po")
                        for half in range(2):
                            kt = 2 * kt2 + half
                            diag = kt >= 4 * qt
                            P.op("tensor", lambda e, kt=kt, pt=pt, half=half,
                                 nkt=nkt: e.matmul(
                                     psn(6, 1, 0, 65),
                                     va[:, 66 * kt: 66 * kt + 65],
                                     pt[:, 512 * half: 512 * (half + 1)],
                                     start=(kt == 0), stop=(kt == nkt - 1)),
                                 wait=([("po", P.val("po"))] if diag
                                       else [("se", exp_mark["e"])])
                                 + ([("se", avsb_mark)] if kt == 0 else []),
                                 inc=["pe"])
                        exp_mark[("av", kt2 % 2)] = P.val("pe")
                    av_pe = P.val("pe")
                    P.op("vector", lambda e: e.reciprocal(
                        rowf(2, 512), PSAP[64:65, 3072:3584]),
                        wait=[("pe", av_pe)])
                    P.op("scalar", lambda e: e.copy(
                        arb(A_O + 20480, 512, parts=1), rowf(2, 512)),
                        wait=["ve"], inc=["se"])
                    P.op("tensor", lambda e: e.matmul(
                        psn(7, 1, 0, 64), C_("ones_row_bf")[:, 0:64],
                        arb(A_O + 20480, 512, parts=1), start=True, stop=True),
                        wait=["se", ("ve", bcfree_mark)], inc=["pe"])
                    P.op("scalar", lambda e: e.copy(
                        av_sb, PSAP[0:64, 3072:3584]), wait=["pe"], inc=["se"])
                    avsb_mark = P.val("se")
                    if hh == 0:
                        P.op("vector", lambda e, pp=pp, qt=qt: e.tensor_mul(
                            ao[pp][0:64, 512 * qt: 512 * (qt + 1)], av_sb,
                            PSAP[0:64, 3584:4096]), wait=["se"], inc=["ve"])
                    else:
                        P.op("vector", lambda e: e.tensor_mul(
                            scr[2][0:64, 0:512], av_sb,
                            PSAP[0:64, 3584:4096]),
                            wait=["se", ("po", P.val("po"))], inc=["ve"])
                        P.op("gpsimd", lambda e, pp=pp, qt=qt: e.dma_start(
                            out=ao[pp][64:128, 512 * qt: 512 * (qt + 1)],
                            in_=scr[2][0:64, 0:512]), wait=["ve"],
                            inc=[("dgp", 16)])
                    bcfree_mark = P.val("ve")
            pair_done_pe[pp] = P.val("pe")

        qpe = qproj(0)
        rope_q(0, qpe)
        bounds(0)
        for pp in range(8):
            if pp + 1 < 8:
                qpe = qproj(pp + 1)
                rope_q(pp + 1, qpe)
            st_av(pp)
            if pp + 1 < 8:
                bounds(pp + 1)

        wo = WS(WSO, 1024, WSO + 4096, 1024)
        wo.load(wsrc(woT, slice(0, 128)), 0)
        stt_mark = {}
        for m in range(NCH):
            buf = m % 2
            if m + 1 < NCH:
                wo.load(wsrc(woT, slice(128 * (m + 1), 128 * (m + 2))),
                        (m + 1) % 2)
            pb = (m % 2) * 4
            for tt in range(NTT):
                for kk in range(NCH):
                    P.op("tensor", lambda e, kk=kk, tt=tt, buf=buf, pb=pb:
                         e.matmul(psn(pb + tt),
                                  wo.bf(buf)[:, 128 * kk: 128 * (kk + 1)],
                                  ao[kk][:, 512 * tt: 512 * (tt + 1)],
                                  start=(kk == 0), stop=(kk == NCH - 1)),
                         wait=((wo.wait_use(buf)
                                + [("ve", P.val("ve")),
                                   ("po", P.val("po")),
                                   ("ve", stt_mark.get(m - 2, 0))])
                               if kk == 0 and tt == 0 else []),
                         inc=(["pe"] if kk == NCH - 1 and tt == NTT - 1 else []))
            wo.mark_done(buf)
            P.op("vector", lambda e, m=m, pb=pb: e.scalar_tensor_tensor(
                X(m), PSAP[:, 2048 * (m % 2): 2048 * (m % 2) + 2048],
                CNs(CN_ONE), X(m), ALU.mult, ALU.add),
                wait=["pe"], inc=["ve"])
            stt_mark[m] = P.val("ve")
        attn._chunk_marks = dict(stt_mark)

    def ffn():
        sh = [arb(C_O + 4096 * m, T) for m in range(NCH)]
        W1_O = 32768
        w2q = arb(C_O + 49280, NCH * 1024)
        w2stage = arf(A_O + 16384, T)
        gf8 = arb(A_O, T, parts=8)
        grow = arb(A_O + 4096, T, parts=1)
        gate_bc = arb(A_O + 8192, T)
        gsc = [arf(A_O + 12288 + i * 64, 8) for i in range(4)]
        gcol = [arf(A_O + 12544 + i * 16, 1) for i in range(4)]
        gwb = arb(A_O + 12672, NCH * NE)
        gstage = arf(A_O + 12928, NCH * NE)

        P.op("sync", lambda e: e.dma_start(out=gstage, in_=gateT),
             wait=P.all_waits(), inc=[("dma", 16)])
        P.op("vector", lambda e: e.tensor_copy(gwb, gstage), wait=["ys"],
             inc=["ve"])
        for tc in range(16):
            for kk in range(NCH):
                P.op("tensor", lambda e, tc=tc, kk=kk: e.matmul(
                    PSAP[:, 8 * tc: 8 * tc + 8],
                    xn_t(kk)[:, 128 * tc: 128 * (tc + 1)],
                    gwb[:, 8 * kk: 8 * (kk + 1)],
                    start=(kk == 0), stop=(kk == NCH - 1)),
                    wait=(["ve"] + P.all_waits() if tc == 0 and kk == 0 else []),
                    inc=(["pe"] if kk == NCH - 1 else []))
        tr_mark = 0
        for tc in range(16):
            lg = PSAP[:, 8 * tc: 8 * tc + 8]
            P.op("vector", lambda e, lg=lg: e.reduce_max(
                gcol[0], lg, axis=AX.X), wait=(["pe"] if tc == 0 else []))
            P.op("vector", lambda e, lg=lg: e.tensor_scalar(
                gsc[0], lg, gcol[0], None, ALU.is_equal))
            P.op("vector", lambda e, lg=lg: e.scalar_tensor_tensor(
                gsc[1], gsc[0], CNs(CN_NINF), lg, ALU.mult, ALU.add))
            P.op("vector", lambda e: e.reduce_max(gcol[1], gsc[1], axis=AX.X))
            P.op("vector", lambda e: e.tensor_sub(gcol[2], gcol[0], gcol[1]),
                 inc=["ve"])
            P.op("scalar", lambda e: e.activation(
                gcol[3], gcol[2], AF.Sigmoid), wait=["ve"], inc=["se"])
            P.op("vector", lambda e, lg=lg: e.tensor_scalar(
                gsc[2], lg, gcol[1], None, ALU.is_equal), wait=["se"])
            P.op("vector", lambda e: e.tensor_scalar(
                gsc[3], gsc[2], gcol[3], CNs(CN_NEG1), ALU.mult, ALU.mult))
            P.op("vector", lambda e: e.tensor_scalar(
                gsc[0], gsc[0], gcol[3], None, ALU.mult))
            P.op("vector", lambda e: e.tensor_add(gsc[0], gsc[0], gsc[2]))
            P.op("vector", lambda e: e.tensor_add(gsc[0], gsc[0], gsc[3]))
            P.op("vector", lambda e: e.tensor_copy(
                gsc[1].bitcast(BF16)[:, 0:8], gsc[0]), inc=["ve"])
            P.op("tensor", lambda e: e.transpose(
                psbf(4, 128, 0, 8), gsc[1].bitcast(BF16)[:, 0:8],
                C_("ident_bf")), wait=["ve", ("ve", tr_mark)], inc=["pe"])
            P.op("vector", lambda e, tc=tc: e.tensor_copy(
                gf8[:, 128 * tc: 128 * (tc + 1)], psbf(4, 128, 0, 8)),
                wait=["pe"], inc=["ve"])
            tr_mark = P.val("ve")

        w1 = WS(W1_O, 2048, W1_O + 8192, 2048)
        silu_mark = {}
        stt_mark = {}
        w2cast = [P.val("ve")]
        for ei in range(NEX):
            if ei >= 1:
                P.op("gpsimd", lambda e, ei=ei: e.dma_start(
                    out=grow, in_=gf8[ei - 1:ei, :]),
                    wait=["ve", "pe", "po"], inc=[("dgp", 16)])
                for tt in range(NTT):
                    P.op("tensor", lambda e, tt=tt: e.matmul(
                        psn(4 + tt), C_("ones_row_bf"),
                        grow[:, 512 * tt: 512 * (tt + 1)], start=True,
                        stop=True),
                        wait=(["po", ("se", silu_mark.get("last", 0)),
                               ("ve", stt_mark.get("last", 0))]
                              if tt == 0 else []),
                        inc=(["pe"] if tt == NTT - 1 else []))
                P.op("vector", lambda e: e.tensor_copy(
                    gate_bc, PSAP[:, 2048:4096]), wait=["pe"], inc=["ve"])
            gbc_ve = P.val("ve")
            for q in range(4):
                w2pe = stt_mark.get("w2pe", 0)

                def w1load(g, q=q, ei=ei):
                    w1.load(wsrc(ffn_w1[ei],
                                 slice(1024 * q + 256 * g,
                                       1024 * q + 256 * (g + 1))), g % 2)

                def w2load(g2, q=q, ei=ei):
                    P.op("sync", lambda e, g2=g2, ei=ei, q=q: e.dma_start(
                        out=w2stage.rearrange("p (a b) -> p a b", a=2),
                        in_=ffn_w2[ei][1024 * q + 256 * g2:
                                       1024 * q + 256 * (g2 + 1), :]
                        .rearrange("(k p) c -> p k c", p=128)),
                        wait=[("ve", w2cast[0]), ("pe", w2pe)],
                        inc=[("dma", 16)])
                    P.op("vector", lambda e, g2=g2: e.tensor_copy(
                        w2q[:, 2048 * g2: 2048 * (g2 + 1)], w2stage),
                        wait=["ys"], inc=["ve"])
                    w2cast[0] = P.val("ve")

                w1load(0)
                w1load(1)
                w2load(0)
                w2load(1)
                for m in range(NCH):
                    g = m // 2
                    buf = g % 2
                    loc = m % 2
                    if loc == 1:
                        if g + 1 < 4:
                            w1load(g + 1)
                        if g < 2:
                            w2load(g + 2)
                    pb = (m % 2) * 4
                    for tt in range(NTT):
                        for kk in range(NCH):
                            P.op("tensor", lambda e, kk=kk, tt=tt, buf=buf,
                                 loc=loc, pb=pb: e.matmul(
                                     psn(pb + tt),
                                     w1.bf(buf)[:, 256 * kk + 128 * loc:
                                                256 * kk + 128 * (loc + 1)],
                                     xn_t(kk)[:, 512 * tt: 512 * (tt + 1)],
                                     start=(kk == 0), stop=(kk == NCH - 1)),
                                 wait=((w1.wait_use(buf)
                                        + [("se", silu_mark.get((q, m - 2), 0)),
                                           ("ve", gbc_ve),
                                           ("ve", stt_mark.get("last", 0))])
                                       if kk == 0 and tt == 0 else []),
                                 inc=(["pe"] if kk == NCH - 1 and tt == NTT - 1
                                      else []))
                    if loc == 1:
                        w1.mark_done(buf)
                    P.op("scalar", lambda e, m=m, pb=pb: e.activation(
                        sh[m], PSAP[:, 2048 * (m % 2): 2048 * (m % 2) + 2048],
                        AF.Silu), wait=["pe"], inc=["se"])
                    silu_mark[(q, m)] = P.val("se")
                    if ei >= 1:
                        P.op("vector", lambda e, m=m: e.tensor_mul(
                            sh[m], sh[m], gate_bc), wait=["se"], inc=["ve"])
                silu_mark["last"] = P.val("se")
                pomark = P.val("ve")
                w2rdy = w2cast[0]
                for m2 in range(NCH):
                    pb = (m2 % 2) * 4
                    for tt in range(NTT):
                        for k2 in range(NCH):
                            P.op("tensor", lambda e, m2=m2, k2=k2, tt=tt,
                                 pb=pb: e.matmul(
                                     psn(pb + tt),
                                     w2q[:, 1024 * k2 + 128 * m2:
                                         1024 * k2 + 128 * (m2 + 1)],
                                     sh[k2][:, 512 * tt: 512 * (tt + 1)],
                                     start=(k2 == 0), stop=(k2 == NCH - 1)),
                                 wait=(([("ve", w2rdy), ("ve", pomark),
                                         ("se", silu_mark["last"]),
                                         ("ve", stt_mark.get(m2 - 2, 0))])
                                       if k2 == 0 and tt == 0 else []),
                                 inc=(["pe"] if k2 == NCH - 1 and tt == NTT - 1
                                      else []))
                    P.op("vector", lambda e, m2=m2, pb=pb:
                         e.scalar_tensor_tensor(
                             X(m2),
                             PSAP[:, 2048 * (m2 % 2): 2048 * (m2 % 2) + 2048],
                             CNs(CN_ONE), X(m2), ALU.mult, ALU.add),
                         wait=["pe"], inc=["ve"])
                    stt_mark[m2] = P.val("ve")
                stt_mark["last"] = P.val("ve")
                stt_mark["w2pe"] = P.val("pe")
        ffn._chunk_marks = dict((m2, stt_mark[m2]) for m2 in range(NCH))

    def mamba(p, src_t):
        mp = mbp[p]
        ug = [arb(C_O + 4096 * c, T) for c in range(NCH)]
        WI_O = 32768
        xs = arf(C_O + 49152, 2051)
        cacc = arf(C_O + 57360, 2048)
        cacc2 = [cacc, arf(C_O + 49152, 2048)]
        gs_b = arb(C_O + 65552, T)
        mpar = arf(A_O + 0, 176)
        sAB = arf(A_O + 1024, SSD, parts=SSD)
        sATB = arf(A_O + 1216, SSD, parts=SSD)
        sBC = arf(A_O + 1344, 4, parts=SSD)
        wtb = arf(A_O + 1408, 132, parts=SSD)
        pw = [arf(A_O + 1952 + i * 128, SSD, parts=SSD) for i in range(8)]
        pwT = [arf(A_O + 2976 + i * 128, SSD, parts=SSD) for i in range(8)]
        rsm4 = [arf(A_O + 4000 + i * 128, SSD, parts=SSD) for i in range(4)]
        rsm4b = [arb(A_O + 4512 + i * 64, SSD, parts=SSD) for i in range(4)]
        ssmw_bf = arb(A_O + 4768, 256)
        ssb = arf(A_O + 5280, 4 * SSD)
        hx = arf(A_O + 5792, 16, parts=SSD)
        hb = arf(A_O + 5856, 16, parts=SSD)
        msbf = arf(A_O + 5920, 128)
        shsb = arf(A_O + 6432, 128)
        wtbf = arb(A_O + 6944, 132, parts=SSD)
        u_b2 = [arb(A_O + 7232, T), arb(A_O + 16384 + 4096, T)]
        usb = arb(A_O + 11328, T, parts=SSD)
        crow = arb(A_O + 15424, T, parts=1)

        ew = P.all_waits()
        loads = [("inb", mpar[:, 0:16]), ("convb", mpar[:, 16:24]),
                 ("convw", mpar[:, 24:56]), ("D", mpar[:, 56:64]),
                 ("outb", mpar[:, 64:72]), ("A", sAB), ("AT", sATB),
                 ("B", sBC[0:SSD, 0:1]), ("C", sBC[0:SSD, 1:2]),
                 ("ssmb", sBC[0:SSD, 2:3])]
        for i, (nm, dst) in enumerate(loads):
            P.op("sync", lambda e, nm=nm, dst=dst: e.dma_start(
                out=dst, in_=mp[nm]), wait=(ew if i == 0 else []),
                inc=[("dma", 16)])
        P.op("sync", lambda e: e.dma_start(
            out=xs[:, 0:256].rearrange("p (a b) -> p a b", a=8),
            in_=mp["ssmw"].rearrange("(k p) c -> p k c", p=128)),
            inc=[("dma", 16)])
        P.op("vector", lambda e: e.tensor_copy(ssmw_bf, xs[:, 0:256]),
             wait=["ys"], inc=["ve"])
        P.op("vector", lambda e: e.tensor_mul(
            sBC[0:SSD, 3:4], sBC[0:SSD, 2:3], sBC[0:SSD, 0:1]), inc=["ve"])
        P.op("scalar", lambda e: e.copy(pw[0], sATB), wait=["ys"])
        P.op("scalar", lambda e: e.copy(pwT[0], sAB))
        P.op("scalar", lambda e: e.copy(wtb[:, 0:1], sBC[0:SSD, 1:2]),
             inc=["se"])
        P.op("tensor", lambda e: e.matmul(
            psn(0, 1, 0, SSD)[:, 0:1], sAB, wtb[:, 0:1], start=True,
            stop=True), wait=["se", "ve", "pe"], inc=["pe"])
        P.op("scalar", lambda e: e.copy(
            wtb[:, 1:2], psn(0, 1, 0, SSD)[:, 0:1]), wait=["pe"], inc=["se"])
        for lvl in range(1, 8):
            cols = 1 << lvl
            P.op("tensor", lambda e, lvl=lvl: e.matmul(
                psn(1, 1, 0, SSD)[:, 0:SSD], pwT[lvl - 1], pw[lvl - 1],
                start=True, stop=True), wait=["se"], inc=["pe"])
            P.op("tensor", lambda e, lvl=lvl: e.matmul(
                psn(2, 1, 0, SSD)[:, 0:SSD], pw[lvl - 1], pwT[lvl - 1],
                start=True, stop=True), inc=["pe"])
            P.op("scalar", lambda e, lvl=lvl: e.copy(
                pw[lvl], psn(1, 1, 0, SSD)[:, 0:SSD]), wait=["pe"])
            P.op("scalar", lambda e, lvl=lvl: e.copy(
                pwT[lvl], psn(2, 1, 0, SSD)[:, 0:SSD]), inc=["se"])
            take = min(cols, 129 - cols)
            P.op("tensor", lambda e, lvl=lvl, cols=cols, take=take: e.matmul(
                psn(0, 1, 0, SSD)[:, 0:take], pwT[lvl], wtb[:, 0:take],
                start=True, stop=True), wait=["se"], inc=["pe"])
            P.op("scalar", lambda e, cols=cols, take=take: e.copy(
                wtb[:, cols:cols + take], psn(0, 1, 0, SSD)[:, 0:take]),
                wait=["pe"], inc=["se"])
        P.op("tensor", lambda e: e.matmul(
            psn(3, 1, 0, SSD)[:, 0:SSD], pwT[1], sATB, start=True, stop=True),
            wait=["se"], inc=["pe"])
        P.op("scalar", lambda e: e.copy(
            rsm4[0], psn(3, 1, 0, SSD)[:, 0:SSD]), wait=["pe"])
        P.op("scalar", lambda e: e.copy(rsm4[1], pw[1]))
        P.op("scalar", lambda e: e.copy(rsm4[2], sATB))
        P.op("scalar", lambda e: e.copy(
            rsm4[3], C_("ident_f")[0:SSD, 0:SSD]), inc=["se"])
        for ip in range(4):
            P.op("vector", lambda e, ip=ip: e.tensor_copy(
                rsm4b[ip], rsm4[ip]), wait=(["se"] if ip == 0 else []))
        P.op("vector", lambda e: e.tensor_copy(wtbf, wtb), inc=["ve"])
        P.op("vector", lambda e: e.memset(xs[:, 0:3], 0.0), inc=["ve"])
        P.op("vector", lambda e: e.memset(hx, 0.0), inc=["ve"])

        wi = WS(WI_O, 2048, WI_O + 8192, 2048)
        wi.load(wsrc(mp["inw"], slice(0, 256)), 0)
        mk = {}
        for c in range(NCH):
            buf = c % 2
            if c + 1 < NCH:
                wi.load(wsrc(mp["inw"], slice(256 * (c + 1), 256 * (c + 2))),
                        (c + 1) % 2)
            for mloc in range(2):
                for tt in range(NTT):
                    for kk in range(NCH):
                        P.op("tensor", lambda e, kk=kk, tt=tt, buf=buf,
                             mloc=mloc: e.matmul(
                                 PSAP[:, 512 * tt: 512 * (tt + 1)],
                                 wi.bf(buf)[:, 256 * kk + 128 * mloc:
                                            256 * kk + 128 * (mloc + 1)],
                                 src_t(kk)[:, 512 * tt: 512 * (tt + 1)],
                                 start=(kk == 0), stop=(kk == NCH - 1)),
                             wait=((wi.wait_use(buf)
                                    + [("se", mk.get(("ev", c, mloc - 1),
                                                     mk.get(("ev", c - 1, 1), 0)))]
                                    + ([("ve", P.val("ve"))] if c == 0 and
                                       mloc == 0 else []))
                                   if kk == 0 and tt == 0 else []),
                             inc=(["pe"] if kk == NCH - 1 and tt == NTT - 1
                                  else []))
                if mloc == 0:
                    P.op("scalar", lambda e, c=c: e.activation(
                        xs[:, 3:2051], PSAP[:, 0:2048], AF.Identity,
                        bias=mpar[:, 2 * c:2 * c + 1]),
                        wait=["pe", ("ve", mk.get(("dw", c - 1), 0))],
                        inc=["se"])
                    mk[("ev", c, 0)] = P.val("se")
                else:
                    P.op("scalar", lambda e, c=c: e.activation(
                        gs_b, PSAP[:, 0:2048], AF.Silu,
                        bias=mpar[:, 2 * c + 1:2 * c + 2]),
                        wait=["pe", ("po", mk.get(("ug", c - 1), 0))],
                        inc=["se"])
                    mk[("ev", c, 1)] = P.val("se")
            wi.mark_done(buf)
            P.op("vector", lambda e, c=c: e.tensor_scalar(
                cacc, xs[:, 0:2048], mpar[:, 24 + 4 * c: 25 + 4 * c], None,
                ALU.mult),
                wait=[("se", mk[("ev", c, 0)]),
                      ("se", mk.get(("u", c - 1), 0))])
            for tau in range(1, 4):
                P.op("vector", lambda e, c=c, tau=tau: e.scalar_tensor_tensor(
                    cacc, xs[:, tau: tau + 2048],
                    mpar[:, 24 + 4 * c + tau: 25 + 4 * c + tau], cacc,
                    ALU.mult, ALU.add),
                    inc=(["ve"] if tau == 3 else []))
            mk[("dw", c)] = P.val("ve")
            ub = u_b2[c % 2]
            P.op("scalar", lambda e, c=c, ub=ub: e.activation(
                ub, cacc, AF.Silu, bias=mpar[:, 16 + c:17 + c]),
                wait=["ve", ("po", mk.get(("ug", c - 1), 0)),
                      ("pe", mk.get(("xs", c - 2), 0))], inc=["se"])
            mk[("u", c)] = P.val("se")

            def xssm_mms(c):
                ub = u_b2[c % 2]
                for tt in range(NTT):
                    P.op("tensor", lambda e, c=c, tt=tt, ub=ub: e.matmul(
                        PSAP[0:SSD, 2048 + 512 * tt: 2048 + 512 * (tt + 1)],
                        ssmw_bf[:, 32 * c: 32 * (c + 1)],
                        ub[:, 512 * tt: 512 * (tt + 1)],
                        start=(c == 0), stop=(c == NCH - 1)),
                        wait=([("se", mk[("u", c)])] if tt == 0 else []),
                        inc=(["pe"] if tt == NTT - 1 else []))
                mk[("xs", c)] = P.val("pe")
            mk[("defer", c)] = xssm_mms
            if c >= 1:
                mk[("defer", c - 1)](c - 1)
            if c == NCH - 1:
                mk[("defer", c)](c)
            P.op("gpsimd", lambda e, c=c, ub=ub: e.tensor_mul(
                ug[c], ub, gs_b), wait=[("se", P.val("se"))], inc=["po"])
            mk[("ug", c)] = P.val("po")
        P.op("vector", lambda e: e.tensor_scalar(
            usb, PSAP[0:SSD, 2048:4096], sBC[0:SSD, 0:1], sBC[0:SSD, 3:4],
            ALU.mult, ALU.add), wait=["pe"], inc=["ve"])
        u_ve = P.val("ve")

        for g in range(4):
            for ip in range(4):
                P.op("tensor", lambda e, g=g, ip=ip: e.matmul(
                    PSAP[:, 3584 + 32 * g: 3584 + 32 * (g + 1)],
                    usb.rearrange("p (n s) -> p s n", s=4)[:, ip, :]
                    [:, 128 * g: 128 * (g + 1)],
                    rsm4b[ip], start=(ip == 0), stop=(ip == 3)),
                    wait=([("ve", u_ve)] if g == 0 and ip == 0 else []),
                    inc=(["pe"] if ip == 3 else []))
        P.op("vector", lambda e: e.tensor_copy(ssb, PSAP[:, 3584:3712]),
             wait=["pe"], inc=["ve"])
        ssb_ve = P.val("ve")

        for b in range(32):
            if b > 0:
                P.op("tensor", lambda e: e.matmul(
                    PSAP[0:SSD, 3072:3088], pw[2], hx, start=True,
                    stop=False),
                    wait=[("se", mk.get(("hx", b - 1), 0))], inc=[])
            for g in range(4):
                P.op("tensor", lambda e, b=b, g=g: e.matmul(
                    PSAP[0:SSD, 3072:3088][:, 4 * g: 4 * (g + 1)],
                    ssb[:, 32 * g: 32 * (g + 1)],
                    C_("sel")[:, 4 * b: 4 * (b + 1)],
                    start=(b == 0 and g == 0), stop=(g == 3)),
                    wait=([("ve", ssb_ve),
                           ("se", mk.get(("hx", b - 1), 0))]
                          if g == 0 else []),
                    inc=(["pe"] if g == 3 else []))
            P.op("scalar", lambda e: e.copy(hx, PSAP[0:SSD, 3072:3088]),
                 wait=["pe"], inc=["se"])
            mk[("hx", b)] = P.val("se")
        P.op("scalar", lambda e: e.copy(hb[:, 0:1], hx[:, 0:1]), inc=["se"])
        for j in range(1, 16):
            P.op("tensor", lambda e, j=j: e.matmul(
                PSAP[0:SSD, 3136:3137], pw[7], hb[:, j - 1:j], start=True,
                stop=False), wait=["se"], inc=[])
            P.op("tensor", lambda e, j=j: e.matmul(
                PSAP[0:SSD, 3136:3137], C_("ident_f")[0:SSD, 0:SSD],
                hx[:, j:j + 1], start=False, stop=True), inc=["pe"])
            P.op("scalar", lambda e, j=j: e.copy(
                hb[:, j:j + 1], PSAP[0:SSD, 3136:3137]), wait=["pe"],
                inc=["se"])
        hb_se = P.val("se")

        dshift_w = [dscan[33024 * i: 33024 * i + 33024]
                    .rearrange("(a j) -> a j", a=128, j=258)[:, 0:128]
                    for i in range(2)]
        dshift_r = [dscan[33024 * i: 33024 * i + 32896]
                    .rearrange("(a j) -> a j", a=128, j=257)[:, 0:128]
                    for i in range(2)]
        for j in range(16):
            P.op("tensor", lambda e, j=j: e.matmul(
                PSAP[:, 0:128], usb[:, 128 * j: 128 * (j + 1)],
                wtbf[:, 0:128], start=True, stop=True),
                wait=([("ve", u_ve), ("se", hb_se)] if j == 0 else
                      [("se", mk.get(("ms", j - 1), 0))]), inc=["pe"])
            P.op("scalar", lambda e, j=j: e.copy(msbf, PSAP[:, 0:128]),
                 wait=["pe", ("dgp", mk.get(("wr", j - 1), 0))], inc=["se"])
            mk[("ms", j)] = P.val("se")
            P.op("gpsimd", lambda e, j=j: e.dma_start(
                out=dshift_w[j % 2], in_=msbf),
                wait=[("se", mk[("ms", j)]), ("dgp", mk.get(("rd", j - 2), 0))],
                inc=[("dgp", 16)])
            mk[("wr", j)] = P.val("dgp")
            P.op("gpsimd", lambda e, j=j: e.dma_start(
                out=shsb, in_=dshift_r[j % 2]),
                wait=[("dgp", mk[("wr", j)]),
                      ("pe", mk.get(("cl", j - 1), 0))],
                inc=[("dgp", 16)])
            mk[("rd", j)] = P.val("dgp")
            mk[("rdf", j)] = P.val("po")
            P.op("tensor", lambda e, j=j: e.matmul(
                PSAP[0:1, 512 + 128 * j: 512 + 128 * (j + 1)],
                C_("ones_col_f"), shsb, start=True, stop=(j == 0)),
                wait=[("po", mk[("rdf", j)])], inc=(["pe"] if j == 0 else []))
            if j > 0:
                P.op("tensor", lambda e, j=j: e.matmul(
                    PSAP[0:1, 512 + 128 * j: 512 + 128 * (j + 1)],
                    hb[:, j - 1:j], wtb[:, 1:129], start=False, stop=True),
                    inc=["pe"])
            mk[("cl", j)] = P.val("pe")
        P.op("scalar", lambda e: e.copy(crow, PSAP[0:1, 512:2560]),
             wait=["pe"], inc=["se"])
        for tt in range(NTT):
            P.op("tensor", lambda e, tt=tt: e.matmul(
                psn(tt), C_("ones_row_bf"), crow[:, 512 * tt: 512 * (tt + 1)],
                start=True, stop=True), wait=(["se"] if tt == 0 else []),
                inc=(["pe"] if tt == NTT - 1 else []))
        cbc_pe = P.val("pe")
        yv = {}
        for c in range(NCH):
            P.op("scalar", lambda e, c=c: e.activation(
                cacc, PSAP[:, 0:2048], AF.Identity,
                bias=mpar[:, 56 + c:57 + c]),
                wait=[("pe", cbc_pe), ("ve", yv.get(c - 1, 0))], inc=["se"])
            P.op("vector", lambda e, c=c: e.tensor_mul(
                ug[c], cacc, ug[c]), wait=["se"], inc=["ve"])
            yv[c] = P.val("ve")
        wo2 = WS(WI_O, 2048, WI_O + 8192, 2048)
        wo2.load(wsrc(mp["outw"], slice(0, 256)), 0)
        for m in range(NCH):
            gidx, loc = divmod(m, 2)
            buf = gidx % 2
            if loc == 0 and gidx + 1 < 4:
                wo2.load(wsrc(mp["outw"],
                              slice(256 * (gidx + 1), 256 * (gidx + 2))),
                         (gidx + 1) % 2)
            ob = 2048 * (m % 2)
            for tt in range(NTT):
                for kk in range(NCH):
                    P.op("tensor", lambda e, kk=kk, tt=tt, buf=buf, loc=loc,
                         ob=ob: e.matmul(
                             PSAP[:, ob + 512 * tt: ob + 512 * (tt + 1)],
                             wo2.bf(buf)[:, 256 * kk + 128 * loc:
                                         256 * kk + 128 * (loc + 1)],
                             ug[kk][:, 512 * tt: 512 * (tt + 1)],
                             start=(kk == 0), stop=(kk == NCH - 1)),
                         wait=((wo2.wait_use(buf)
                                + [("ve", yv.get(NCH - 1, 0)),
                                   ("se", mk.get(("ow", m - 2), 0))])
                               if kk == 0 and tt == 0 else []),
                         inc=(["pe"] if kk == NCH - 1 and tt == NTT - 1
                              else []))
            if loc == 1:
                wo2.mark_done(buf)
            P.op("scalar", lambda e, m=m, ob=ob: e.activation(
                cacc2[m % 2], PSAP[:, ob: ob + 2048], AF.Identity,
                bias=mpar[:, 64 + m:65 + m]),
                wait=["pe", ("ve", mk.get(("owtt", m - 2), 0))], inc=["se"])
            mk[("ow", m)] = P.val("se")
            P.op("vector", lambda e, m=m: e.tensor_add(
                src_t(m), cacc2[m % 2], src_t(m)), wait=["se"], inc=["ve"])
            mk[("owtt", m)] = P.val("ve")

    def conv():
        oc = [arb(C_O + 4096 * m, T) for m in range(NCH)]
        WSO = 32768
        o5a = arf(C_O + 57360, 2048)
        c1bs = arf(A_O + 0, 4)
        stage3 = arf(C_O + WSO, 3072)
        bf3 = [arb(C_O + WSO + 12288 + i * 6144, 3072) for i in range(2)]
        P.op("sync", lambda e: e.dma_start(out=c1bs, in_=c1b_d),
             wait=P.all_waits(), inc=[("dma", 16)])
        P.op("vector", lambda e: e.tensor_scalar(
            c1bs, c1bs, CNs(CN_BNS), None, ALU.mult), wait=["ys"],
            inc=["ve"])

        cast_rdy = [0, 0]
        free_pe = [0, 0]
        last_cast = [P.val("po")]

        def load3(w3, m, buf, three=True):
            ntau = 3 if three else 1
            for tau in range(ntau):
                P.op("sync", lambda e, w3=w3, m=m, tau=tau, three=three:
                     e.dma_start(
                         out=stage3[:, 1024 * tau: 1024 * (tau + 1)]
                         .rearrange("p (a b) -> p a b", a=8),
                         in_=wsrc(w3[tau] if three else w3,
                                  slice(128 * m, 128 * (m + 1)))),
                     wait=([("po", last_cast[0]), ("pe", free_pe[buf])]
                           if tau == 0 else []),
                     inc=[("dma", 16)])
            P.op("gpsimd", lambda e, buf=buf, ntau=ntau: e.tensor_copy(
                bf3[buf][:, 0:1024 * ntau], stage3[:, 0:1024 * ntau]),
                wait=["ys"], inc=["po"])
            cast_rdy[buf] = P.val("po")
            last_cast[0] = P.val("po")

        def cmms(src_t_, m, buf, ntau, pb, wait_first, evac):
            first = [True]
            for tt in range(NTT):
                for tau in ([1, 0, 2] if ntau == 3 else [0]):
                    for kk in range(NCH):
                        sh_ = tau - 1 if ntau == 3 else 0
                        s0 = 512 * tt + sh_
                        o0, n = 512 * tt, 512
                        if s0 < 0:
                            o0, s0, n = 1, 0, 511
                        elif s0 + 512 > 2048:
                            n = 2048 - s0
                        fin = (tau == (2 if ntau == 3 else 0)
                               and kk == NCH - 1)
                        last = fin and tt == NTT - 1
                        st_ = (tau == (1 if ntau == 3 else 0) and kk == 0)
                        P.op("tensor", lambda e, tau=tau, kk=kk, s0=s0,
                             o0=o0, n=n, buf=buf, pb=pb, f=st_, l=fin:
                             e.matmul(
                                 PSAP[:, 2048 * pb + o0: 2048 * pb + o0 + n],
                                 bf3[buf][:, 1024 * tau + 128 * kk:
                                          1024 * tau + 128 * (kk + 1)],
                                 src_t_(kk)[:, s0: s0 + n],
                                 start=f, stop=l),
                             wait=(wait_first + [("po", cast_rdy[buf])]
                                   if first[0] else []),
                             inc=(["pe"] if last else []))
                        first[0] = False
            free_pe[buf] = P.val("pe")
            evac()

        evm = {}
        load3(c1wT, 0, 0)
        for m in range(4):
            buf = m % 2
            if m + 1 < 4:
                load3(c1wT, m + 1, (m + 1) % 2)
            pb = m % 2

            def ev1(m=m, pb=pb):
                P.op("scalar", lambda e, m=m, pb=pb: e.activation(
                    oc[m], PSAP[:, 2048 * pb: 2048 * (pb + 1)], AF.Relu,
                    bias=c1bs[:, m:m + 1], scale=CNs(CN_BNS)),
                    wait=["pe"], inc=["se"])
                evm[m] = P.val("se")
            cmms(xn_t, m, buf, 3, pb,
                 [("se", evm.get(m - 2, 0))] + (P.all_waits() if m == 0
                                                else []), ev1)
        load3(c4wT, 0, 0, three=False)
        for m in range(4):
            buf = m % 2
            if m + 1 < 4:
                load3(c4wT, m + 1, (m + 1) % 2, three=False)
            pb = m % 2

            def ev4(m=m, pb=pb):
                P.op("scalar", lambda e, m=m, pb=pb: e.activation(
                    oc[4 + m], PSAP[:, 2048 * pb: 2048 * (pb + 1)], AF.Relu),
                    wait=["pe"], inc=["se"])
                evm[4 + m] = P.val("se")
            cmms(xn_t, m, buf, 1, pb, [("se", evm.get(4 + m - 2, 0))], ev4)
        oc_se = P.val("se")
        sttm = {}
        load3(c5wT, 0, 0)
        for m in range(NCH):
            buf = m % 2
            if m + 1 < NCH:
                load3(c5wT, m + 1, (m + 1) % 2)
            pb = m % 2

            def ev5(m=m, pb=pb):
                P.op("scalar", lambda e, m=m, pb=pb: e.activation(
                    o5a, PSAP[:, 2048 * pb: 2048 * (pb + 1)], AF.Relu,
                    scale=CNs(CN_BNS)),
                    wait=["pe", ("ve", sttm.get(m - 1, 0))], inc=["se"])
                P.op("vector", lambda e, m=m: e.scalar_tensor_tensor(
                    X(m), o5a, CNs(CN_ONE), X(m), ALU.mult, ALU.add),
                    wait=["se"])
                P.op("vector", lambda e, m=m: e.scalar_tensor_tensor(
                    X(m), xn_t(m), CNs(CN_ONE), X(m), ALU.mult, ALU.add),
                    inc=["ve"])
                sttm[m] = P.val("ve")
                evm[("o5", m)] = P.val("se")
            cmms(lambda kk: oc[kk], m, buf, 3, pb,
                 [("se", oc_se), ("se", evm.get(("o5", m - 2), 0))], ev5)
        conv._chunk_marks = dict(sttm)

    def pout():
        ostg = [arf(C_O + i * 4096, 1024) for i in range(2)]
        vml = {}
        for tc in range(16):
            b = tc % 2
            for c in range(NCH):
                P.op("tensor", lambda e, c=c, tc=tc: e.transpose(
                    psn(c).bitcast(F32)[:, 0:128],
                    X(c)[:, 128 * tc: 128 * (tc + 1)], C_("ident_f")),
                    wait=(P.all_waits() if tc == 0 and c == 0 else
                          ([("ys", P.val("ys") - 1),
                            ("ve", vml.get(tc - 1, 0))] if c == 0 else [])),
                    inc=(["pe"] if c == NCH - 1 else []))
            for c in range(NCH):
                P.op("vector", lambda e, c=c, b=b: e.tensor_copy(
                    ostg[b][:, 128 * c: 128 * (c + 1)],
                    psn(c).bitcast(F32)[:, 0:128]),
                    wait=(["pe"] if c == 0 else []),
                    inc=(["ve"] if c == NCH - 1 else []))
            vml[tc] = P.val("ve")
            P.op("sync", lambda e, b=b, tc=tc: e.dma_start(
                out=out[128 * tc: 128 * (tc + 1), :], in_=ostg[b]),
                wait=["ve"], inc=[("dma", 16)])
        P.op("sync", lambda e: e.nop(), wait=["dma"])

    def tap(name):
        if name not in tap_outs:
            return
        P.op("sync", lambda e, name=name: e.dma_start(
            out=tap_outs[name][:, 0:AR_BYTES // 4], in_=ARH.ap()),
            wait=P.all_waits(), inc=[("dma", 16)])
        P.op("sync", lambda e, name=name: e.dma_start(
            out=tap_outs[name][:, 32256: 32256 + NCH * T], in_=XB.ap()),
            inc=[("dma", 16)])

    P.phase = "p0"
    p0()
    tap("x0")
    if "attn" in stages:
        P.phase = "ln1"
        ln(X)
        tap("ln1")
        P.phase = "attn"
        attn()
        tap("attn")
    if "ffn1" in stages:
        P.phase = "ln2"
        ln(X)
        P.phase = "ffn1"
        ffn()
        tap("ffn1")
    if "mamba1" in stages:
        P.phase = "ln3"
        ln(X)
        P.phase = "mamba1"
        mamba("m1", xn_t)
        tap("mamba1")
    if "mamba2" in stages:
        P.phase = "mamba2"
        mamba("m2", xn_t)
        tap("mamba2")
    if "conv" in stages:
        P.phase = "conv"
        conv()
        tap("conv")
    if "ffn2" in stages:
        P.phase = "ln4"
        ln(X)
        P.phase = "ffn2"
        ffn()
        tap("ffn2")
    P.phase = "pout"
    pout()

    semh = {s: nc.alloc_semaphore(s) for s in Prog.SEMS}

    def run(eng_name, e):
        import contextlib
        cur = [None]
        stack = [None]

        def set_scope(ph):
            if ph == cur[0]:
                return
            if stack[0] is not None:
                stack[0].__exit__(None, None, None)
                stack[0] = None
            cur[0] = ph
            if ph is not None:
                stack[0] = nc.named_scope(ph)
                stack[0].__enter__()

        for eng, waits, fn, incs, phase in P.ops:
            if eng != eng_name:
                continue
            set_scope(phase)
            seen = set()
            for s, v in waits:
                if (s, v) in seen:
                    continue
                seen.add((s, v))
                e.wait_ge(semh[s], v)
            ins = fn(e)
            for s, n in incs:
                ins.then_inc(semh[s], n)
        set_scope(None)

    with nc.Block() as block:
        @block.sync
        def _(e):
            run("sync", e)

        @block.tensor
        def _(e):
            run("tensor", e)

        @block.vector
        def _(e):
            run("vector", e)

        @block.scalar
        def _(e):
            run("scalar", e)

        @block.gpsimd
        def _(e):
            run("gpsimd", e)

    return nc


def kernel(**inputs):
    g = host_prep(inputs)
    x = np.ascontiguousarray(inputs["L0_attn_a_x"], dtype=np.float32)
    nc = build_program()
    shared = {n: g[n] for n in SHARED_NAMES}
    shared.update({"c_" + n: g[n] for n in CONST_NAMES})
    in_maps = []
    for i in range(8):
        m = dict(shared)
        m["x"] = np.ascontiguousarray(x[i])
        in_maps.append(m)
    res = run_bass_kernel_spmd(nc, in_maps, core_ids=list(range(8)))
    return np.stack([r["out"] for r in res.results])


# version 08 (final)
# speedup vs baseline: 1.3655x; 1.3655x over previous
import numpy as np
import concourse.bass as bass
import concourse.mybir as mybir
from concourse.bass_utils import run_bass_kernel_spmd
from concourse.bass_core_rust import IndirectOffsetOnAxis

F32 = mybir.dt.float32
BF16 = mybir.dt.bfloat16
I32 = mybir.dt.int32
AF = mybir.ActivationFunctionType
ALU = mybir.AluOpType
AX = mybir.AxisListType

D, T, H, HD = 1024, 2048, 16, 64
NCH, NTT = 8, 4
NE, HID, SSD, CONVK = 8, 4096, 32, 4
NEX = 9
CAP = 1024
NSLOT = NE * CAP
SPARSE = True

STAGES = ("attn", "ffn1", "mamba1", "mamba2", "conv", "ffn2")

CN_INVD, CN_EPS, CN_BNS, CN_NEG1, CN_HALF, CN_ONE, CN_NK, CN_NINF = range(8)


def _rope_perm():
    perm = []
    for h in range(H):
        perm += [h * HD + 2 * j for j in range(HD // 2)]
        perm += [h * HD + 2 * j + 1 for j in range(HD // 2)]
    return np.array(perm)


def _cos_sin():
    freqs = 1.0 / 1000.0 ** (np.arange(0, HD, 2, dtype=np.float64) / HD)
    ang = np.arange(T, dtype=np.float64)[:, None] * freqs[None, :]
    cos = np.tile(np.cos(ang).T, (4, 1)).astype(np.float32)
    sin = np.tile(np.sin(ang).T, (4, 1)).astype(np.float32)
    sgn = np.ones((128, 1), np.float32)
    sgn[32:64] = -1.0
    sgn[96:128] = -1.0
    return cos, sgn * sin


def _bias_tile(b):
    n = b.shape[0]
    return np.ascontiguousarray(b.reshape(n // 128, 128).T).astype(np.float32)


def host_prep(inp):
    import ml_dtypes
    bf = ml_dtypes.bfloat16
    g = {}
    perm = _rope_perm()
    g["wqT"] = np.ascontiguousarray(inp["L0_attn_a_wq"].T[:, perm])
    g["wkp"] = np.ascontiguousarray(inp["L0_attn_a_wk"][:, perm[:HD]])
    g["wv"] = np.ascontiguousarray(inp["L0_attn_a_wv"])
    g["woT"] = np.ascontiguousarray(inp["L0_attn_a_wo"].T)

    w1 = [np.ascontiguousarray(inp["L0_ffn_shared_w1"].T)]
    w2 = [np.ascontiguousarray(inp["L0_ffn_shared_w2"].T)]
    for e in range(NE):
        w1.append(np.ascontiguousarray(inp["L0_ffn_experts_w1"][e].T))
        w2.append(np.ascontiguousarray(inp["L0_ffn_experts_w2"][e].T))
    g["ffn_w1"] = np.stack(w1)
    g["ffn_w2"] = np.stack(w2)
    gw = inp["L0_ffn_gate_w"].T
    g["gateT"] = np.ascontiguousarray(
        gw.reshape(NCH, 128, NE).transpose(1, 0, 2).reshape(128, NCH * NE))

    order = []
    for c in range(NCH):
        order += list(range(128 * c, 128 * (c + 1)))
        order += list(range(D + 128 * c, D + 128 * (c + 1)))
    for p in ("m1", "m2"):
        pre = f"L0_attn_b_{p}_"
        g[p + "_inw"] = np.ascontiguousarray(inp[pre + "in_w"][:, order])
        g[p + "_inb"] = _bias_tile(inp[pre + "in_b"][order])
        g[p + "_convw"] = np.ascontiguousarray(
            inp[pre + "conv_w"][:, 0, :].reshape(NCH, 128, CONVK)
            .transpose(1, 0, 2).reshape(128, NCH * CONVK))
        g[p + "_convb"] = _bias_tile(inp[pre + "conv_b"])
        g[p + "_ssmw"] = np.ascontiguousarray(inp[pre + "ssm_w"])
        g[p + "_ssmb"] = np.ascontiguousarray(
            inp[pre + "ssm_b"].reshape(SSD, 1)).astype(np.float32)
        g[p + "_A"] = np.ascontiguousarray(inp[pre + "A"])
        g[p + "_AT"] = np.ascontiguousarray(inp[pre + "A"].T)
        g[p + "_B"] = np.ascontiguousarray(inp[pre + "B"].reshape(SSD, 1))
        g[p + "_C"] = np.ascontiguousarray(inp[pre + "C"].reshape(SSD, 1))
        g[p + "_D"] = _bias_tile(inp[pre + "D"])
        g[p + "_outw"] = np.ascontiguousarray(inp[pre + "out_w"])
        g[p + "_outb"] = _bias_tile(inp[pre + "out_b"])

    g["c1wT"] = np.ascontiguousarray(inp["L0_attn_b_c1_w"].transpose(2, 1, 0))
    g["c1b"] = np.ascontiguousarray(
        inp["L0_attn_b_c1_b"].reshape(4, 128).T).astype(np.float32)
    g["c4wT"] = np.ascontiguousarray(inp["L0_attn_b_c4_w"][:, :, 0].T)
    g["c5wT"] = np.ascontiguousarray(inp["L0_attn_b_c5_w"].transpose(2, 1, 0))

    cos, s2 = _cos_sin()
    g["cosq"] = (cos / 8.0).astype(bf)
    g["s2q"] = (s2 / 8.0).astype(bf)
    g["cosk"] = cos.astype(bf)
    g["s2k"] = s2.astype(bf)
    g["ident_f"] = np.eye(128, dtype=np.float32)
    g["ident_bf"] = np.eye(128, dtype=np.float32).astype(bf)
    sel = np.zeros((128, 128), np.float32)
    for b in range(32):
        for js in range(4):
            sel[32 * js + b, 4 * b + js] = 1.0
    g["sel"] = sel
    g["ones_col_f"] = np.ones((128, 1), np.float32)
    g["ones_col_bf"] = np.ones((128, 1), np.float32).astype(bf)
    g["ones_row_bf"] = np.ones((1, 128), np.float32).astype(bf)
    g["ecol8"] = np.tile(8.0 * np.arange(8, dtype=np.float32), (128, 1))
    cvals = [1.0 / D, 1e-5, float(1.0 / np.sqrt(np.float32(1.00001))), -1.0,
             0.5, 1.0, -1.02, -30000.0]
    g["CN"] = np.tile(np.array(cvals, np.float32)[None, :], (128, 1))
    return g


SHARED_NAMES = (
    ["wqT", "wkp", "wv", "woT", "ffn_w1", "ffn_w2", "gateT",
     "c1wT", "c1b", "c4wT", "c5wT"]
    + [f"{p}_{s}" for p in ("m1", "m2") for s in
       ("inw", "inb", "convw", "convb", "ssmw", "ssmb", "A", "AT", "B", "C",
        "D", "outw", "outb")])
CONST_NAMES = ("cosq", "s2q", "cosk", "s2k", "ident_f", "ident_bf", "sel",
               "ones_col_f", "ones_col_bf", "ones_row_bf", "CN", "ecol8")
CONST_BF = {"cosq", "s2q", "cosk", "s2k", "ident_bf", "ones_col_bf",
            "ones_row_bf"}


class Prog:
    SEMS = ("dma", "dgp", "pe", "ve", "se", "po", "ys")

    def __init__(self):
        self.ops = []
        self.c = {s: 0 for s in self.SEMS}
        self.phase = "init"

    ESEM = {"vector": "ve", "scalar": "se", "gpsimd": "po"}

    def op(self, eng, fn, wait=(), inc=()):
        waits = []
        for w in wait:
            if isinstance(w, tuple):
                if w[1] > 0:
                    waits.append(w)
            elif self.c[w] > 0:
                waits.append((w, self.c[w]))
        es = self.ESEM.get(eng)
        es_prior = self.c[es] if es is not None else 0
        incs = []
        fence = None
        isdma = False
        for i in inc:
            s, n = i if isinstance(i, tuple) else (i, 1)
            if s in ("dma", "dgp"):
                isdma = True
                if s == "dma" and eng == "sync":
                    fence = ("dma", "ys")
                if s == "dgp" and eng == "gpsimd":
                    fence = ("dgp", "po")
            self.c[s] += n
            incs.append((s, n))
        if es is not None and not isdma:
            if es_prior > 0:
                waits.append((es, es_prior))
            if not any(s == es for s, _ in incs):
                self.c[es] += 1
                incs.append((es, 1))
        self.ops.append((eng, waits, fn, incs, self.phase))
        if fence is not None:
            fs, ms = fence
            selfw = [(fs, self.c[fs])]
            if ms == "po" and self.c["po"] > 0:
                selfw.append(("po", self.c["po"]))
            self.c[ms] += 1
            self.ops.append((eng, selfw, lambda e: e.nop(), [(ms, 1)],
                             self.phase))

    def val(self, sem):
        return self.c[sem]

    def all_waits(self):
        return [(s, self.c[s]) for s in self.SEMS]


def _ncols(ap):
    if len(ap.shape) == 3:
        return ap.shape[1] * ap.shape[2]
    return ap.shape[1]


def build_program(stages=STAGES, taps=()):
    nc = bass.Bass("TRN2", target_bir_lowering=False)

    def din(name, shape, dtype=F32):
        return nc.dram_tensor(name, list(shape), dtype, kind="ExternalInput").ap()

    x_in = din("x", (T, D))
    out = nc.dram_tensor("out", [T, D], F32, kind="ExternalOutput").ap()
    tap_outs = {tp: nc.dram_tensor("tap_" + tp, [128, 32256 + NCH * T], F32,
                                   kind="ExternalOutput").ap() for tp in taps}

    wqT = din("wqT", (D, D))
    wkp = din("wkp", (D, HD))
    wv_d = din("wv", (D, HD))
    woT = din("woT", (D, D))
    ffn_w1 = din("ffn_w1", (NEX, D, HID))
    ffn_w2 = din("ffn_w2", (NEX, HID, D))
    gateT = din("gateT", (128, NCH * NE))
    mbp = {}
    for p in ("m1", "m2"):
        mbp[p] = {
            "inw": din(p + "_inw", (D, 2 * D)),
            "inb": din(p + "_inb", (128, 16)),
            "convw": din(p + "_convw", (128, NCH * CONVK)),
            "convb": din(p + "_convb", (128, NCH)),
            "ssmw": din(p + "_ssmw", (D, SSD)),
            "ssmb": din(p + "_ssmb", (SSD, 1)),
            "A": din(p + "_A", (SSD, SSD)),
            "AT": din(p + "_AT", (SSD, SSD)),
            "B": din(p + "_B", (SSD, 1)),
            "C": din(p + "_C", (SSD, 1)),
            "D": din(p + "_D", (128, NCH)),
            "outw": din(p + "_outw", (D, D)),
            "outb": din(p + "_outb", (128, NCH)),
        }
    c1wT = din("c1wT", (3, D, 512))
    c1b_d = din("c1b", (128, 4))
    c4wT = din("c4wT", (D, 512))
    c5wT = din("c5wT", (3, D, D))
    cshape = {"cosq": (128, T), "s2q": (128, T), "cosk": (128, T),
              "s2k": (128, T), "ident_f": (128, 128), "ident_bf": (128, 128),
              "sel": (128, 128), "ones_col_f": (128, 1),
              "ones_col_bf": (128, 1), "ones_row_bf": (1, 128),
              "CN": (128, 8), "ecol8": (128, 8)}
    cdram = {n: din("c_" + n, cshape[n], BF16 if n in CONST_BF else F32)
             for n in cshape}

    dscan = nc.dram_tensor("dscan", [2 * 33024], F32, kind="Internal").ap()
    xnd = nc.dram_tensor("xnd", [T + 1, D + 8], BF16, kind="Internal").ap()
    idxd = nc.dram_tensor("idxd", [NSLOT + 1, 1], I32, kind="Internal").ap()
    ysd = nc.dram_tensor("ysd", [NSLOT + 1, D], BF16, kind="Internal").ap()

    XB = nc.alloc_sbuf_tensor("XB", [128, NCH * T], F32)

    def X(c):
        return XB.ap()[:, T * c: T * (c + 1)]

    CONST = {n: nc.alloc_sbuf_tensor("sb_" + n, list(cshape[n]),
                                     BF16 if n in CONST_BF else F32)
             for n in cshape}

    def C_(n):
        return CONST[n].ap()

    def CNs(i):
        return C_("CN")[:, i:i + 1]

    def CN1(i):
        return C_("CN")[0:1, i:i + 1]

    AR_BYTES = 129024
    ARH = nc.alloc_sbuf_tensor("AR", [128, AR_BYTES // 4], F32)

    def arf(off, cols, parts=128, p0=0):
        assert off % 4 == 0 and off + 4 * cols <= AR_BYTES, (off, cols)
        return ARH.ap()[p0:p0 + parts, off // 4: off // 4 + cols]

    def arb(off, cols, parts=128, p0=0):
        assert off % 4 == 0 and off + 2 * cols <= AR_BYTES + 2, (off, cols)
        w = (2 * cols + 3) // 4
        return ARH.ap()[p0:p0 + parts, off // 4: off // 4 + w] \
            .bitcast(BF16)[:, 0:cols]

    PSH = nc.alloc_psum_tensor("PS", [128, 4096], F32)
    PSAP = PSH.ap()

    def psn(bank, nbank=1, row0=0, nrow=128):
        return PSAP[row0:row0 + nrow, 512 * bank: 512 * (bank + nbank)]

    def psbf(bank, cols, row0=0, nrow=128):
        return PSAP[row0:row0 + nrow, 512 * bank: 512 * (bank + 1)] \
            .bitcast(BF16)[:, 0:cols]

    P = Prog()

    A_O, B_O, C_O = 0, 24576, 57344

    def xn_t(c):
        return arb(B_O + 4096 * c, T)

    def rowf(i, cols=T):
        return arf(A_O + 8192 * i, cols, parts=1)

    def p0():
        for n in cdram:
            P.op("sync", lambda e, a=CONST[n].ap(), b=cdram[n]:
                 e.dma_start(out=a, in_=b), inc=[("dma", 16)])
        P.op("vector", lambda e: e.memset(ARH.ap(), 0.0), inc=["ve"])
        mz = P.val("ve")
        for i in range(2):
            P.op("gpsimd", lambda e, i=i: e.dma_start(
                out=dscan[33024 * i: 33024 * (i + 1)]
                .rearrange("(p f) -> p f", p=128, f=258),
                in_=arf(C_O, 258)), wait=(["ve"] if i == 0 else []),
                inc=[("dgp", 16)])
        stg = [arf(C_O + 8192 + i * 4096, 1024) for i in range(2)]
        vml = {}
        for tc in range(16):
            b = tc % 2
            P.op("sync", lambda e, b=b, tc=tc: e.dma_start(
                out=stg[b], in_=x_in[128 * tc: 128 * (tc + 1), :]),
                wait=([("ve", vml.get(tc - 2, mz))]), inc=[("dma", 16)])
            for c in range(NCH):
                P.op("tensor", lambda e, b=b, c=c: e.transpose(
                    psn(c).bitcast(F32)[:, 0:128],
                    stg[b][:, 128 * c: 128 * (c + 1)], C_("ident_f")),
                    wait=(["ys", ("ve", vml.get(tc - 1, 0))] if c == 0 else []),
                    inc=(["pe"] if c == NCH - 1 else []))
            for c in range(NCH):
                P.op("vector", lambda e, c=c, tc=tc: e.tensor_copy(
                    X(c)[:, 128 * tc: 128 * (tc + 1)],
                    psn(c).bitcast(F32)[:, 0:128]),
                    wait=(["pe"] if c == 0 else []),
                    inc=(["ve"] if c == NCH - 1 else []))
            vml[tc] = P.val("ve")

    class WS:
        def __init__(self, stage_off, stage_cols, bf_off, bf_cols,
                     cast_eng="vector"):
            self.stage = arf(C_O + stage_off, stage_cols)
            step = ((2 * bf_cols + 63) // 64) * 64
            self.bfo = [arb(C_O + bf_off + i * step, bf_cols) for i in range(2)]
            self.cast_eng = cast_eng
            self.cs = {"vector": "ve", "scalar": "se", "gpsimd": "po"}[cast_eng]
            self.ready = [0, 0]
            self.freepe = [P.val("pe"), P.val("pe")]
            self.stage_free = P.val(self.cs)

        def load(self, src_ap, buf, eng="sync"):
            n = _ncols(src_ap)
            three = len(src_ap.shape) == 3
            a = src_ap.shape[1] if three else 1
            dsem = "dma" if eng == "sync" else "dgp"
            qw = "ys" if eng == "sync" else "po"
            P.op(eng, lambda e, s=src_ap, n=n, a=a, th=three: e.dma_start(
                out=(self.stage[:, 0:n].rearrange("p (a b) -> p a b", a=a)
                     if th else self.stage[:, 0:n]), in_=s),
                wait=[(self.cs, self.stage_free), ("pe", self.freepe[buf])],
                inc=[(dsem, 16)])
            P.op(self.cast_eng, lambda e, b=buf, n=n: e.tensor_copy(
                self.bfo[b][:, 0:n], self.stage[:, 0:n]),
                wait=[qw], inc=[self.cs])
            self.ready[buf] = P.val(self.cs)
            self.stage_free = P.val(self.cs)

        def bf(self, buf):
            return self.bfo[buf]

        def wait_use(self, buf):
            return [(self.cs, self.ready[buf])]

        def mark_done(self, buf):
            self.freepe[buf] = P.val("pe")

    def wsrc(dram2d, colsl):
        return dram2d[:, colsl].rearrange("(k p) c -> p k c", p=128)

    def ln(src, ready=None):
        sq = [arf(C_O + i * 8192, T) for i in range(2)]
        entry = P.all_waits()
        pe_marks = {}
        for c in range(NCH):
            w = ([("pe", pe_marks.get(c - 2, 0))] if c >= 2 else
                 ([] if ready else list(entry)))
            if ready:
                w = w + list(ready(c))
            P.op("scalar", lambda e, c=c: e.activation(
                sq[c % 2], src(c), AF.Square), wait=w, inc=["se"])
            for tt in range(NTT):
                P.op("tensor", lambda e, c=c, tt=tt: e.matmul(
                    psn(tt, 1, 0, 1), C_("ones_col_f"),
                    src(c)[:, 512 * tt: 512 * (tt + 1)],
                    start=(c == 0), stop=(c == NCH - 1)),
                    wait=((P.all_waits() if not ready else list(ready(c)) +
                           [("pe", P.val("pe"))])
                          if (c == 0 and tt == 0) else
                          (list(ready(c)) if ready and tt == 0 else [])))
            for tt in range(NTT):
                P.op("tensor", lambda e, c=c, tt=tt: e.matmul(
                    psn(4 + tt, 1, 0, 1), C_("ones_col_f"),
                    sq[c % 2][:, 512 * tt: 512 * (tt + 1)],
                    start=(c == 0), stop=(c == NCH - 1)),
                    wait=(["se"] if tt == 0 else []),
                    inc=(["pe"] if tt == NTT - 1 else []))
            pe_marks[c] = P.val("pe")
        r_m, r_s2, r_r = rowf(0), rowf(1), rowf(2)
        P.op("scalar", lambda e: e.activation(
            r_m, PSAP[0:1, 0:2048], AF.Copy, scale=CN1(CN_INVD)), wait=["pe"])
        P.op("scalar", lambda e: e.activation(
            r_s2, PSAP[0:1, 2048:4096], AF.Copy, scale=CN1(CN_INVD)),
            inc=["se"])
        P.op("vector", lambda e: e.tensor_mul(r_r, r_m, r_m), wait=["se"])
        P.op("vector", lambda e: e.tensor_sub(r_s2, r_s2, r_r), inc=["ve"])
        P.op("scalar", lambda e: e.activation(
            r_s2, r_s2, AF.Sqrt, bias=CN1(CN_EPS)), wait=["ve"], inc=["se"])
        P.op("vector", lambda e: e.reciprocal(r_r, r_s2), wait=["se"])
        P.op("vector", lambda e: e.tensor_mul(r_m, r_m, r_r), inc=["ve"])
        rb_r = arb(A_O + 8192, T, parts=1)
        rb_m = arb(A_O + 8192 + 4096, T, parts=1)
        P.op("scalar", lambda e: e.copy(rb_r, r_r), wait=["ve"])
        P.op("scalar", lambda e: e.copy(rb_m, r_m), inc=["se"])
        for tt in range(NTT):
            P.op("tensor", lambda e, tt=tt: e.matmul(
                psn(tt), C_("ones_row_bf"), rb_r[:, 512 * tt: 512 * (tt + 1)],
                start=True, stop=True), wait=(["se"] if tt == 0 else []))
            P.op("tensor", lambda e, tt=tt: e.matmul(
                psn(4 + tt), C_("ones_row_bf"), rb_m[:, 512 * tt: 512 * (tt + 1)],
                start=True, stop=True),
                inc=(["pe"] if tt == NTT - 1 else []))
        tmp = arf(C_O, T)
        for c in range(NCH):
            P.op("vector", lambda e, c=c: e.tensor_mul(
                tmp, src(c), PSAP[:, 0:2048]), wait=(["pe"] if c == 0 else []))
            P.op("vector", lambda e, c=c: e.tensor_sub(
                xn_t(c), tmp, PSAP[:, 2048:4096]),
                inc=(["ve"] if c == NCH - 1 else []))

    def attn():
        kaug = arb(C_O + 0, T, parts=65)
        va = arb(C_O + 4224, 16 * 66)
        scr = [arb(C_O + 6400 + i * 2048, 1024) for i in range(3)]
        WSO = 12544
        qs = [arb(C_O + 20736 + i * 4160, T, parts=65) for i in range(4)]
        ao = [arb(C_O + 37376 + i * 4096, T) for i in range(8)]
        av_sb = arb(C_O + 70144, 512, parts=64)
        ptil = [arb(A_O + 8192 + i * 2048, 1024) for i in range(2)]
        sqscr = arb(A_O + 12288, T, parts=64)
        negk = arf(A_O + 0, 1, parts=1)

        def qst(pp, hh):
            return qs[2 * (pp % 2) + hh]

        wkv = WS(WSO, 1024, WSO + 4096, 1024)
        wkv.load(wsrc(wkp, slice(0, HD)), 0)
        wkv.load(wsrc(wv_d, slice(0, HD)), 1)
        for tt in range(NTT):
            for kk in range(NCH):
                P.op("tensor", lambda e, kk=kk, tt=tt: e.matmul(
                    psn(tt, 1, 0, 64), wkv.bf(0)[:, 64 * kk: 64 * (kk + 1)],
                    xn_t(kk)[:, 512 * tt: 512 * (tt + 1)],
                    start=(kk == 0), stop=(kk == NCH - 1)),
                    wait=(wkv.wait_use(0) + P.all_waits()
                          if kk == 0 and tt == 0 else []),
                    inc=(["pe"] if kk == NCH - 1 and tt == NTT - 1 else []))
        for hf in range(2):
            sl = slice(1024 * hf, 1024 * (hf + 1))
            P.op("vector", lambda e, sl=sl: e.tensor_mul(
                scr[0][0:64, :], PSAP[0:64, 0:2048][:, sl], C_("cosk")[0:64, sl]),
                wait=(["pe"] if hf == 0 else []))
            P.op("vector", lambda e, sl=sl: e.tensor_mul(
                scr[1][0:64, :], PSAP[0:64, 0:2048][:, sl], C_("s2k")[0:64, sl]),
                inc=["ve"])
            P.op("gpsimd", lambda e: e.dma_start(
                out=scr[2][0:32, :], in_=scr[1][32:64, :]),
                wait=["ve"], inc=[("dgp", 16)])
            P.op("gpsimd", lambda e: e.dma_start(
                out=scr[2][32:64, :], in_=scr[1][0:32, :]), inc=[("dgp", 16)])
            P.op("vector", lambda e, sl=sl: e.tensor_add(
                kaug[0:64, sl], scr[0][0:64, :], scr[2][0:64, :]),
                wait=["po"], inc=["ve"])
        P.op("vector", lambda e: e.memset(rowf(0), 1.0), inc=["ve"])
        P.op("scalar", lambda e: e.copy(
            arb(A_O, T, parts=1), rowf(0)), wait=["ve"], inc=["se"])
        P.op("gpsimd", lambda e: e.dma_start(
            out=kaug[64:65, :], in_=arb(A_O, T, parts=1)), wait=["se"],
            inc=[("dgp", 16)])
        for tt in range(NTT):
            for kk in range(NCH):
                P.op("tensor", lambda e, kk=kk, tt=tt: e.matmul(
                    psn(4 + tt, 1, 0, 64), wkv.bf(1)[:, 64 * kk: 64 * (kk + 1)],
                    xn_t(kk)[:, 512 * tt: 512 * (tt + 1)],
                    start=(kk == 0), stop=(kk == NCH - 1)),
                    wait=(wkv.wait_use(1) if kk == 0 and tt == 0 else []),
                    inc=(["pe"] if kk == NCH - 1 and tt == NTT - 1 else []))
        wkv.mark_done(0)
        wkv.mark_done(1)
        P.op("scalar", lambda e: e.copy(sqscr, PSAP[0:64, 2048:4096]),
             wait=["pe"], inc=["se"])
        for kt in range(16):
            P.op("tensor", lambda e, kt=kt: e.transpose(
                psbf(0, 64), sqscr[:, 128 * kt: 128 * (kt + 1)],
                C_("ident_bf")[0:64, 0:64]),
                wait=(["se", "ve"] if kt == 0 else ["ve"]), inc=["pe"])
            P.op("vector", lambda e, kt=kt: e.tensor_copy(
                va[:, 66 * kt: 66 * kt + 64], psbf(0, 64)), wait=["pe"])
            P.op("vector", lambda e, kt=kt: e.memset(
                va[:, 66 * kt + 64: 66 * kt + 65], 1.0), inc=["ve"])
        P.op("scalar", lambda e: e.activation(
            sqscr, kaug[0:64, :], AF.Square), wait=["ve"], inc=["se"])
        for tt in range(NTT):
            P.op("tensor", lambda e, tt=tt: e.matmul(
                psn(tt, 1, 0, 1), C_("ones_col_bf")[0:64, :],
                sqscr[:, 512 * tt: 512 * (tt + 1)], start=True, stop=True),
                wait=(["se"] if tt == 0 else []),
                inc=(["pe"] if tt == NTT - 1 else []))
        P.op("scalar", lambda e: e.activation(
            rowf(0), PSAP[0:1, 0:2048], AF.Sqrt),
            wait=["pe", ("po", P.val("po"))], inc=["se"])
        P.op("vector", lambda e: e.reduce_max(
            rowf(2, 1), rowf(0), axis=AX.X), wait=["se"])
        P.op("vector", lambda e: e.tensor_scalar(
            negk, rowf(2, 1), CN1(CN_NK), None, ALU.mult), inc=["ve"])

        wq = WS(WSO, 1024, WSO + 4096, 1024)
        exp_mark = {}
        avsb_mark = 0
        bcfree_mark = 0
        pair_done_pe = {}
        rope_done_ve = {}
        qrdy = {}

        def qproj(pp):
            buf = pp % 2
            wq.load(wsrc(wqT, slice(128 * pp, 128 * (pp + 1))), buf)
            for tt in range(NTT):
                for kk in range(NCH):
                    P.op("tensor", lambda e, kk=kk, tt=tt, buf=buf: e.matmul(
                        psn(tt), wq.bf(buf)[:, 128 * kk: 128 * (kk + 1)],
                        xn_t(kk)[:, 512 * tt: 512 * (tt + 1)],
                        start=(kk == 0), stop=(kk == NCH - 1)),
                        wait=(wq.wait_use(buf)
                              + [("ve", rope_done_ve.get(pp - 1, 0))]
                              if kk == 0 and tt == 0 else []),
                        inc=(["pe"] if kk == NCH - 1 and tt == NTT - 1 else []))
            wq.mark_done(buf)
            return P.val("pe")

        def rope_q(pp, qpe):
            for hf in range(2):
                sl = slice(1024 * hf, 1024 * (hf + 1))
                P.op("vector", lambda e, sl=sl: e.tensor_mul(
                    scr[0], PSAP[:, 0:2048][:, sl], C_("cosq")[:, sl]),
                    wait=([("pe", qpe)] if hf == 0 else [])
                    + [("po", P.val("po"))])
                P.op("vector", lambda e, sl=sl: e.tensor_mul(
                    scr[1], PSAP[:, 0:2048][:, sl], C_("s2q")[:, sl]),
                    inc=["ve"])
                for blk in range(4):
                    so, do = [32, 0, 96, 64][blk], [0, 32, 64, 96][blk]
                    P.op("gpsimd", lambda e, so=so, do=do: e.dma_start(
                        out=scr[2][do:do + 32, :], in_=scr[1][so:so + 32, :]),
                        wait=(["ve"] if blk == 0 else []), inc=[("dgp", 16)])
                P.op("vector", lambda e: e.tensor_add(
                    scr[1], scr[0], scr[2]), wait=["po"], inc=["ve"])
                for hh in range(2):
                    P.op("gpsimd", lambda e, hh=hh, sl=sl, pp=pp: e.dma_start(
                        out=qst(pp, hh)[0:64, sl],
                        in_=scr[1][64 * hh: 64 * hh + 64, :]),
                        wait=["ve", ("pe", pair_done_pe.get(pp - 2, 0))],
                        inc=[("dgp", 16)])
            rope_done_ve[pp] = P.val("ve")

        def bounds(pp):
            for hh in range(2):
                P.op("scalar", lambda e, pp=pp, hh=hh: e.activation(
                    sqscr, qst(pp, hh)[0:64, :], AF.Square),
                    wait=["po"], inc=["se"])
                for tt in range(NTT):
                    P.op("tensor", lambda e, tt=tt: e.matmul(
                        psn(tt, 1, 0, 1), C_("ones_col_bf")[0:64, :],
                        sqscr[:, 512 * tt: 512 * (tt + 1)], start=True,
                        stop=True), wait=(["se"] if tt == 0 else []),
                        inc=(["pe"] if tt == NTT - 1 else []))
                P.op("scalar", lambda e: e.activation(
                    rowf(2), PSAP[0:1, 0:2048], AF.Sqrt),
                    wait=["pe", ("ve", P.val("ve"))], inc=["se"])
                P.op("vector", lambda e, pp=pp, hh=hh: e.tensor_scalar(
                    qst(pp, hh)[64:65, :], rowf(2), negk, CN1(CN_HALF),
                    ALU.mult, ALU.subtract), wait=["se"], inc=["ve"])
            qrdy[pp] = P.val("ve")

        def st_av(pp):
            nonlocal avsb_mark, bcfree_mark
            for hh in range(2):
                for qt in range(NTT):
                    nkt = 4 * (qt + 1)
                    for kt2 in range(nkt // 2):
                        for half in range(2):
                            kt = 2 * kt2 + half
                            P.op("tensor", lambda e, kt=kt, qt=qt, pp=pp,
                                 hh=hh, half=half: e.matmul(
                                     psn(4 + half),
                                     kaug[:, 128 * kt: 128 * (kt + 1)],
                                     qst(pp, hh)[:, 512 * qt: 512 * (qt + 1)],
                                     start=True, stop=True),
                                 wait=[("ve", qrdy[pp]),
                                       ("se", exp_mark.get("e", 0))],
                                 inc=["pe"])
                        st_pe = P.val("pe")
                        pt = ptil[kt2 % 2]
                        P.op("scalar", lambda e, pt=pt: e.activation(
                            pt, PSAP[:, 2048:3072], AF.Exp),
                            wait=[("pe", st_pe),
                                  ("po", exp_mark.get(("a", kt2 % 2), 0)),
                                  ("pe", exp_mark.get(("av", kt2 % 2), 0))],
                            inc=["se"])
                        exp_mark["e"] = P.val("se")
                        for half in range(2):
                            kt = 2 * kt2 + half
                            diag = kt >= 4 * qt
                            if diag:
                                P.op("gpsimd", lambda e, kt=kt, qt=qt, pt=pt,
                                     half=half: e.affine_select(
                                         pt[:, 512 * half: 512 * (half + 1)],
                                         pt[:, 512 * half: 512 * (half + 1)],
                                         pattern=[[1, 512]],
                                         compare_op=ALU.is_ge, fill=0.0,
                                         base=512 * qt - 128 * kt,
                                         channel_multiplier=-1),
                                     wait=["se"], inc=["po"])
                        exp_mark[("a", kt2 % 2)] = P.val("po")
                        for half in range(2):
                            kt = 2 * kt2 + half
                            diag = kt >= 4 * qt
                            P.op("tensor", lambda e, kt=kt, pt=pt, half=half,
                                 nkt=nkt: e.matmul(
                                     psn(6, 1, 0, 65),
                                     va[:, 66 * kt: 66 * kt + 65],
                                     pt[:, 512 * half: 512 * (half + 1)],
                                     start=(kt == 0), stop=(kt == nkt - 1)),
                                 wait=([("po", P.val("po"))] if diag
                                       else [("se", exp_mark["e"])])
                                 + ([("se", avsb_mark)] if kt == 0 else []),
                                 inc=["pe"])
                        exp_mark[("av", kt2 % 2)] = P.val("pe")
                    av_pe = P.val("pe")
                    P.op("vector", lambda e: e.reciprocal(
                        rowf(2, 512), PSAP[64:65, 3072:3584]),
                        wait=[("pe", av_pe)])
                    P.op("scalar", lambda e: e.copy(
                        arb(A_O + 20480, 512, parts=1), rowf(2, 512)),
                        wait=["ve"], inc=["se"])
                    P.op("tensor", lambda e: e.matmul(
                        psn(7, 1, 0, 64), C_("ones_row_bf")[:, 0:64],
                        arb(A_O + 20480, 512, parts=1), start=True, stop=True),
                        wait=["se", ("ve", bcfree_mark)], inc=["pe"])
                    P.op("scalar", lambda e: e.copy(
                        av_sb, PSAP[0:64, 3072:3584]), wait=["pe"], inc=["se"])
                    avsb_mark = P.val("se")
                    if hh == 0:
                        P.op("vector", lambda e, pp=pp, qt=qt: e.tensor_mul(
                            ao[pp][0:64, 512 * qt: 512 * (qt + 1)], av_sb,
                            PSAP[0:64, 3584:4096]), wait=["se"], inc=["ve"])
                    else:
                        P.op("vector", lambda e: e.tensor_mul(
                            scr[2][0:64, 0:512], av_sb,
                            PSAP[0:64, 3584:4096]),
                            wait=["se", ("po", P.val("po"))], inc=["ve"])
                        P.op("gpsimd", lambda e, pp=pp, qt=qt: e.dma_start(
                            out=ao[pp][64:128, 512 * qt: 512 * (qt + 1)],
                            in_=scr[2][0:64, 0:512]), wait=["ve"],
                            inc=[("dgp", 16)])
                    bcfree_mark = P.val("ve")
            pair_done_pe[pp] = P.val("pe")

        qpe = qproj(0)
        rope_q(0, qpe)
        bounds(0)
        for pp in range(8):
            if pp + 1 < 8:
                qpe = qproj(pp + 1)
                rope_q(pp + 1, qpe)
            st_av(pp)
            if pp + 1 < 8:
                bounds(pp + 1)

        wo = WS(WSO, 1024, WSO + 4096, 1024)
        wo.load(wsrc(woT, slice(0, 128)), 0)
        stt_mark = {}
        for m in range(NCH):
            buf = m % 2
            if m + 1 < NCH:
                wo.load(wsrc(woT, slice(128 * (m + 1), 128 * (m + 2))),
                        (m + 1) % 2)
            pb = (m % 2) * 4
            for tt in range(NTT):
                for kk in range(NCH):
                    P.op("tensor", lambda e, kk=kk, tt=tt, buf=buf, pb=pb:
                         e.matmul(psn(pb + tt),
                                  wo.bf(buf)[:, 128 * kk: 128 * (kk + 1)],
                                  ao[kk][:, 512 * tt: 512 * (tt + 1)],
                                  start=(kk == 0), stop=(kk == NCH - 1)),
                         wait=((wo.wait_use(buf)
                                + [("ve", P.val("ve")),
                                   ("po", P.val("po")),
                                   ("ve", stt_mark.get(m - 2, 0))])
                               if kk == 0 and tt == 0 else []),
                         inc=(["pe"] if kk == NCH - 1 and tt == NTT - 1 else []))
            wo.mark_done(buf)
            P.op("vector", lambda e, m=m, pb=pb: e.scalar_tensor_tensor(
                X(m), PSAP[:, 2048 * (m % 2): 2048 * (m % 2) + 2048],
                CNs(CN_ONE), X(m), ALU.mult, ALU.add),
                wait=["pe"], inc=["ve"])
            stt_mark[m] = P.val("ve")
        attn._chunk_marks = dict(stt_mark)

    def ffn():
        sh = [arb(C_O + 4096 * m, T) for m in range(NCH)]
        W1_O = 32768
        w2q = arb(C_O + 49280, NCH * 1024)
        w2stage = arf(A_O + 16384, T)
        gf8 = arb(A_O, T, parts=8)
        grow = arb(A_O + 4096, T, parts=1)
        gate_bc = arb(A_O + 8192, T)
        gsc = [arf(A_O + 12288 + i * 64, 8) for i in range(4)]
        gcol = [arf(A_O + 12544 + i * 16, 1) for i in range(4)]
        gwb = arb(A_O + 12672, NCH * NE)
        gstage = arf(A_O + 12928, NCH * NE)

        P.op("sync", lambda e: e.dma_start(out=gstage, in_=gateT),
             wait=P.all_waits(), inc=[("dma", 16)])
        P.op("vector", lambda e: e.tensor_copy(gwb, gstage), wait=["ys"],
             inc=["ve"])
        for tc in range(16):
            for kk in range(NCH):
                P.op("tensor", lambda e, tc=tc, kk=kk: e.matmul(
                    PSAP[:, 8 * tc: 8 * tc + 8],
                    xn_t(kk)[:, 128 * tc: 128 * (tc + 1)],
                    gwb[:, 8 * kk: 8 * (kk + 1)],
                    start=(kk == 0), stop=(kk == NCH - 1)),
                    wait=(["ve"] + P.all_waits() if tc == 0 and kk == 0 else []),
                    inc=(["pe"] if kk == NCH - 1 else []))
        tr_mark = 0
        for tc in range(16):
            lg = PSAP[:, 8 * tc: 8 * tc + 8]
            P.op("vector", lambda e, lg=lg: e.reduce_max(
                gcol[0], lg, axis=AX.X), wait=(["pe"] if tc == 0 else []))
            P.op("vector", lambda e, lg=lg: e.tensor_scalar(
                gsc[0], lg, gcol[0], None, ALU.is_equal))
            P.op("vector", lambda e, lg=lg: e.scalar_tensor_tensor(
                gsc[1], gsc[0], CNs(CN_NINF), lg, ALU.mult, ALU.add))
            P.op("vector", lambda e: e.reduce_max(gcol[1], gsc[1], axis=AX.X))
            P.op("vector", lambda e: e.tensor_sub(gcol[2], gcol[0], gcol[1]),
                 inc=["ve"])
            P.op("scalar", lambda e: e.activation(
                gcol[3], gcol[2], AF.Sigmoid), wait=["ve"], inc=["se"])
            P.op("vector", lambda e, lg=lg: e.tensor_scalar(
                gsc[2], lg, gcol[1], None, ALU.is_equal), wait=["se"])
            P.op("vector", lambda e: e.tensor_scalar(
                gsc[3], gsc[2], gcol[3], CNs(CN_NEG1), ALU.mult, ALU.mult))
            P.op("vector", lambda e: e.tensor_scalar(
                gsc[0], gsc[0], gcol[3], None, ALU.mult))
            P.op("vector", lambda e: e.tensor_add(gsc[0], gsc[0], gsc[2]))
            P.op("vector", lambda e: e.tensor_add(gsc[0], gsc[0], gsc[3]))
            P.op("vector", lambda e: e.tensor_copy(
                gsc[1].bitcast(BF16)[:, 0:8], gsc[0]), inc=["ve"])
            P.op("tensor", lambda e: e.transpose(
                psbf(4, 128, 0, 8), gsc[1].bitcast(BF16)[:, 0:8],
                C_("ident_bf")), wait=["ve", ("ve", tr_mark)], inc=["pe"])
            P.op("vector", lambda e, tc=tc: e.tensor_copy(
                gf8[:, 128 * tc: 128 * (tc + 1)], psbf(4, 128, 0, 8)),
                wait=["pe"], inc=["ve"])
            tr_mark = P.val("ve")

        w1 = WS(W1_O, 2048, W1_O + 8192, 2048)
        silu_mark = {}
        stt_mark = {}
        w2cast = [P.val("ve")]
        for ei in range(NEX):
            if ei >= 1:
                P.op("gpsimd", lambda e, ei=ei: e.dma_start(
                    out=grow, in_=gf8[ei - 1:ei, :]),
                    wait=["ve", "pe", "po"], inc=[("dgp", 16)])
                for tt in range(NTT):
                    P.op("tensor", lambda e, tt=tt: e.matmul(
                        psn(4 + tt), C_("ones_row_bf"),
                        grow[:, 512 * tt: 512 * (tt + 1)], start=True,
                        stop=True),
                        wait=(["po", ("se", silu_mark.get("last", 0)),
                               ("ve", stt_mark.get("last", 0))]
                              if tt == 0 else []),
                        inc=(["pe"] if tt == NTT - 1 else []))
                P.op("vector", lambda e: e.tensor_copy(
                    gate_bc, PSAP[:, 2048:4096]), wait=["pe"], inc=["ve"])
            gbc_ve = P.val("ve")
            for q in range(4):
                w2pe = stt_mark.get("w2pe", 0)

                def w1load(g, q=q, ei=ei):
                    w1.load(wsrc(ffn_w1[ei],
                                 slice(1024 * q + 256 * g,
                                       1024 * q + 256 * (g + 1))), g % 2)

                def w2load(g2, q=q, ei=ei):
                    P.op("sync", lambda e, g2=g2, ei=ei, q=q: e.dma_start(
                        out=w2stage.rearrange("p (a b) -> p a b", a=2),
                        in_=ffn_w2[ei][1024 * q + 256 * g2:
                                       1024 * q + 256 * (g2 + 1), :]
                        .rearrange("(k p) c -> p k c", p=128)),
                        wait=[("ve", w2cast[0]), ("pe", w2pe)],
                        inc=[("dma", 16)])
                    P.op("vector", lambda e, g2=g2: e.tensor_copy(
                        w2q[:, 2048 * g2: 2048 * (g2 + 1)], w2stage),
                        wait=["ys"], inc=["ve"])
                    w2cast[0] = P.val("ve")

                w1load(0)
                w1load(1)
                w2load(0)
                w2load(1)
                for m in range(NCH):
                    g = m // 2
                    buf = g % 2
                    loc = m % 2
                    if loc == 1:
                        if g + 1 < 4:
                            w1load(g + 1)
                        if g < 2:
                            w2load(g + 2)
                    pb = (m % 2) * 4
                    for tt in range(NTT):
                        for kk in range(NCH):
                            P.op("tensor", lambda e, kk=kk, tt=tt, buf=buf,
                                 loc=loc, pb=pb: e.matmul(
                                     psn(pb + tt),
                                     w1.bf(buf)[:, 256 * kk + 128 * loc:
                                                256 * kk + 128 * (loc + 1)],
                                     xn_t(kk)[:, 512 * tt: 512 * (tt + 1)],
                                     start=(kk == 0), stop=(kk == NCH - 1)),
                                 wait=((w1.wait_use(buf)
                                        + [("se", silu_mark.get((q, m - 2), 0)),
                                           ("ve", gbc_ve),
                                           ("ve", stt_mark.get("last", 0))])
                                       if kk == 0 and tt == 0 else []),
                                 inc=(["pe"] if kk == NCH - 1 and tt == NTT - 1
                                      else []))
                    if loc == 1:
                        w1.mark_done(buf)
                    P.op("scalar", lambda e, m=m, pb=pb: e.activation(
                        sh[m], PSAP[:, 2048 * (m % 2): 2048 * (m % 2) + 2048],
                        AF.Silu), wait=["pe"], inc=["se"])
                    silu_mark[(q, m)] = P.val("se")
                    if ei >= 1:
                        P.op("vector", lambda e, m=m: e.tensor_mul(
                            sh[m], sh[m], gate_bc), wait=["se"], inc=["ve"])
                silu_mark["last"] = P.val("se")
                pomark = P.val("ve")
                w2rdy = w2cast[0]
                for m2 in range(NCH):
                    pb = (m2 % 2) * 4
                    for tt in range(NTT):
                        for k2 in range(NCH):
                            P.op("tensor", lambda e, m2=m2, k2=k2, tt=tt,
                                 pb=pb: e.matmul(
                                     psn(pb + tt),
                                     w2q[:, 1024 * k2 + 128 * m2:
                                         1024 * k2 + 128 * (m2 + 1)],
                                     sh[k2][:, 512 * tt: 512 * (tt + 1)],
                                     start=(k2 == 0), stop=(k2 == NCH - 1)),
                                 wait=(([("ve", w2rdy), ("ve", pomark),
                                         ("se", silu_mark["last"]),
                                         ("ve", stt_mark.get(m2 - 2, 0))])
                                       if k2 == 0 and tt == 0 else []),
                                 inc=(["pe"] if k2 == NCH - 1 and tt == NTT - 1
                                      else []))
                    P.op("vector", lambda e, m2=m2, pb=pb:
                         e.scalar_tensor_tensor(
                             X(m2),
                             PSAP[:, 2048 * (m2 % 2): 2048 * (m2 % 2) + 2048],
                             CNs(CN_ONE), X(m2), ALU.mult, ALU.add),
                         wait=["pe"], inc=["ve"])
                    stt_mark[m2] = P.val("ve")
                stt_mark["last"] = P.val("ve")
                stt_mark["w2pe"] = P.val("pe")
        ffn._chunk_marks = dict((m2, stt_mark[m2]) for m2 in range(NCH))

    def ffn_sparse(last=False):
        ph0 = P.phase
        entry = P.all_waits()
        xstage = [arb(A_O + 2112 * i, D + 8) for i in range(2)]
        gsc = [arf(A_O + 4352 + i * 64, 8) for i in range(4)]
        gcol = [arf(A_O + 4608 + i * 16, 1) for i in range(4)]
        gwb = arb(A_O + 4672, NCH * NE)
        gstage = arf(A_O + 4864, NCH * NE)
        mstg = arf(C_O + 65728, 128)
        ctok = arf(C_O + 66240, 8)
        tok1 = arf(C_O + 66272, 8)
        tokm = arf(C_O + 66304, 8)
        tok4 = arf(C_O + 66336, 8)
        tok5 = arf(C_O + 66368, 8)
        p2c = arf(C_O + 66400, 1)
        pmc = arf(C_O + 66416, 1)
        itok = arf(C_O + 67264, 8).bitcast(I32)
        imt = arf(C_O + 67296, 8).bitcast(I32)
        pcf = arf(C_O + 66432, 32)
        pcolsi = arf(C_O + 66560, 32).bitcast(I32)
        icols = arf(C_O + 66688, 16).bitcast(I32)
        idxc = arf(C_O + 66752, 64).bitcast(I32)
        ggatf = arf(C_O + 67008, 64)
        zrow = arb(A_O + 8192, D + 8, parts=1, p0=0)
        m8 = arf(A_O + 8192, T, parts=8, p0=32)
        cum8 = arf(A_O + 0, T, parts=8, p0=32)
        sh = [arb(C_O + 4096 * m, T) for m in range(NCH)]
        W1_O = 32832
        w2q = arb(C_O + 49344, NCH * 1024)
        w2stage = arf(A_O + 16384, T)
        xgT = [arb(C_O + 8192 * i, 4096) for i in range(2)]
        xg = arb(C_O + 16384, 4 * (D + 8))
        yT = arb(C_O + 24640, 4096)
        shq = arb(B_O, 8192)
        ysb = arb(B_O + 16384, 8192)

        P.op("vector", lambda e: e.memset(zrow, 0.0), wait=entry, inc=["ve"])
        P.op("sync", lambda e: e.dma_start(
            out=ysd[NSLOT:NSLOT + 1, :], in_=zrow[:, 0:D]),
            wait=["ve"] + entry, inc=[("dma", 16)])
        P.op("sync", lambda e: e.dma_start(
            out=xnd[T:T + 1, :], in_=zrow), inc=[("dma", 16)])
        P.op("sync", lambda e: e.dma_start(out=gstage, in_=gateT),
             inc=[("dma", 16)])
        P.op("vector", lambda e: e.tensor_copy(gwb, gstage), wait=["ys"],
             inc=["ve"])
        xev, xys = {}, {}
        tr_mark = 0
        for tc in range(16):
            pbx = psbf(4 + tc % 2, 1024)
            for c in range(NCH):
                P.op("tensor", lambda e, c=c, tc=tc, pbx=pbx: e.transpose(
                    pbx[:, 128 * c: 128 * (c + 1)],
                    xn_t(c)[:, 128 * tc: 128 * (tc + 1)], C_("ident_bf")),
                    wait=((entry + ["ve"]) if tc == 0 and c == 0 else
                          ([("ve", xev.get(tc - 2, 0))] if c == 0 else [])),
                    inc=(["pe"] if c == NCH - 1 else []))
            P.op("vector", lambda e, tc=tc, pbx=pbx: e.tensor_copy(
                xstage[tc % 2][:, 0:D], pbx),
                wait=["pe", ("ys", xys.get(tc - 2, 0))], inc=["ve"])
            xev[tc] = P.val("ve")
            for kk in range(NCH):
                P.op("tensor", lambda e, tc=tc, kk=kk: e.matmul(
                    PSAP[:, 8 * tc: 8 * tc + 8],
                    xn_t(kk)[:, 128 * tc: 128 * (tc + 1)],
                    gwb[:, 8 * kk: 8 * (kk + 1)],
                    start=(kk == 0), stop=(kk == NCH - 1)),
                    wait=(["ve"] if tc == 0 and kk == 0 else []),
                    inc=(["pe"] if kk == NCH - 1 else []))
            lg = PSAP[:, 8 * tc: 8 * tc + 8]
            P.op("vector", lambda e, lg=lg: e.reduce_max(
                gcol[0], lg, axis=AX.X), wait=["pe"])
            P.op("vector", lambda e, lg=lg: e.tensor_scalar(
                gsc[0], lg, gcol[0], None, ALU.is_equal))
            P.op("vector", lambda e, lg=lg: e.scalar_tensor_tensor(
                gsc[1], gsc[0], CNs(CN_NINF), lg, ALU.mult, ALU.add))
            P.op("vector", lambda e: e.reduce_max(gcol[1], gsc[1], axis=AX.X))
            P.op("vector", lambda e: e.tensor_sub(gcol[2], gcol[0], gcol[1]),
                 inc=["ve"])
            P.op("scalar", lambda e: e.activation(
                gcol[3], gcol[2], AF.Sigmoid), wait=["ve"], inc=["se"])
            P.op("vector", lambda e, lg=lg: e.tensor_scalar(
                gsc[2], lg, gcol[1], None, ALU.is_equal), wait=["se"])
            P.op("vector", lambda e: e.tensor_scalar(
                gsc[3], gsc[2], gcol[3], CNs(CN_NEG1), ALU.mult, ALU.mult))
            P.op("vector", lambda e: e.tensor_scalar(
                gsc[0], gsc[0], gcol[3], None, ALU.mult))
            P.op("vector", lambda e: e.tensor_add(gsc[0], gsc[0], gsc[2]))
            P.op("vector", lambda e: e.tensor_add(gsc[0], gsc[0], gsc[3]))
            P.op("vector", lambda e, tc=tc: e.tensor_copy(
                xstage[tc % 2][:, D:D + 8], gsc[0]))
            P.op("vector", lambda e, tc=tc: e.tensor_scalar(
                mstg[:, 8 * tc: 8 * tc + 8], gsc[0], 0.0, None, ALU.is_gt),
                inc=["ve"])
            P.op("sync", lambda e, tc=tc: e.dma_start(
                out=xnd[128 * tc: 128 * (tc + 1), :], in_=xstage[tc % 2]),
                wait=["ve"], inc=[("dma", 16)])
            xys[tc] = P.val("ys")
            P.op("tensor", lambda e, tc=tc: e.transpose(
                PSAP[0:8, 3072:3200], mstg[:, 8 * tc: 8 * tc + 8],
                C_("ident_f")),
                wait=["ve", ("ve", tr_mark)], inc=["pe"])
            P.op("vector", lambda e, tc=tc: e.tensor_copy(
                m8[:, 128 * tc: 128 * (tc + 1)], PSAP[0:8, 3072:3200]),
                wait=["pe"], inc=["ve"])
            tr_mark = P.val("ve")
        xnd_ys = P.val("ys")

        P.op("vector", lambda e: e.tensor_tensor_scan(
            cum8, m8, m8, 0.0, ALU.add, ALU.bypass),
            wait=[("ys", xnd_ys)], inc=["ve"])
        tr2 = {}
        for tc in range(16):
            bk6 = 3072 + 512 * (tc % 2)
            P.op("tensor", lambda e, tc=tc, bk6=bk6: e.transpose(
                PSAP[:, bk6: bk6 + 8],
                cum8[:, 128 * tc: 128 * (tc + 1)],
                C_("ident_f")[32:40, 32:40]),
                wait=(["ve"] if tc == 0 else [("ve", tr2.get(tc - 2, 0))]),
                inc=["pe"])
            P.op("vector", lambda e, bk6=bk6: e.tensor_copy(
                ctok, PSAP[:, bk6: bk6 + 8]), wait=["pe"])
            P.op("vector", lambda e: e.tensor_scalar(
                tok1, ctok, float(CAP), None, ALU.is_le))
            P.op("vector", lambda e: e.tensor_scalar(
                ctok, ctok, -1.0, None, ALU.add))
            P.op("vector", lambda e: e.tensor_copy(itok, ctok))
            P.op("vector", lambda e: e.tensor_scalar(
                imt, itok, 7, None, ALU.bitwise_and))
            P.op("vector", lambda e: e.tensor_copy(tokm, imt))
            P.op("vector", lambda e: e.tensor_sub(ctok, ctok, tokm))
            P.op("vector", lambda e: e.tensor_scalar(
                ctok, ctok, 8.0, None, ALU.mult))
            P.op("vector", lambda e: e.tensor_add(ctok, ctok, tokm))
            P.op("vector", lambda e: e.tensor_add(ctok, ctok, C_("ecol8")))
            P.op("vector", lambda e: e.tensor_scalar(
                ctok, ctok, -float(NSLOT), None, ALU.add))
            P.op("vector", lambda e: e.tensor_mul(ctok, ctok, tok1))
            P.op("vector", lambda e: e.tensor_scalar(
                ctok, ctok, float(NSLOT), None, ALU.add))
            P.op("vector", lambda e, tc=tc: e.tensor_tensor_scan(
                tok4, mstg[:, 8 * tc: 8 * tc + 8],
                mstg[:, 8 * tc: 8 * tc + 8], 0.0, ALU.add, ALU.bypass))
            P.op("vector", lambda e: e.tensor_scalar(
                tok5, tok4, 1.0, None, ALU.is_equal))
            P.op("vector", lambda e, tc=tc: e.tensor_mul(
                tok5, tok5, mstg[:, 8 * tc: 8 * tc + 8]))
            P.op("vector", lambda e: e.tensor_scalar(
                tok4, tok4, 2.0, None, ALU.is_equal))
            P.op("vector", lambda e, tc=tc: e.tensor_mul(
                tok4, tok4, mstg[:, 8 * tc: 8 * tc + 8]))
            P.op("vector", lambda e: e.reduce_sum(pmc, tok4, axis=AX.X))
            P.op("vector", lambda e: e.tensor_mul(tok5, tok5, ctok))
            P.op("vector", lambda e, tc=tc: e.reduce_sum(
                pcf[:, tc: tc + 1], tok5, axis=AX.X))
            P.op("vector", lambda e: e.tensor_mul(tok4, tok4, ctok))
            P.op("vector", lambda e: e.reduce_sum(p2c, tok4, axis=AX.X))
            P.op("vector", lambda e: e.tensor_scalar(
                p2c, p2c, -float(NSLOT), None, ALU.add))
            P.op("vector", lambda e: e.tensor_mul(p2c, p2c, pmc))
            P.op("vector", lambda e, tc=tc: e.tensor_scalar(
                pcf[:, 16 + tc: 17 + tc], p2c, float(NSLOT), None, ALU.add),
                inc=["ve"])
            tr2[tc] = P.val("ve")
        P.op("vector", lambda e: e.tensor_copy(pcolsi, pcf), inc=["ve"])
        vposm = P.val("ve")
        P.op("gpsimd", lambda e: e.iota(
            icols, pattern=[[128, 16]], base=0, channel_multiplier=1),
            wait=entry, inc=["po"])
        P.op("gpsimd", lambda e: e.memset(idxc, T), inc=["po"])
        P.op("gpsimd", lambda e: e.dma_start(
            out=idxd[0:NSLOT].rearrange("(p f) o -> p (f o)", p=128),
            in_=idxc), wait=["po"], inc=[("dgp", 16)])
        for j in range(32):
            P.op("gpsimd", lambda e, j=j: e.indirect_dma_start(
                idxd, IndirectOffsetOnAxis(pcolsi[:, j:j + 1], 0),
                icols[:, j % 16: j % 16 + 1], None),
                wait=([("ve", vposm), "po"] if j == 0 else []),
                inc=[("dgp", 16)])
        P.op("gpsimd", lambda e: e.dma_start(
            out=idxc, in_=idxd[0:NSLOT].rearrange("(p f) o -> p (f o)",
                                                  p=128)),
            inc=[("dgp", 16)])

        P.phase = ph0 + "s"
        w1 = WS(W1_O, 2048, W1_O + 8192, 2048)
        silu_mark = {}
        stt_mark = {}
        w2cast = [P.val("ve")]

        def w1load(g, ei, q, eng="sync"):
            w1.load(wsrc(ffn_w1[ei], slice(1024 * q + 256 * g,
                                           1024 * q + 256 * (g + 1))),
                    g % 2, eng=eng)

        def w2load(g2, ei, q):
            P.op("sync", lambda e, g2=g2, ei=ei, q=q: e.dma_start(
                out=w2stage.rearrange("p (a b) -> p a b", a=2),
                in_=ffn_w2[ei][1024 * q + 256 * g2:
                               1024 * q + 256 * (g2 + 1), :]
                .rearrange("(k p) c -> p k c", p=128)),
                wait=[("ve", w2cast[0]), ("pe", stt_mark.get("w2pe", 0))],
                inc=[("dma", 16)])
            P.op("vector", lambda e, g2=g2: e.tensor_copy(
                w2q[:, 2048 * g2: 2048 * (g2 + 1)], w2stage),
                wait=["ys"], inc=["ve"])
            w2cast[0] = P.val("ve")

        for q in range(4):
            w1load(0, 0, q)
            w1load(1, 0, q)
            w2load(0, 0, q)
            w2load(1, 0, q)
            for m in range(NCH):
                g = m // 2
                buf = g % 2
                loc = m % 2
                if loc == 1 and g < 2:
                    w2load(g + 2, 0, q)
                pb = (m % 2) * 4
                for tt in range(NTT):
                    for kk in range(NCH):
                        P.op("tensor", lambda e, kk=kk, tt=tt, buf=buf,
                             loc=loc, pb=pb: e.matmul(
                                 psn(pb + tt),
                                 w1.bf(buf)[:, 256 * kk + 128 * loc:
                                            256 * kk + 128 * (loc + 1)],
                                 xn_t(kk)[:, 512 * tt: 512 * (tt + 1)],
                                 start=(kk == 0), stop=(kk == NCH - 1)),
                             wait=((w1.wait_use(buf)
                                    + [("se", silu_mark.get(("s", q, m - 2), 0)),
                                       ("ve", vposm),
                                       ("ve", stt_mark.get("last", 0))])
                                   if kk == 0 and tt == 0 else []),
                             inc=(["pe"] if kk == NCH - 1 and tt == NTT - 1
                                  else []))
                if loc == 1:
                    w1.mark_done(buf)
                    if g + 2 < 4:
                        w1load(g + 2, 0, q)
                P.op("scalar", lambda e, m=m, pb=pb: e.activation(
                    sh[m], PSAP[:, 2048 * (m % 2): 2048 * (m % 2) + 2048],
                    AF.Silu), wait=["pe"], inc=["se"])
                silu_mark[("s", q, m)] = P.val("se")
            silu_mark["last"] = P.val("se")
            w2rdy = w2cast[0]
            for m2 in range(NCH):
                pb = (m2 % 2) * 4
                for tt in range(NTT):
                    for k2 in range(NCH):
                        P.op("tensor", lambda e, m2=m2, k2=k2, tt=tt,
                             pb=pb: e.matmul(
                                 psn(pb + tt),
                                 w2q[:, 1024 * k2 + 128 * m2:
                                     1024 * k2 + 128 * (m2 + 1)],
                                 sh[k2][:, 512 * tt: 512 * (tt + 1)],
                                 start=(k2 == 0), stop=(k2 == NCH - 1)),
                             wait=(([("ve", w2rdy),
                                     ("se", silu_mark["last"]),
                                     ("ve", stt_mark.get(m2 - 2, 0))])
                                   if k2 == 0 and tt == 0 else []),
                             inc=(["pe"] if k2 == NCH - 1 and tt == NTT - 1
                                  else []))
                P.op("vector", lambda e, m2=m2, pb=pb:
                     e.scalar_tensor_tensor(
                         X(m2),
                         PSAP[:, 2048 * (m2 % 2): 2048 * (m2 % 2) + 2048],
                         CNs(CN_ONE), X(m2), ALU.mult, ALU.add),
                     wait=["pe"], inc=["ve"])
                stt_mark[m2] = P.val("ve")
            stt_mark["last"] = P.val("ve")
            stt_mark["w2pe"] = P.val("pe")
        shared_pe = P.val("pe")

        P.phase = ph0 + "r"
        bank_ev = {bk: P.val("ve") for bk in range(8)}
        xgT_ev = {0: 0, 1: 0}
        ysb_mark = {}
        ytr_pe = {0: 0}
        ytr_ve = {0: 0}
        ytr_ve_g0 = {0: 0}
        yst_ys = {}

        def gathers(ei, grp, wv):
            e8 = ei - 1
            for j in range(4):
                c = 8 * e8 + 4 * grp + j
                P.op("gpsimd", lambda e, c=c, j=j: e.indirect_dma_start(
                    xg[:, 1032 * j: 1032 * (j + 1)], None, xnd,
                    IndirectOffsetOnAxis(idxc[:, c:c + 1], 0)),
                    wait=(wv if j == 0 else []), inc=[("dgp", 16)])

        P.op("gpsimd", lambda e: e.memset(xg[:], 0.0),
             wait=[("pe", shared_pe)], inc=["po"])
        gathers(1, 0, [("pe", shared_pe), ("ys", xnd_ys), "po"])
        g0po = {1: P.val("po")}

        for ei in range(1, NEX):
            e8 = ei - 1
            for grp in range(2):
                if grp == 1:
                    w1load(0, ei, 0, "gpsimd")
                    w1load(1, ei, 0, "gpsimd")
                    gathers(ei, 1, [("ve", xgT_ev[0])])
                    g1po = P.val("po")
                for j in range(4):
                    rot = 4 * grp + j
                    P.op("vector", lambda e, j=j, rot=rot: e.tensor_copy(
                        ggatf[:, 8 * rot: 8 * rot + 8],
                        xg[:, 1032 * j + D: 1032 * j + D + 8]),
                        wait=([("po", g0po[ei] if grp == 0 else g1po)]
                              if j == 0 else []))
                    for f in range(NCH):
                        bk = (j * NCH + f) % 4
                        P.op("tensor", lambda e, j=j, f=f, bk=bk, grp=grp:
                             e.transpose(
                                 psbf(bk, 128),
                                 xg[:, 1032 * j + 128 * f:
                                    1032 * j + 128 * (f + 1)],
                                 C_("ident_bf")),
                             wait=(([("po", g0po[ei] if grp == 0 else g1po)]
                                    if j == 0 and f == 0 else [])
                                   + [("ve", bank_ev.get(bk, 0))]),
                             inc=["pe"])
                        P.op("vector", lambda e, j=j, f=f, bk=bk, grp=grp:
                             e.tensor_copy(
                                 xgT[grp][:, 512 * f + 128 * j:
                                          512 * f + 128 * (j + 1)],
                                 psbf(bk, 128)),
                             wait=["pe"], inc=["ve"])
                        bank_ev[bk] = P.val("ve")
                xgT_ev[grp] = P.val("ve")
            for q in range(4):
                if q > 0:
                    w1load(0, ei, q, "gpsimd")
                    w1load(1, ei, q, "gpsimd")
                w2load(0, ei, q)
                w2load(1, ei, q)
                for m in range(NCH):
                    g = m // 2
                    buf = g % 2
                    loc = m % 2
                    mc = 8 * q + m
                    if loc == 1 and g < 2:
                        w2load(g + 2, ei, q)
                    for grp in range(2):
                        bk = (m % 2) * 2 + grp
                        for kk in range(NCH):
                            P.op("tensor", lambda e, kk=kk, buf=buf, loc=loc,
                                 bk=bk, grp=grp: e.matmul(
                                     psn(bk),
                                     w1.bf(buf)[:, 256 * kk + 128 * loc:
                                                256 * kk + 128 * (loc + 1)],
                                     xgT[grp][:, 512 * kk: 512 * (kk + 1)],
                                     start=(kk == 0), stop=(kk == NCH - 1)),
                                 wait=((w1.wait_use(buf)
                                        + [("se", silu_mark.get(
                                            ("r", ei, mc - 2, grp), 0)),
                                           ("ve", xgT_ev[grp]),
                                           ("ve", bank_ev.get(bk, 0))])
                                       if kk == 0 else []),
                                 inc=(["pe"] if kk == NCH - 1 else []))
                        P.op("scalar", lambda e, m=m, grp=grp, bk=bk:
                             e.activation(
                                 shq[:, 1024 * m + 512 * grp:
                                     1024 * m + 512 * (grp + 1)],
                                 psn(bk), AF.Silu),
                             wait=(["pe"]
                                   + ([("pe", silu_mark.get(
                                       ("rw2pe", ei, q - 1), 0))]
                                      if m == 0 and grp == 0 else [])),
                             inc=["se"])
                        silu_mark[("r", ei, mc, grp)] = P.val("se")
                    if loc == 1:
                        w1.mark_done(buf)
                        if g + 2 < 4:
                            w1load(g + 2, ei, q, "gpsimd")
                slast = P.val("se")
                w2rdy = w2cast[0]
                for f in range(NCH):
                    for grp in range(2):
                        bk = 4 + (f % 2) * 2 + grp
                        for k2 in range(NCH):
                            P.op("tensor", lambda e, k2=k2, f=f, grp=grp,
                                 bk=bk: e.matmul(
                                     psn(bk),
                                     w2q[:, 1024 * k2 + 128 * f:
                                         1024 * k2 + 128 * (f + 1)],
                                     shq[:, 1024 * k2 + 512 * grp:
                                         1024 * k2 + 512 * (grp + 1)],
                                     start=(k2 == 0), stop=(k2 == NCH - 1)),
                                 wait=(([("ve", w2rdy), ("se", slast),
                                         ("ve", ysb_mark.get((f - 2, grp), 0)),
                                         ("ve", bank_ev.get(bk, 0))])
                                       if k2 == 0 else []),
                                 inc=(["pe"] if k2 == NCH - 1 else []))
                        P.op("vector", lambda e, f=f, grp=grp, bk=bk, q=q:
                             (e.tensor_copy(
                                 ysb[:, 1024 * f + 512 * grp:
                                     1024 * f + 512 * (grp + 1)], psn(bk))
                              if q == 0 else
                              e.tensor_add(
                                  ysb[:, 1024 * f + 512 * grp:
                                      1024 * f + 512 * (grp + 1)],
                                  ysb[:, 1024 * f + 512 * grp:
                                      1024 * f + 512 * (grp + 1)],
                                  psn(bk))),
                             wait=(["pe"]
                                   + ([("pe", ytr_pe[ei - 1])]
                                      if q == 0 else [])),
                             inc=["ve"])
                        ysb_mark[(f, grp)] = P.val("ve")
                        bank_ev[bk] = P.val("ve")
                silu_mark[("rw2pe", ei, q)] = P.val("pe")
                stt_mark["w2pe"] = P.val("pe")
            for grp in range(2):
                for j in range(4):
                    c = 8 * e8 + 4 * grp + j
                    rot = 4 * grp + j
                    for f in range(NCH):
                        bk = 4 + (j * NCH + f) % 4
                        P.op("tensor", lambda e, j=j, f=f, grp=grp, bk=bk:
                             e.transpose(
                                 psbf(bk, 128),
                                 ysb[:, 1024 * f + 512 * grp + 128 * j:
                                     1024 * f + 512 * grp + 128 * (j + 1)],
                                 C_("ident_bf")),
                             wait=([("ve", bank_ev.get(bk, 0))]
                                   + ([("ve", P.val("ve"))]
                                      if j == 0 and f == 0 and grp == 0
                                      else [])),
                             inc=["pe"])
                        P.op("vector", lambda e, j=j, f=f, bk=bk, rot=rot,
                             e8=e8: e.tensor_scalar(
                                 yT[:, 1024 * j + 128 * f:
                                    1024 * j + 128 * (f + 1)],
                                 psbf(bk, 128),
                                 ggatf[:, 8 * rot + e8: 8 * rot + e8 + 1],
                                 None, ALU.mult),
                             wait=(["pe"]
                                   + ([("ys", yst_ys.get((1 - grp, j), 0))]
                                      if f == 0 else [])
                                   + ([("po", P.val("po"))]
                                      if f == 0 and j == 0 else [])),
                             inc=["ve"])
                        bank_ev[bk] = P.val("ve")
                    P.op("sync", lambda e, j=j, c=c: e.dma_start(
                        out=ysd[0:NSLOT]
                        .rearrange("(p f) d -> p (f d)", p=128)
                        [:, 1024 * c: 1024 * (c + 1)],
                        in_=yT[:, 1024 * j: 1024 * (j + 1)]),
                        wait=["ve"], inc=[("dma", 16)])
                    yst_ys[(grp, j)] = P.val("ys")
                if grp == 0:
                    ytr_ve_g0[ei] = P.val("ve")
            ytr_pe[ei] = P.val("pe")
            ytr_ve[ei] = P.val("ve")
            if ei + 1 < NEX:
                gathers(ei + 1, 0, [("ve", xgT_ev[1])])
                g0po[ei + 1] = P.val("po")
        ysd_done = P.val("ys")

        P.phase = ph0 + "c"
        gAB = [[arb(C_O + 16384 + 2048 * (2 * b + h), 1024) for h in range(2)]
               for b in range(2)]
        Sb = [arb(C_O + 24576 + 2048 * b, 1024) for b in range(2)]
        ostg = [arf(C_O + i * 4096, 1024) for i in range(2)]
        comb_ve = {}
        comb_ys = {}
        for tc in range(16):
            b = tc % 2
            P.op("gpsimd", lambda e, tc=tc, b=b: e.indirect_dma_start(
                gAB[b][0], None, ysd,
                IndirectOffsetOnAxis(pcolsi[:, tc:tc + 1], 0)),
                wait=(([("ys", ysd_done)] + P.all_waits()) if tc == 0 else
                      [("ve", comb_ve.get(tc - 2, 0))]),
                inc=[("dgp", 16)])
            P.op("gpsimd", lambda e, tc=tc, b=b: e.indirect_dma_start(
                gAB[b][1], None, ysd,
                IndirectOffsetOnAxis(pcolsi[:, 16 + tc: 17 + tc], 0)),
                inc=[("dgp", 16)])
            P.op("vector", lambda e, b=b: e.tensor_add(
                Sb[b], gAB[b][0], gAB[b][1]), wait=["po"], inc=["ve"])
            sve = P.val("ve")
            if last:
                for c in range(NCH):
                    P.op("tensor", lambda e, c=c, tc=tc: e.transpose(
                        psn(c).bitcast(F32)[:, 0:128],
                        X(c)[:, 128 * tc: 128 * (tc + 1)], C_("ident_f")),
                        wait=(([("ve", comb_ve.get(tc - 1, sve))]
                               + ([("ve", sve)] if tc == 0 else []))
                              if c == 0 else []),
                        inc=(["pe"] if c == NCH - 1 else []))
                for c in range(NCH):
                    P.op("vector", lambda e, c=c, b=b: e.tensor_add(
                        ostg[b][:, 128 * c: 128 * (c + 1)],
                        psn(c).bitcast(F32)[:, 0:128],
                        Sb[b][:, 128 * c: 128 * (c + 1)]),
                        wait=(["pe", ("ve", sve),
                               ("ys", comb_ys.get(tc - 2, 0))]
                              if c == 0 else []),
                        inc=(["ve"] if c == NCH - 1 else []))
                P.op("sync", lambda e, b=b, tc=tc: e.dma_start(
                    out=out[128 * tc: 128 * (tc + 1), :], in_=ostg[b]),
                    wait=["ve"], inc=[("dma", 16)])
                comb_ys[tc] = P.val("ys")
            else:
                for c in range(NCH):
                    P.op("tensor", lambda e, c=c, b=b: e.transpose(
                        psbf(c, 128), Sb[b][:, 128 * c: 128 * (c + 1)],
                        C_("ident_bf")),
                        wait=(([("ve", sve),
                                ("ve", comb_ve.get(tc - 1, 0))])
                              if c == 0 else []),
                        inc=(["pe"] if c == NCH - 1 else []))
                for c in range(NCH):
                    P.op("vector", lambda e, c=c, tc=tc:
                         e.scalar_tensor_tensor(
                             X(c)[:, 128 * tc: 128 * (tc + 1)],
                             psbf(c, 128), CNs(CN_ONE),
                             X(c)[:, 128 * tc: 128 * (tc + 1)],
                             ALU.mult, ALU.add),
                         wait=(["pe"] if c == 0 else []),
                         inc=(["ve"] if c == NCH - 1 else []))
            comb_ve[tc] = P.val("ve")
        if last:
            P.op("sync", lambda e: e.nop(), wait=["dma"])

    def mamba(p, src_t):
        mp = mbp[p]
        ug = [arb(C_O + 4096 * c, T) for c in range(NCH)]
        WI_O = 32768
        xs = arf(C_O + 49152, 2051)
        cacc = arf(C_O + 57360, 2048)
        cacc2 = [cacc, arf(C_O + 49152, 2048)]
        gs_b = arb(C_O + 65552, T)
        mpar = arf(A_O + 0, 176)
        sAB = arf(A_O + 1024, SSD, parts=SSD)
        sATB = arf(A_O + 1216, SSD, parts=SSD)
        sBC = arf(A_O + 1344, 4, parts=SSD)
        wtb = arf(A_O + 1408, 132, parts=SSD)
        pw = [arf(A_O + 1952 + i * 128, SSD, parts=SSD) for i in range(8)]
        pwT = [arf(A_O + 2976 + i * 128, SSD, parts=SSD) for i in range(8)]
        rsm4 = [arf(A_O + 4000 + i * 128, SSD, parts=SSD) for i in range(4)]
        rsm4b = [arb(A_O + 4512 + i * 64, SSD, parts=SSD) for i in range(4)]
        ssmw_bf = arb(A_O + 4768, 256)
        ssb = arf(A_O + 5280, 4 * SSD)
        hx = arf(A_O + 5792, 16, parts=SSD)
        hb = arf(A_O + 5856, 16, parts=SSD)
        msbf = arf(A_O + 5920, 128)
        shsb = arf(A_O + 6432, 128)
        wtbf = arb(A_O + 6944, 132, parts=SSD)
        u_b2 = [arb(A_O + 7232, T), arb(A_O + 16384 + 4096, T)]
        usb = arb(A_O + 11328, T, parts=SSD)
        crow = arb(A_O + 15424, T, parts=1)

        ew = P.all_waits()
        loads = [("inb", mpar[:, 0:16]), ("convb", mpar[:, 16:24]),
                 ("convw", mpar[:, 24:56]), ("D", mpar[:, 56:64]),
                 ("outb", mpar[:, 64:72]), ("A", sAB), ("AT", sATB),
                 ("B", sBC[0:SSD, 0:1]), ("C", sBC[0:SSD, 1:2]),
                 ("ssmb", sBC[0:SSD, 2:3])]
        for i, (nm, dst) in enumerate(loads):
            P.op("sync", lambda e, nm=nm, dst=dst: e.dma_start(
                out=dst, in_=mp[nm]), wait=(ew if i == 0 else []),
                inc=[("dma", 16)])
        P.op("sync", lambda e: e.dma_start(
            out=xs[:, 0:256].rearrange("p (a b) -> p a b", a=8),
            in_=mp["ssmw"].rearrange("(k p) c -> p k c", p=128)),
            inc=[("dma", 16)])
        P.op("vector", lambda e: e.tensor_copy(ssmw_bf, xs[:, 0:256]),
             wait=["ys"], inc=["ve"])
        P.op("vector", lambda e: e.tensor_mul(
            sBC[0:SSD, 3:4], sBC[0:SSD, 2:3], sBC[0:SSD, 0:1]), inc=["ve"])
        P.op("scalar", lambda e: e.copy(pw[0], sATB), wait=["ys"])
        P.op("scalar", lambda e: e.copy(pwT[0], sAB))
        P.op("scalar", lambda e: e.copy(wtb[:, 0:1], sBC[0:SSD, 1:2]),
             inc=["se"])
        P.op("tensor", lambda e: e.matmul(
            psn(0, 1, 0, SSD)[:, 0:1], sAB, wtb[:, 0:1], start=True,
            stop=True), wait=["se", "ve", "pe"], inc=["pe"])
        P.op("scalar", lambda e: e.copy(
            wtb[:, 1:2], psn(0, 1, 0, SSD)[:, 0:1]), wait=["pe"], inc=["se"])
        for lvl in range(1, 8):
            cols = 1 << lvl
            P.op("tensor", lambda e, lvl=lvl: e.matmul(
                psn(1, 1, 0, SSD)[:, 0:SSD], pwT[lvl - 1], pw[lvl - 1],
                start=True, stop=True), wait=["se"], inc=["pe"])
            P.op("tensor", lambda e, lvl=lvl: e.matmul(
                psn(2, 1, 0, SSD)[:, 0:SSD], pw[lvl - 1], pwT[lvl - 1],
                start=True, stop=True), inc=["pe"])
            P.op("scalar", lambda e, lvl=lvl: e.copy(
                pw[lvl], psn(1, 1, 0, SSD)[:, 0:SSD]), wait=["pe"])
            P.op("scalar", lambda e, lvl=lvl: e.copy(
                pwT[lvl], psn(2, 1, 0, SSD)[:, 0:SSD]), inc=["se"])
            take = min(cols, 129 - cols)
            P.op("tensor", lambda e, lvl=lvl, cols=cols, take=take: e.matmul(
                psn(0, 1, 0, SSD)[:, 0:take], pwT[lvl], wtb[:, 0:take],
                start=True, stop=True), wait=["se"], inc=["pe"])
            P.op("scalar", lambda e, cols=cols, take=take: e.copy(
                wtb[:, cols:cols + take], psn(0, 1, 0, SSD)[:, 0:take]),
                wait=["pe"], inc=["se"])
        P.op("tensor", lambda e: e.matmul(
            psn(3, 1, 0, SSD)[:, 0:SSD], pwT[1], sATB, start=True, stop=True),
            wait=["se"], inc=["pe"])
        P.op("scalar", lambda e: e.copy(
            rsm4[0], psn(3, 1, 0, SSD)[:, 0:SSD]), wait=["pe"])
        P.op("scalar", lambda e: e.copy(rsm4[1], pw[1]))
        P.op("scalar", lambda e: e.copy(rsm4[2], sATB))
        P.op("scalar", lambda e: e.copy(
            rsm4[3], C_("ident_f")[0:SSD, 0:SSD]), inc=["se"])
        for ip in range(4):
            P.op("vector", lambda e, ip=ip: e.tensor_copy(
                rsm4b[ip], rsm4[ip]), wait=(["se"] if ip == 0 else []))
        P.op("vector", lambda e: e.tensor_copy(wtbf, wtb), inc=["ve"])
        P.op("vector", lambda e: e.memset(xs[:, 0:3], 0.0), inc=["ve"])
        P.op("vector", lambda e: e.memset(hx, 0.0), inc=["ve"])

        wi = WS(WI_O, 2048, WI_O + 8192, 2048)
        wi.load(wsrc(mp["inw"], slice(0, 256)), 0)
        mk = {}
        for c in range(NCH):
            buf = c % 2
            if c + 1 < NCH:
                wi.load(wsrc(mp["inw"], slice(256 * (c + 1), 256 * (c + 2))),
                        (c + 1) % 2)
            for mloc in range(2):
                for tt in range(NTT):
                    for kk in range(NCH):
                        P.op("tensor", lambda e, kk=kk, tt=tt, buf=buf,
                             mloc=mloc: e.matmul(
                                 PSAP[:, 512 * tt: 512 * (tt + 1)],
                                 wi.bf(buf)[:, 256 * kk + 128 * mloc:
                                            256 * kk + 128 * (mloc + 1)],
                                 src_t(kk)[:, 512 * tt: 512 * (tt + 1)],
                                 start=(kk == 0), stop=(kk == NCH - 1)),
                             wait=((wi.wait_use(buf)
                                    + [("se", mk.get(("ev", c, mloc - 1),
                                                     mk.get(("ev", c - 1, 1), 0)))]
                                    + ([("ve", P.val("ve"))] if c == 0 and
                                       mloc == 0 else []))
                                   if kk == 0 and tt == 0 else []),
                             inc=(["pe"] if kk == NCH - 1 and tt == NTT - 1
                                  else []))
                if mloc == 0:
                    P.op("scalar", lambda e, c=c: e.activation(
                        xs[:, 3:2051], PSAP[:, 0:2048], AF.Identity,
                        bias=mpar[:, 2 * c:2 * c + 1]),
                        wait=["pe", ("ve", mk.get(("dw", c - 1), 0))],
                        inc=["se"])
                    mk[("ev", c, 0)] = P.val("se")
                else:
                    P.op("scalar", lambda e, c=c: e.activation(
                        gs_b, PSAP[:, 0:2048], AF.Silu,
                        bias=mpar[:, 2 * c + 1:2 * c + 2]),
                        wait=["pe", ("po", mk.get(("ug", c - 1), 0))],
                        inc=["se"])
                    mk[("ev", c, 1)] = P.val("se")
            wi.mark_done(buf)
            P.op("vector", lambda e, c=c: e.tensor_scalar(
                cacc, xs[:, 0:2048], mpar[:, 24 + 4 * c: 25 + 4 * c], None,
                ALU.mult),
                wait=[("se", mk[("ev", c, 0)]),
                      ("se", mk.get(("u", c - 1), 0))])
            for tau in range(1, 4):
                P.op("vector", lambda e, c=c, tau=tau: e.scalar_tensor_tensor(
                    cacc, xs[:, tau: tau + 2048],
                    mpar[:, 24 + 4 * c + tau: 25 + 4 * c + tau], cacc,
                    ALU.mult, ALU.add),
                    inc=(["ve"] if tau == 3 else []))
            mk[("dw", c)] = P.val("ve")
            ub = u_b2[c % 2]
            P.op("scalar", lambda e, c=c, ub=ub: e.activation(
                ub, cacc, AF.Silu, bias=mpar[:, 16 + c:17 + c]),
                wait=["ve", ("po", mk.get(("ug", c - 1), 0)),
                      ("pe", mk.get(("xs", c - 2), 0))], inc=["se"])
            mk[("u", c)] = P.val("se")

            def xssm_mms(c):
                ub = u_b2[c % 2]
                for tt in range(NTT):
                    P.op("tensor", lambda e, c=c, tt=tt, ub=ub: e.matmul(
                        PSAP[0:SSD, 2048 + 512 * tt: 2048 + 512 * (tt + 1)],
                        ssmw_bf[:, 32 * c: 32 * (c + 1)],
                        ub[:, 512 * tt: 512 * (tt + 1)],
                        start=(c == 0), stop=(c == NCH - 1)),
                        wait=([("se", mk[("u", c)])] if tt == 0 else []),
                        inc=(["pe"] if tt == NTT - 1 else []))
                mk[("xs", c)] = P.val("pe")
            mk[("defer", c)] = xssm_mms
            if c >= 1:
                mk[("defer", c - 1)](c - 1)
            if c == NCH - 1:
                mk[("defer", c)](c)
            P.op("gpsimd", lambda e, c=c, ub=ub: e.tensor_mul(
                ug[c], ub, gs_b), wait=[("se", P.val("se"))], inc=["po"])
            mk[("ug", c)] = P.val("po")
        P.op("vector", lambda e: e.tensor_scalar(
            usb, PSAP[0:SSD, 2048:4096], sBC[0:SSD, 0:1], sBC[0:SSD, 3:4],
            ALU.mult, ALU.add), wait=["pe"], inc=["ve"])
        u_ve = P.val("ve")

        for g in range(4):
            for ip in range(4):
                P.op("tensor", lambda e, g=g, ip=ip: e.matmul(
                    PSAP[:, 3584 + 32 * g: 3584 + 32 * (g + 1)],
                    usb.rearrange("p (n s) -> p s n", s=4)[:, ip, :]
                    [:, 128 * g: 128 * (g + 1)],
                    rsm4b[ip], start=(ip == 0), stop=(ip == 3)),
                    wait=([("ve", u_ve)] if g == 0 and ip == 0 else []),
                    inc=(["pe"] if ip == 3 else []))
        P.op("vector", lambda e: e.tensor_copy(ssb, PSAP[:, 3584:3712]),
             wait=["pe"], inc=["ve"])
        ssb_ve = P.val("ve")

        for b in range(32):
            if b > 0:
                P.op("tensor", lambda e: e.matmul(
                    PSAP[0:SSD, 3072:3088], pw[2], hx, start=True,
                    stop=False),
                    wait=[("se", mk.get(("hx", b - 1), 0))], inc=[])
            for g in range(4):
                P.op("tensor", lambda e, b=b, g=g: e.matmul(
                    PSAP[0:SSD, 3072:3088][:, 4 * g: 4 * (g + 1)],
                    ssb[:, 32 * g: 32 * (g + 1)],
                    C_("sel")[:, 4 * b: 4 * (b + 1)],
                    start=(b == 0 and g == 0), stop=(g == 3)),
                    wait=([("ve", ssb_ve),
                           ("se", mk.get(("hx", b - 1), 0))]
                          if g == 0 else []),
                    inc=(["pe"] if g == 3 else []))
            P.op("scalar", lambda e: e.copy(hx, PSAP[0:SSD, 3072:3088]),
                 wait=["pe"], inc=["se"])
            mk[("hx", b)] = P.val("se")
        P.op("scalar", lambda e: e.copy(hb[:, 0:1], hx[:, 0:1]), inc=["se"])
        for j in range(1, 16):
            P.op("tensor", lambda e, j=j: e.matmul(
                PSAP[0:SSD, 3136:3137], pw[7], hb[:, j - 1:j], start=True,
                stop=False), wait=["se"], inc=[])
            P.op("tensor", lambda e, j=j: e.matmul(
                PSAP[0:SSD, 3136:3137], C_("ident_f")[0:SSD, 0:SSD],
                hx[:, j:j + 1], start=False, stop=True), inc=["pe"])
            P.op("scalar", lambda e, j=j: e.copy(
                hb[:, j:j + 1], PSAP[0:SSD, 3136:3137]), wait=["pe"],
                inc=["se"])
        hb_se = P.val("se")

        dshift_w = [dscan[33024 * i: 33024 * i + 33024]
                    .rearrange("(a j) -> a j", a=128, j=258)[:, 0:128]
                    for i in range(2)]
        dshift_r = [dscan[33024 * i: 33024 * i + 32896]
                    .rearrange("(a j) -> a j", a=128, j=257)[:, 0:128]
                    for i in range(2)]
        for j in range(16):
            P.op("tensor", lambda e, j=j: e.matmul(
                PSAP[:, 0:128], usb[:, 128 * j: 128 * (j + 1)],
                wtbf[:, 0:128], start=True, stop=True),
                wait=([("ve", u_ve), ("se", hb_se)] if j == 0 else
                      [("se", mk.get(("ms", j - 1), 0))]), inc=["pe"])
            P.op("scalar", lambda e, j=j: e.copy(msbf, PSAP[:, 0:128]),
                 wait=["pe", ("dgp", mk.get(("wr", j - 1), 0))], inc=["se"])
            mk[("ms", j)] = P.val("se")
            P.op("gpsimd", lambda e, j=j: e.dma_start(
                out=dshift_w[j % 2], in_=msbf),
                wait=[("se", mk[("ms", j)]), ("dgp", mk.get(("rd", j - 2), 0))],
                inc=[("dgp", 16)])
            mk[("wr", j)] = P.val("dgp")
            P.op("gpsimd", lambda e, j=j: e.dma_start(
                out=shsb, in_=dshift_r[j % 2]),
                wait=[("dgp", mk[("wr", j)]),
                      ("pe", mk.get(("cl", j - 1), 0))],
                inc=[("dgp", 16)])
            mk[("rd", j)] = P.val("dgp")
            mk[("rdf", j)] = P.val("po")
            P.op("tensor", lambda e, j=j: e.matmul(
                PSAP[0:1, 512 + 128 * j: 512 + 128 * (j + 1)],
                C_("ones_col_f"), shsb, start=True, stop=(j == 0)),
                wait=[("po", mk[("rdf", j)])], inc=(["pe"] if j == 0 else []))
            if j > 0:
                P.op("tensor", lambda e, j=j: e.matmul(
                    PSAP[0:1, 512 + 128 * j: 512 + 128 * (j + 1)],
                    hb[:, j - 1:j], wtb[:, 1:129], start=False, stop=True),
                    inc=["pe"])
            mk[("cl", j)] = P.val("pe")
        P.op("scalar", lambda e: e.copy(crow, PSAP[0:1, 512:2560]),
             wait=["pe"], inc=["se"])
        for tt in range(NTT):
            P.op("tensor", lambda e, tt=tt: e.matmul(
                psn(tt), C_("ones_row_bf"), crow[:, 512 * tt: 512 * (tt + 1)],
                start=True, stop=True), wait=(["se"] if tt == 0 else []),
                inc=(["pe"] if tt == NTT - 1 else []))
        cbc_pe = P.val("pe")
        yv = {}
        for c in range(NCH):
            P.op("scalar", lambda e, c=c: e.activation(
                cacc, PSAP[:, 0:2048], AF.Identity,
                bias=mpar[:, 56 + c:57 + c]),
                wait=[("pe", cbc_pe), ("ve", yv.get(c - 1, 0))], inc=["se"])
            P.op("vector", lambda e, c=c: e.tensor_mul(
                ug[c], cacc, ug[c]), wait=["se"], inc=["ve"])
            yv[c] = P.val("ve")
        wo2 = WS(WI_O, 2048, WI_O + 8192, 2048)
        wo2.load(wsrc(mp["outw"], slice(0, 256)), 0)
        for m in range(NCH):
            gidx, loc = divmod(m, 2)
            buf = gidx % 2
            if loc == 0 and gidx + 1 < 4:
                wo2.load(wsrc(mp["outw"],
                              slice(256 * (gidx + 1), 256 * (gidx + 2))),
                         (gidx + 1) % 2)
            ob = 2048 * (m % 2)
            for tt in range(NTT):
                for kk in range(NCH):
                    P.op("tensor", lambda e, kk=kk, tt=tt, buf=buf, loc=loc,
                         ob=ob: e.matmul(
                             PSAP[:, ob + 512 * tt: ob + 512 * (tt + 1)],
                             wo2.bf(buf)[:, 256 * kk + 128 * loc:
                                         256 * kk + 128 * (loc + 1)],
                             ug[kk][:, 512 * tt: 512 * (tt + 1)],
                             start=(kk == 0), stop=(kk == NCH - 1)),
                         wait=((wo2.wait_use(buf)
                                + [("ve", yv.get(NCH - 1, 0)),
                                   ("se", mk.get(("ow", m - 2), 0))])
                               if kk == 0 and tt == 0 else []),
                         inc=(["pe"] if kk == NCH - 1 and tt == NTT - 1
                              else []))
            if loc == 1:
                wo2.mark_done(buf)
            P.op("scalar", lambda e, m=m, ob=ob: e.activation(
                cacc2[m % 2], PSAP[:, ob: ob + 2048], AF.Identity,
                bias=mpar[:, 64 + m:65 + m]),
                wait=["pe", ("ve", mk.get(("owtt", m - 2), 0))], inc=["se"])
            mk[("ow", m)] = P.val("se")
            P.op("vector", lambda e, m=m: e.tensor_add(
                src_t(m), cacc2[m % 2], src_t(m)), wait=["se"], inc=["ve"])
            mk[("owtt", m)] = P.val("ve")

    def conv():
        oc = [arb(C_O + 4096 * m, T) for m in range(NCH)]
        WSO = 32768
        o5a = arf(C_O + 57360, 2048)
        c1bs = arf(A_O + 0, 4)
        stage3 = arf(C_O + WSO, 3072)
        bf3 = [arb(C_O + WSO + 12288 + i * 6144, 3072) for i in range(2)]
        P.op("sync", lambda e: e.dma_start(out=c1bs, in_=c1b_d),
             wait=P.all_waits(), inc=[("dma", 16)])
        P.op("vector", lambda e: e.tensor_scalar(
            c1bs, c1bs, CNs(CN_BNS), None, ALU.mult), wait=["ys"],
            inc=["ve"])

        cast_rdy = [0, 0]
        free_pe = [0, 0]
        last_cast = [P.val("po")]

        def load3(w3, m, buf, three=True):
            ntau = 3 if three else 1
            for tau in range(ntau):
                P.op("sync", lambda e, w3=w3, m=m, tau=tau, three=three:
                     e.dma_start(
                         out=stage3[:, 1024 * tau: 1024 * (tau + 1)]
                         .rearrange("p (a b) -> p a b", a=8),
                         in_=wsrc(w3[tau] if three else w3,
                                  slice(128 * m, 128 * (m + 1)))),
                     wait=([("po", last_cast[0]), ("pe", free_pe[buf])]
                           if tau == 0 else []),
                     inc=[("dma", 16)])
            P.op("gpsimd", lambda e, buf=buf, ntau=ntau: e.tensor_copy(
                bf3[buf][:, 0:1024 * ntau], stage3[:, 0:1024 * ntau]),
                wait=["ys"], inc=["po"])
            cast_rdy[buf] = P.val("po")
            last_cast[0] = P.val("po")

        def cmms(src_t_, m, buf, ntau, pb, wait_first, evac):
            first = [True]
            for tt in range(NTT):
                for tau in ([1, 0, 2] if ntau == 3 else [0]):
                    for kk in range(NCH):
                        sh_ = tau - 1 if ntau == 3 else 0
                        s0 = 512 * tt + sh_
                        o0, n = 512 * tt, 512
                        if s0 < 0:
                            o0, s0, n = 1, 0, 511
                        elif s0 + 512 > 2048:
                            n = 2048 - s0
                        fin = (tau == (2 if ntau == 3 else 0)
                               and kk == NCH - 1)
                        last = fin and tt == NTT - 1
                        st_ = (tau == (1 if ntau == 3 else 0) and kk == 0)
                        P.op("tensor", lambda e, tau=tau, kk=kk, s0=s0,
                             o0=o0, n=n, buf=buf, pb=pb, f=st_, l=fin:
                             e.matmul(
                                 PSAP[:, 2048 * pb + o0: 2048 * pb + o0 + n],
                                 bf3[buf][:, 1024 * tau + 128 * kk:
                                          1024 * tau + 128 * (kk + 1)],
                                 src_t_(kk)[:, s0: s0 + n],
                                 start=f, stop=l),
                             wait=(wait_first + [("po", cast_rdy[buf])]
                                   if first[0] else []),
                             inc=(["pe"] if last else []))
                        first[0] = False
            free_pe[buf] = P.val("pe")
            evac()

        evm = {}
        load3(c1wT, 0, 0)
        for m in range(4):
            buf = m % 2
            if m + 1 < 4:
                load3(c1wT, m + 1, (m + 1) % 2)
            pb = m % 2

            def ev1(m=m, pb=pb):
                P.op("scalar", lambda e, m=m, pb=pb: e.activation(
                    oc[m], PSAP[:, 2048 * pb: 2048 * (pb + 1)], AF.Relu,
                    bias=c1bs[:, m:m + 1], scale=CNs(CN_BNS)),
                    wait=["pe"], inc=["se"])
                evm[m] = P.val("se")
            cmms(xn_t, m, buf, 3, pb,
                 [("se", evm.get(m - 2, 0))] + (P.all_waits() if m == 0
                                                else []), ev1)
        load3(c4wT, 0, 0, three=False)
        for m in range(4):
            buf = m % 2
            if m + 1 < 4:
                load3(c4wT, m + 1, (m + 1) % 2, three=False)
            pb = m % 2

            def ev4(m=m, pb=pb):
                P.op("scalar", lambda e, m=m, pb=pb: e.activation(
                    oc[4 + m], PSAP[:, 2048 * pb: 2048 * (pb + 1)], AF.Relu),
                    wait=["pe"], inc=["se"])
                evm[4 + m] = P.val("se")
            cmms(xn_t, m, buf, 1, pb, [("se", evm.get(4 + m - 2, 0))], ev4)
        oc_se = P.val("se")
        sttm = {}
        load3(c5wT, 0, 0)
        for m in range(NCH):
            buf = m % 2
            if m + 1 < NCH:
                load3(c5wT, m + 1, (m + 1) % 2)
            pb = m % 2

            def ev5(m=m, pb=pb):
                P.op("scalar", lambda e, m=m, pb=pb: e.activation(
                    o5a, PSAP[:, 2048 * pb: 2048 * (pb + 1)], AF.Relu,
                    scale=CNs(CN_BNS)),
                    wait=["pe", ("ve", sttm.get(m - 1, 0))], inc=["se"])
                P.op("vector", lambda e, m=m: e.scalar_tensor_tensor(
                    X(m), o5a, CNs(CN_ONE), X(m), ALU.mult, ALU.add),
                    wait=["se"])
                P.op("vector", lambda e, m=m: e.scalar_tensor_tensor(
                    X(m), xn_t(m), CNs(CN_ONE), X(m), ALU.mult, ALU.add),
                    inc=["ve"])
                sttm[m] = P.val("ve")
                evm[("o5", m)] = P.val("se")
            cmms(lambda kk: oc[kk], m, buf, 3, pb,
                 [("se", oc_se), ("se", evm.get(("o5", m - 2), 0))], ev5)
        conv._chunk_marks = dict(sttm)

    def pout():
        ostg = [arf(C_O + i * 4096, 1024) for i in range(2)]
        vml = {}
        for tc in range(16):
            b = tc % 2
            for c in range(NCH):
                P.op("tensor", lambda e, c=c, tc=tc: e.transpose(
                    psn(c).bitcast(F32)[:, 0:128],
                    X(c)[:, 128 * tc: 128 * (tc + 1)], C_("ident_f")),
                    wait=(P.all_waits() if tc == 0 and c == 0 else
                          ([("ys", P.val("ys") - 1),
                            ("ve", vml.get(tc - 1, 0))] if c == 0 else [])),
                    inc=(["pe"] if c == NCH - 1 else []))
            for c in range(NCH):
                P.op("vector", lambda e, c=c, b=b: e.tensor_copy(
                    ostg[b][:, 128 * c: 128 * (c + 1)],
                    psn(c).bitcast(F32)[:, 0:128]),
                    wait=(["pe"] if c == 0 else []),
                    inc=(["ve"] if c == NCH - 1 else []))
            vml[tc] = P.val("ve")
            P.op("sync", lambda e, b=b, tc=tc: e.dma_start(
                out=out[128 * tc: 128 * (tc + 1), :], in_=ostg[b]),
                wait=["ve"], inc=[("dma", 16)])
        P.op("sync", lambda e: e.nop(), wait=["dma"])

    def tap(name):
        if name not in tap_outs:
            return
        P.op("sync", lambda e, name=name: e.dma_start(
            out=tap_outs[name][:, 0:AR_BYTES // 4], in_=ARH.ap()),
            wait=P.all_waits(), inc=[("dma", 16)])
        P.op("sync", lambda e, name=name: e.dma_start(
            out=tap_outs[name][:, 32256: 32256 + NCH * T], in_=XB.ap()),
            inc=[("dma", 16)])

    P.phase = "p0"
    p0()
    tap("x0")
    if "attn" in stages:
        P.phase = "ln1"
        ln(X)
        tap("ln1")
        P.phase = "attn"
        attn()
        tap("attn")
    if "ffn1" in stages:
        P.phase = "ln2"
        ln(X)
        P.phase = "ffn1"
        if SPARSE:
            ffn_sparse(False)
        else:
            ffn()
        tap("ffn1")
    if "mamba1" in stages:
        P.phase = "ln3"
        ln(X)
        P.phase = "mamba1"
        mamba("m1", xn_t)
        tap("mamba1")
    if "mamba2" in stages:
        P.phase = "mamba2"
        mamba("m2", xn_t)
        tap("mamba2")
    if "conv" in stages:
        P.phase = "conv"
        conv()
        tap("conv")
    if "ffn2" in stages:
        P.phase = "ln4"
        ln(X)
        P.phase = "ffn2"
        if SPARSE:
            ffn_sparse(True)
        else:
            ffn()
        tap("ffn2")
    if not (SPARSE and "ffn2" in stages):
        P.phase = "pout"
        pout()

    semh = {s: nc.alloc_semaphore(s) for s in Prog.SEMS}

    def run(eng_name, e):
        import contextlib
        cur = [None]
        stack = [None]

        def set_scope(ph):
            if ph == cur[0]:
                return
            if stack[0] is not None:
                stack[0].__exit__(None, None, None)
                stack[0] = None
            cur[0] = ph
            if ph is not None:
                stack[0] = nc.named_scope(ph)
                stack[0].__enter__()

        for opi, (eng, waits, fn, incs, phase) in enumerate(P.ops):
            if eng != eng_name:
                continue
            set_scope(phase)
            seen = set()
            for s, v in waits:
                if (s, v) in seen:
                    continue
                seen.add((s, v))
                e.wait_ge(semh[s], v)
            try:
                ins = fn(e)
            except Exception:
                print("FAILED op", opi, "eng", eng, "phase", phase)
                raise
            for s, n in incs:
                ins.then_inc(semh[s], n)
        set_scope(None)

    with nc.Block() as block:
        @block.sync
        def _(e):
            run("sync", e)

        @block.tensor
        def _(e):
            run("tensor", e)

        @block.vector
        def _(e):
            run("vector", e)

        @block.scalar
        def _(e):
            run("scalar", e)

        @block.gpsimd
        def _(e):
            run("gpsimd", e)

    return nc


def kernel(**inputs):
    g = host_prep(inputs)
    x = np.ascontiguousarray(inputs["L0_attn_a_x"], dtype=np.float32)
    nc = build_program()
    shared = {n: g[n] for n in SHARED_NAMES}
    shared.update({"c_" + n: g[n] for n in CONST_NAMES})
    in_maps = []
    for i in range(8):
        m = dict(shared)
        m["x"] = np.ascontiguousarray(x[i])
        in_maps.append(m)
    res = run_bass_kernel_spmd(nc, in_maps, core_ids=list(range(8)))
    return np.stack([r["out"] for r in res.results])


# version 12
# speedup vs baseline: 1.3751x; 1.0070x over previous
import numpy as np
import concourse.bass as bass
import concourse.mybir as mybir
from concourse.bass_utils import run_bass_kernel_spmd
from concourse.bass_core_rust import IndirectOffsetOnAxis

F32 = mybir.dt.float32
BF16 = mybir.dt.bfloat16
I32 = mybir.dt.int32
AF = mybir.ActivationFunctionType
ALU = mybir.AluOpType
AX = mybir.AxisListType

D, T, H, HD = 1024, 2048, 16, 64
NCH, NTT = 8, 4
NE, HID, SSD, CONVK = 8, 4096, 32, 4
NEX = 9
CAP = 1024
NSLOT = NE * CAP
SPARSE = True

STAGES = ("attn", "ffn1", "mamba1", "mamba2", "conv", "ffn2")

CN_INVD, CN_EPS, CN_BNS, CN_NEG1, CN_HALF, CN_ONE, CN_NK, CN_NINF = range(8)


def _rope_perm():
    perm = []
    for h in range(H):
        perm += [h * HD + 2 * j for j in range(HD // 2)]
        perm += [h * HD + 2 * j + 1 for j in range(HD // 2)]
    return np.array(perm)


def _cos_sin():
    freqs = 1.0 / 1000.0 ** (np.arange(0, HD, 2, dtype=np.float64) / HD)
    ang = np.arange(T, dtype=np.float64)[:, None] * freqs[None, :]
    cos = np.tile(np.cos(ang).T, (4, 1)).astype(np.float32)
    sin = np.tile(np.sin(ang).T, (4, 1)).astype(np.float32)
    sgn = np.ones((128, 1), np.float32)
    sgn[32:64] = -1.0
    sgn[96:128] = -1.0
    return cos, sgn * sin


def _bias_tile(b):
    n = b.shape[0]
    return np.ascontiguousarray(b.reshape(n // 128, 128).T).astype(np.float32)


def host_prep(inp):
    import ml_dtypes
    bf = ml_dtypes.bfloat16
    g = {}
    perm = _rope_perm()
    g["wqT"] = np.ascontiguousarray(inp["L0_attn_a_wq"].T[:, perm])
    g["wkp"] = np.ascontiguousarray(inp["L0_attn_a_wk"][:, perm[:HD]])
    g["wv"] = np.ascontiguousarray(inp["L0_attn_a_wv"])
    g["woT"] = np.ascontiguousarray(inp["L0_attn_a_wo"].T)

    w1 = [np.ascontiguousarray(inp["L0_ffn_shared_w1"].T)]
    w2 = [np.ascontiguousarray(inp["L0_ffn_shared_w2"].T)]
    for e in range(NE):
        w1.append(np.ascontiguousarray(inp["L0_ffn_experts_w1"][e].T))
        w2.append(np.ascontiguousarray(inp["L0_ffn_experts_w2"][e].T))
    g["ffn_w1"] = np.stack(w1)
    g["ffn_w2"] = np.stack(w2)
    gw = inp["L0_ffn_gate_w"].T
    g["gateT"] = np.ascontiguousarray(
        gw.reshape(NCH, 128, NE).transpose(1, 0, 2).reshape(128, NCH * NE))

    order = []
    for c in range(NCH):
        order += list(range(128 * c, 128 * (c + 1)))
        order += list(range(D + 128 * c, D + 128 * (c + 1)))
    for p in ("m1", "m2"):
        pre = f"L0_attn_b_{p}_"
        g[p + "_inw"] = np.ascontiguousarray(inp[pre + "in_w"][:, order])
        g[p + "_inb"] = _bias_tile(inp[pre + "in_b"][order])
        g[p + "_convw"] = np.ascontiguousarray(
            inp[pre + "conv_w"][:, 0, :].reshape(NCH, 128, CONVK)
            .transpose(1, 0, 2).reshape(128, NCH * CONVK))
        g[p + "_convb"] = _bias_tile(inp[pre + "conv_b"])
        g[p + "_ssmw"] = np.ascontiguousarray(inp[pre + "ssm_w"])
        g[p + "_ssmb"] = np.ascontiguousarray(
            inp[pre + "ssm_b"].reshape(SSD, 1)).astype(np.float32)
        g[p + "_A"] = np.ascontiguousarray(inp[pre + "A"])
        g[p + "_AT"] = np.ascontiguousarray(inp[pre + "A"].T)
        g[p + "_B"] = np.ascontiguousarray(inp[pre + "B"].reshape(SSD, 1))
        g[p + "_C"] = np.ascontiguousarray(inp[pre + "C"].reshape(SSD, 1))
        g[p + "_D"] = _bias_tile(inp[pre + "D"])
        g[p + "_outw"] = np.ascontiguousarray(inp[pre + "out_w"])
        g[p + "_outb"] = _bias_tile(inp[pre + "out_b"])

    g["c1wT"] = np.ascontiguousarray(inp["L0_attn_b_c1_w"].transpose(2, 1, 0))
    g["c1b"] = np.ascontiguousarray(
        inp["L0_attn_b_c1_b"].reshape(4, 128).T).astype(np.float32)
    g["c4wT"] = np.ascontiguousarray(inp["L0_attn_b_c4_w"][:, :, 0].T)
    g["c5wT"] = np.ascontiguousarray(inp["L0_attn_b_c5_w"].transpose(2, 1, 0))

    cos, s2 = _cos_sin()
    g["cosq"] = (cos / 8.0).astype(bf)
    g["s2q"] = (s2 / 8.0).astype(bf)
    g["cosk"] = cos.astype(bf)
    g["s2k"] = s2.astype(bf)
    g["ident_f"] = np.eye(128, dtype=np.float32)
    g["ident_bf"] = np.eye(128, dtype=np.float32).astype(bf)
    sel = np.zeros((128, 128), np.float32)
    for b in range(32):
        for js in range(4):
            sel[32 * js + b, 4 * b + js] = 1.0
    g["sel"] = sel
    g["ones_col_f"] = np.ones((128, 1), np.float32)
    g["ones_col_bf"] = np.ones((128, 1), np.float32).astype(bf)
    g["ones_row_bf"] = np.ones((1, 128), np.float32).astype(bf)
    g["ecol8"] = np.tile(8.0 * np.arange(8, dtype=np.float32), (128, 1))
    cvals = [1.0 / D, 1e-5, float(1.0 / np.sqrt(np.float32(1.00001))), -1.0,
             0.5, 1.0, -1.02, -30000.0]
    g["CN"] = np.tile(np.array(cvals, np.float32)[None, :], (128, 1))
    return g


SHARED_NAMES = (
    ["wqT", "wkp", "wv", "woT", "ffn_w1", "ffn_w2", "gateT",
     "c1wT", "c1b", "c4wT", "c5wT"]
    + [f"{p}_{s}" for p in ("m1", "m2") for s in
       ("inw", "inb", "convw", "convb", "ssmw", "ssmb", "A", "AT", "B", "C",
        "D", "outw", "outb")])
CONST_NAMES = ("cosq", "s2q", "cosk", "s2k", "ident_f", "ident_bf", "sel",
               "ones_col_f", "ones_col_bf", "ones_row_bf", "CN", "ecol8")
CONST_BF = {"cosq", "s2q", "cosk", "s2k", "ident_bf", "ones_col_bf",
            "ones_row_bf"}


class Prog:
    SEMS = ("dma", "dgp", "pe", "ve", "se", "po", "ys")

    def __init__(self):
        self.ops = []
        self.c = {s: 0 for s in self.SEMS}
        self.phase = "init"

    ESEM = {"vector": "ve", "scalar": "se", "gpsimd": "po"}

    def op(self, eng, fn, wait=(), inc=(), nofence=False):
        waits = []
        for w in wait:
            if isinstance(w, tuple):
                if w[1] > 0:
                    waits.append(w)
            elif self.c[w] > 0:
                waits.append((w, self.c[w]))
        es = self.ESEM.get(eng)
        es_prior = self.c[es] if es is not None else 0
        incs = []
        fence = None
        isdma = False
        for i in inc:
            s, n = i if isinstance(i, tuple) else (i, 1)
            if s in ("dma", "dgp"):
                isdma = True
                if s == "dma" and eng == "sync":
                    fence = ("dma", "ys")
                if s == "dgp" and eng == "gpsimd":
                    fence = ("dgp", "po")
            self.c[s] += n
            incs.append((s, n))
        if es is not None and not isdma:
            if es_prior > 0:
                waits.append((es, es_prior))
            if not any(s == es for s, _ in incs):
                self.c[es] += 1
                incs.append((es, 1))
        self.ops.append((eng, waits, fn, incs, self.phase))
        if fence is not None and not nofence:
            fs, ms = fence
            selfw = [(fs, self.c[fs])]
            if ms == "po" and self.c["po"] > 0:
                selfw.append(("po", self.c["po"]))
            self.c[ms] += 1
            self.ops.append((eng, selfw, lambda e: e.nop(), [(ms, 1)],
                             self.phase))

    def val(self, sem):
        return self.c[sem]

    def all_waits(self):
        return [(s, self.c[s]) for s in self.SEMS]


def _ncols(ap):
    if len(ap.shape) == 3:
        return ap.shape[1] * ap.shape[2]
    return ap.shape[1]


def build_program(stages=STAGES, taps=()):
    nc = bass.Bass("TRN2", target_bir_lowering=False)

    def din(name, shape, dtype=F32):
        return nc.dram_tensor(name, list(shape), dtype, kind="ExternalInput").ap()

    x_in = din("x", (T, D))
    out = nc.dram_tensor("out", [T, D], F32, kind="ExternalOutput").ap()
    tap_outs = {tp: nc.dram_tensor("tap_" + tp, [128, 32256 + NCH * T], F32,
                                   kind="ExternalOutput").ap() for tp in taps}

    wqT = din("wqT", (D, D))
    wkp = din("wkp", (D, HD))
    wv_d = din("wv", (D, HD))
    woT = din("woT", (D, D))
    ffn_w1 = din("ffn_w1", (NEX, D, HID))
    ffn_w2 = din("ffn_w2", (NEX, HID, D))
    gateT = din("gateT", (128, NCH * NE))
    mbp = {}
    for p in ("m1", "m2"):
        mbp[p] = {
            "inw": din(p + "_inw", (D, 2 * D)),
            "inb": din(p + "_inb", (128, 16)),
            "convw": din(p + "_convw", (128, NCH * CONVK)),
            "convb": din(p + "_convb", (128, NCH)),
            "ssmw": din(p + "_ssmw", (D, SSD)),
            "ssmb": din(p + "_ssmb", (SSD, 1)),
            "A": din(p + "_A", (SSD, SSD)),
            "AT": din(p + "_AT", (SSD, SSD)),
            "B": din(p + "_B", (SSD, 1)),
            "C": din(p + "_C", (SSD, 1)),
            "D": din(p + "_D", (128, NCH)),
            "outw": din(p + "_outw", (D, D)),
            "outb": din(p + "_outb", (128, NCH)),
        }
    c1wT = din("c1wT", (3, D, 512))
    c1b_d = din("c1b", (128, 4))
    c4wT = din("c4wT", (D, 512))
    c5wT = din("c5wT", (3, D, D))
    cshape = {"cosq": (128, T), "s2q": (128, T), "cosk": (128, T),
              "s2k": (128, T), "ident_f": (128, 128), "ident_bf": (128, 128),
              "sel": (128, 128), "ones_col_f": (128, 1),
              "ones_col_bf": (128, 1), "ones_row_bf": (1, 128),
              "CN": (128, 8), "ecol8": (128, 8)}
    cdram = {n: din("c_" + n, cshape[n], BF16 if n in CONST_BF else F32)
             for n in cshape}

    dscan = nc.dram_tensor("dscan", [2 * 33024], F32, kind="Internal").ap()
    xnd = nc.dram_tensor("xnd", [T + 1, D + 8], BF16, kind="Internal").ap()
    idxd = nc.dram_tensor("idxd", [NSLOT + 1, 1], I32, kind="Internal").ap()
    ysd = nc.dram_tensor("ysd", [NSLOT + 1, D], BF16, kind="Internal").ap()

    XB = nc.alloc_sbuf_tensor("XB", [128, NCH * T], F32)

    def X(c):
        return XB.ap()[:, T * c: T * (c + 1)]

    CONST = {n: nc.alloc_sbuf_tensor("sb_" + n, list(cshape[n]),
                                     BF16 if n in CONST_BF else F32)
             for n in cshape}

    def C_(n):
        return CONST[n].ap()

    def CNs(i):
        return C_("CN")[:, i:i + 1]

    def CN1(i):
        return C_("CN")[0:1, i:i + 1]

    AR_BYTES = 129024
    ARH = nc.alloc_sbuf_tensor("AR", [128, AR_BYTES // 4], F32)

    def arf(off, cols, parts=128, p0=0):
        assert off % 4 == 0 and off + 4 * cols <= AR_BYTES, (off, cols)
        return ARH.ap()[p0:p0 + parts, off // 4: off // 4 + cols]

    def arb(off, cols, parts=128, p0=0):
        assert off % 4 == 0 and off + 2 * cols <= AR_BYTES + 2, (off, cols)
        w = (2 * cols + 3) // 4
        return ARH.ap()[p0:p0 + parts, off // 4: off // 4 + w] \
            .bitcast(BF16)[:, 0:cols]

    PSH = nc.alloc_psum_tensor("PS", [128, 4096], F32)
    PSAP = PSH.ap()

    def psn(bank, nbank=1, row0=0, nrow=128):
        return PSAP[row0:row0 + nrow, 512 * bank: 512 * (bank + nbank)]

    def psbf(bank, cols, row0=0, nrow=128):
        return PSAP[row0:row0 + nrow, 512 * bank: 512 * (bank + 1)] \
            .bitcast(BF16)[:, 0:cols]

    P = Prog()

    A_O, B_O, C_O = 0, 24576, 57344

    def xn_t(c):
        return arb(B_O + 4096 * c, T)

    def rowf(i, cols=T):
        return arf(A_O + 8192 * i, cols, parts=1)

    def p0():
        for n in cdram:
            P.op("sync", lambda e, a=CONST[n].ap(), b=cdram[n]:
                 e.dma_start(out=a, in_=b), inc=[("dma", 16)])
        P.op("vector", lambda e: e.memset(ARH.ap(), 0.0), inc=["ve"])
        mz = P.val("ve")
        for i in range(2):
            P.op("gpsimd", lambda e, i=i: e.dma_start(
                out=dscan[33024 * i: 33024 * (i + 1)]
                .rearrange("(p f) -> p f", p=128, f=258),
                in_=arf(C_O, 258)), wait=(["ve"] if i == 0 else []),
                inc=[("dgp", 16)])
        stg = [arf(C_O + 8192 + i * 4096, 1024) for i in range(2)]
        vml = {}
        for tc in range(16):
            b = tc % 2
            P.op("sync", lambda e, b=b, tc=tc: e.dma_start(
                out=stg[b], in_=x_in[128 * tc: 128 * (tc + 1), :]),
                wait=([("ve", vml.get(tc - 2, mz))]), inc=[("dma", 16)])
            for c in range(NCH):
                P.op("tensor", lambda e, b=b, c=c: e.transpose(
                    psn(c).bitcast(F32)[:, 0:128],
                    stg[b][:, 128 * c: 128 * (c + 1)], C_("ident_f")),
                    wait=(["ys", ("ve", vml.get(tc - 1, 0))] if c == 0 else []),
                    inc=(["pe"] if c == NCH - 1 else []))
            for c in range(NCH):
                P.op("vector", lambda e, c=c, tc=tc: e.tensor_copy(
                    X(c)[:, 128 * tc: 128 * (tc + 1)],
                    psn(c).bitcast(F32)[:, 0:128]),
                    wait=(["pe"] if c == 0 else []),
                    inc=(["ve"] if c == NCH - 1 else []))
            vml[tc] = P.val("ve")

    class WS:
        def __init__(self, stage_off, stage_cols, bf_off, bf_cols,
                     cast_eng="vector"):
            self.stage = arf(C_O + stage_off, stage_cols)
            step = ((2 * bf_cols + 63) // 64) * 64
            self.bfo = [arb(C_O + bf_off + i * step, bf_cols) for i in range(2)]
            self.cast_eng = cast_eng
            self.cs = {"vector": "ve", "scalar": "se", "gpsimd": "po"}[cast_eng]
            self.ready = [0, 0]
            self.freepe = [P.val("pe"), P.val("pe")]
            self.stage_free = P.val(self.cs)

        def load(self, src_ap, buf, eng="sync"):
            n = _ncols(src_ap)
            three = len(src_ap.shape) == 3
            a = src_ap.shape[1] if three else 1
            dsem = "dma" if eng == "sync" else "dgp"
            P.op(eng, lambda e, s=src_ap, n=n, a=a, th=three: e.dma_start(
                out=(self.stage[:, 0:n].rearrange("p (a b) -> p a b", a=a)
                     if th else self.stage[:, 0:n]), in_=s),
                wait=[(self.cs, self.stage_free), ("pe", self.freepe[buf])],
                inc=[(dsem, 16)], nofence=(eng != "sync"))
            qw = ["ys"] if eng == "sync" else [("dgp", P.val("dgp"))]
            P.op(self.cast_eng, lambda e, b=buf, n=n: e.tensor_copy(
                self.bfo[b][:, 0:n], self.stage[:, 0:n]),
                wait=qw, inc=[self.cs])
            self.ready[buf] = P.val(self.cs)
            self.stage_free = P.val(self.cs)

        def bf(self, buf):
            return self.bfo[buf]

        def wait_use(self, buf):
            return [(self.cs, self.ready[buf])]

        def mark_done(self, buf):
            self.freepe[buf] = P.val("pe")

    def wsrc(dram2d, colsl):
        return dram2d[:, colsl].rearrange("(k p) c -> p k c", p=128)

    def ln(src, ready=None):
        sq = [arf(C_O + i * 8192, T) for i in range(2)]
        entry = P.all_waits()
        pe_marks = {}
        for c in range(NCH):
            w = ([("pe", pe_marks.get(c - 2, 0))] if c >= 2 else
                 ([] if ready else list(entry)))
            if ready:
                w = w + list(ready(c))
            P.op("scalar", lambda e, c=c: e.activation(
                sq[c % 2], src(c), AF.Square), wait=w, inc=["se"])
            for tt in range(NTT):
                P.op("tensor", lambda e, c=c, tt=tt: e.matmul(
                    psn(tt, 1, 0, 1), C_("ones_col_f"),
                    src(c)[:, 512 * tt: 512 * (tt + 1)],
                    start=(c == 0), stop=(c == NCH - 1)),
                    wait=((P.all_waits() if not ready else list(ready(c)) +
                           [("pe", P.val("pe"))])
                          if (c == 0 and tt == 0) else
                          (list(ready(c)) if ready and tt == 0 else [])))
            for tt in range(NTT):
                P.op("tensor", lambda e, c=c, tt=tt: e.matmul(
                    psn(4 + tt, 1, 0, 1), C_("ones_col_f"),
                    sq[c % 2][:, 512 * tt: 512 * (tt + 1)],
                    start=(c == 0), stop=(c == NCH - 1)),
                    wait=(["se"] if tt == 0 else []),
                    inc=(["pe"] if tt == NTT - 1 else []))
            pe_marks[c] = P.val("pe")
        r_m, r_s2, r_r = rowf(0), rowf(1), rowf(2)
        P.op("scalar", lambda e: e.activation(
            r_m, PSAP[0:1, 0:2048], AF.Copy, scale=CN1(CN_INVD)), wait=["pe"])
        P.op("scalar", lambda e: e.activation(
            r_s2, PSAP[0:1, 2048:4096], AF.Copy, scale=CN1(CN_INVD)),
            inc=["se"])
        P.op("vector", lambda e: e.tensor_mul(r_r, r_m, r_m), wait=["se"])
        P.op("vector", lambda e: e.tensor_sub(r_s2, r_s2, r_r), inc=["ve"])
        P.op("scalar", lambda e: e.activation(
            r_s2, r_s2, AF.Sqrt, bias=CN1(CN_EPS)), wait=["ve"], inc=["se"])
        P.op("vector", lambda e: e.reciprocal(r_r, r_s2), wait=["se"])
        P.op("vector", lambda e: e.tensor_mul(r_m, r_m, r_r), inc=["ve"])
        rb_r = arb(A_O + 8192, T, parts=1)
        rb_m = arb(A_O + 8192 + 4096, T, parts=1)
        P.op("scalar", lambda e: e.copy(rb_r, r_r), wait=["ve"])
        P.op("scalar", lambda e: e.copy(rb_m, r_m), inc=["se"])
        for tt in range(NTT):
            P.op("tensor", lambda e, tt=tt: e.matmul(
                psn(tt), C_("ones_row_bf"), rb_r[:, 512 * tt: 512 * (tt + 1)],
                start=True, stop=True), wait=(["se"] if tt == 0 else []))
            P.op("tensor", lambda e, tt=tt: e.matmul(
                psn(4 + tt), C_("ones_row_bf"), rb_m[:, 512 * tt: 512 * (tt + 1)],
                start=True, stop=True),
                inc=(["pe"] if tt == NTT - 1 else []))
        tmp = arf(C_O, T)
        for c in range(NCH):
            P.op("vector", lambda e, c=c: e.tensor_mul(
                tmp, src(c), PSAP[:, 0:2048]), wait=(["pe"] if c == 0 else []))
            P.op("vector", lambda e, c=c: e.tensor_sub(
                xn_t(c), tmp, PSAP[:, 2048:4096]),
                inc=(["ve"] if c == NCH - 1 else []))

    def attn():
        kaug = arb(C_O + 0, T, parts=65)
        va = arb(C_O + 4224, 16 * 66)
        scr = [arb(C_O + 6400 + i * 2048, 1024) for i in range(3)]
        WSO = 12544
        qs = [arb(C_O + 20736 + i * 4160, T, parts=65) for i in range(4)]
        ao = [arb(C_O + 37376 + i * 4096, T) for i in range(8)]
        av_sb = arb(C_O + 70144, 512, parts=64)
        ptil = [arb(A_O + 8192 + i * 2048, 1024) for i in range(2)]
        sqscr = arb(A_O + 12288, T, parts=64)
        negk = arf(A_O + 0, 1, parts=1)

        def qst(pp, hh):
            return qs[2 * (pp % 2) + hh]

        wkv = WS(WSO, 1024, WSO + 4096, 1024)
        wkv.load(wsrc(wkp, slice(0, HD)), 0)
        wkv.load(wsrc(wv_d, slice(0, HD)), 1)
        for tt in range(NTT):
            for kk in range(NCH):
                P.op("tensor", lambda e, kk=kk, tt=tt: e.matmul(
                    psn(tt, 1, 0, 64), wkv.bf(0)[:, 64 * kk: 64 * (kk + 1)],
                    xn_t(kk)[:, 512 * tt: 512 * (tt + 1)],
                    start=(kk == 0), stop=(kk == NCH - 1)),
                    wait=(wkv.wait_use(0) + P.all_waits()
                          if kk == 0 and tt == 0 else []),
                    inc=(["pe"] if kk == NCH - 1 and tt == NTT - 1 else []))
        for hf in range(2):
            sl = slice(1024 * hf, 1024 * (hf + 1))
            P.op("vector", lambda e, sl=sl: e.tensor_mul(
                scr[0][0:64, :], PSAP[0:64, 0:2048][:, sl], C_("cosk")[0:64, sl]),
                wait=(["pe"] if hf == 0 else []))
            P.op("vector", lambda e, sl=sl: e.tensor_mul(
                scr[1][0:64, :], PSAP[0:64, 0:2048][:, sl], C_("s2k")[0:64, sl]),
                inc=["ve"])
            P.op("gpsimd", lambda e: e.dma_start(
                out=scr[2][0:32, :], in_=scr[1][32:64, :]),
                wait=["ve"], inc=[("dgp", 16)])
            P.op("gpsimd", lambda e: e.dma_start(
                out=scr[2][32:64, :], in_=scr[1][0:32, :]), inc=[("dgp", 16)])
            P.op("vector", lambda e, sl=sl: e.tensor_add(
                kaug[0:64, sl], scr[0][0:64, :], scr[2][0:64, :]),
                wait=["po"], inc=["ve"])
        P.op("vector", lambda e: e.memset(rowf(0), 1.0), inc=["ve"])
        P.op("scalar", lambda e: e.copy(
            arb(A_O, T, parts=1), rowf(0)), wait=["ve"], inc=["se"])
        P.op("gpsimd", lambda e: e.dma_start(
            out=kaug[64:65, :], in_=arb(A_O, T, parts=1)), wait=["se"],
            inc=[("dgp", 16)])
        for tt in range(NTT):
            for kk in range(NCH):
                P.op("tensor", lambda e, kk=kk, tt=tt: e.matmul(
                    psn(4 + tt, 1, 0, 64), wkv.bf(1)[:, 64 * kk: 64 * (kk + 1)],
                    xn_t(kk)[:, 512 * tt: 512 * (tt + 1)],
                    start=(kk == 0), stop=(kk == NCH - 1)),
                    wait=(wkv.wait_use(1) if kk == 0 and tt == 0 else []),
                    inc=(["pe"] if kk == NCH - 1 and tt == NTT - 1 else []))
        wkv.mark_done(0)
        wkv.mark_done(1)
        P.op("scalar", lambda e: e.copy(sqscr, PSAP[0:64, 2048:4096]),
             wait=["pe"], inc=["se"])
        for kt in range(16):
            P.op("tensor", lambda e, kt=kt: e.transpose(
                psbf(0, 64), sqscr[:, 128 * kt: 128 * (kt + 1)],
                C_("ident_bf")[0:64, 0:64]),
                wait=(["se", "ve"] if kt == 0 else ["ve"]), inc=["pe"])
            P.op("vector", lambda e, kt=kt: e.tensor_copy(
                va[:, 66 * kt: 66 * kt + 64], psbf(0, 64)), wait=["pe"])
            P.op("vector", lambda e, kt=kt: e.memset(
                va[:, 66 * kt + 64: 66 * kt + 65], 1.0), inc=["ve"])
        P.op("scalar", lambda e: e.activation(
            sqscr, kaug[0:64, :], AF.Square), wait=["ve"], inc=["se"])
        for tt in range(NTT):
            P.op("tensor", lambda e, tt=tt: e.matmul(
                psn(tt, 1, 0, 1), C_("ones_col_bf")[0:64, :],
                sqscr[:, 512 * tt: 512 * (tt + 1)], start=True, stop=True),
                wait=(["se"] if tt == 0 else []),
                inc=(["pe"] if tt == NTT - 1 else []))
        P.op("scalar", lambda e: e.activation(
            rowf(0), PSAP[0:1, 0:2048], AF.Sqrt),
            wait=["pe", ("po", P.val("po"))], inc=["se"])
        P.op("vector", lambda e: e.reduce_max(
            rowf(2, 1), rowf(0), axis=AX.X), wait=["se"])
        P.op("vector", lambda e: e.tensor_scalar(
            negk, rowf(2, 1), CN1(CN_NK), None, ALU.mult), inc=["ve"])

        wq = WS(WSO, 1024, WSO + 4096, 1024)
        exp_mark = {}
        avsb_mark = 0
        bcfree_mark = 0
        pair_done_pe = {}
        rope_done_ve = {}
        qrdy = {}

        def qproj(pp):
            buf = pp % 2
            wq.load(wsrc(wqT, slice(128 * pp, 128 * (pp + 1))), buf)
            for tt in range(NTT):
                for kk in range(NCH):
                    P.op("tensor", lambda e, kk=kk, tt=tt, buf=buf: e.matmul(
                        psn(tt), wq.bf(buf)[:, 128 * kk: 128 * (kk + 1)],
                        xn_t(kk)[:, 512 * tt: 512 * (tt + 1)],
                        start=(kk == 0), stop=(kk == NCH - 1)),
                        wait=(wq.wait_use(buf)
                              + [("ve", rope_done_ve.get(pp - 1, 0))]
                              if kk == 0 and tt == 0 else []),
                        inc=(["pe"] if kk == NCH - 1 and tt == NTT - 1 else []))
            wq.mark_done(buf)
            return P.val("pe")

        def rope_q(pp, qpe):
            for hf in range(2):
                sl = slice(1024 * hf, 1024 * (hf + 1))
                P.op("vector", lambda e, sl=sl: e.tensor_mul(
                    scr[0], PSAP[:, 0:2048][:, sl], C_("cosq")[:, sl]),
                    wait=([("pe", qpe)] if hf == 0 else [])
                    + [("po", P.val("po"))])
                P.op("vector", lambda e, sl=sl: e.tensor_mul(
                    scr[1], PSAP[:, 0:2048][:, sl], C_("s2q")[:, sl]),
                    inc=["ve"])
                for blk in range(4):
                    so, do = [32, 0, 96, 64][blk], [0, 32, 64, 96][blk]
                    P.op("gpsimd", lambda e, so=so, do=do: e.dma_start(
                        out=scr[2][do:do + 32, :], in_=scr[1][so:so + 32, :]),
                        wait=(["ve"] if blk == 0 else []), inc=[("dgp", 16)])
                P.op("vector", lambda e: e.tensor_add(
                    scr[1], scr[0], scr[2]), wait=["po"], inc=["ve"])
                for hh in range(2):
                    P.op("gpsimd", lambda e, hh=hh, sl=sl, pp=pp: e.dma_start(
                        out=qst(pp, hh)[0:64, sl],
                        in_=scr[1][64 * hh: 64 * hh + 64, :]),
                        wait=["ve", ("pe", pair_done_pe.get(pp - 2, 0))],
                        inc=[("dgp", 16)])
            rope_done_ve[pp] = P.val("ve")

        def bounds(pp):
            for hh in range(2):
                P.op("scalar", lambda e, pp=pp, hh=hh: e.activation(
                    sqscr, qst(pp, hh)[0:64, :], AF.Square),
                    wait=["po"], inc=["se"])
                for tt in range(NTT):
                    P.op("tensor", lambda e, tt=tt: e.matmul(
                        psn(tt, 1, 0, 1), C_("ones_col_bf")[0:64, :],
                        sqscr[:, 512 * tt: 512 * (tt + 1)], start=True,
                        stop=True), wait=(["se"] if tt == 0 else []),
                        inc=(["pe"] if tt == NTT - 1 else []))
                P.op("scalar", lambda e: e.activation(
                    rowf(2), PSAP[0:1, 0:2048], AF.Sqrt),
                    wait=["pe", ("ve", P.val("ve"))], inc=["se"])
                P.op("vector", lambda e, pp=pp, hh=hh: e.tensor_scalar(
                    qst(pp, hh)[64:65, :], rowf(2), negk, CN1(CN_HALF),
                    ALU.mult, ALU.subtract), wait=["se"], inc=["ve"])
            qrdy[pp] = P.val("ve")

        def st_av(pp):
            nonlocal avsb_mark, bcfree_mark
            for hh in range(2):
                for qt in range(NTT):
                    nkt = 4 * (qt + 1)
                    for kt2 in range(nkt // 2):
                        for half in range(2):
                            kt = 2 * kt2 + half
                            P.op("tensor", lambda e, kt=kt, qt=qt, pp=pp,
                                 hh=hh, half=half: e.matmul(
                                     psn(4 + half),
                                     kaug[:, 128 * kt: 128 * (kt + 1)],
                                     qst(pp, hh)[:, 512 * qt: 512 * (qt + 1)],
                                     start=True, stop=True),
                                 wait=[("ve", qrdy[pp]),
                                       ("se", exp_mark.get("e", 0))],
                                 inc=["pe"])
                        st_pe = P.val("pe")
                        pt = ptil[kt2 % 2]
                        P.op("scalar", lambda e, pt=pt: e.activation(
                            pt, PSAP[:, 2048:3072], AF.Exp),
                            wait=[("pe", st_pe),
                                  ("po", exp_mark.get(("a", kt2 % 2), 0)),
                                  ("pe", exp_mark.get(("av", kt2 % 2), 0))],
                            inc=["se"])
                        exp_mark["e"] = P.val("se")
                        for half in range(2):
                            kt = 2 * kt2 + half
                            diag = kt >= 4 * qt
                            if diag:
                                P.op("gpsimd", lambda e, kt=kt, qt=qt, pt=pt,
                                     half=half: e.affine_select(
                                         pt[:, 512 * half: 512 * (half + 1)],
                                         pt[:, 512 * half: 512 * (half + 1)],
                                         pattern=[[1, 512]],
                                         compare_op=ALU.is_ge, fill=0.0,
                                         base=512 * qt - 128 * kt,
                                         channel_multiplier=-1),
                                     wait=["se"], inc=["po"])
                        exp_mark[("a", kt2 % 2)] = P.val("po")
                        for half in range(2):
                            kt = 2 * kt2 + half
                            diag = kt >= 4 * qt
                            P.op("tensor", lambda e, kt=kt, pt=pt, half=half,
                                 nkt=nkt: e.matmul(
                                     psn(6, 1, 0, 65),
                                     va[:, 66 * kt: 66 * kt + 65],
                                     pt[:, 512 * half: 512 * (half + 1)],
                                     start=(kt == 0), stop=(kt == nkt - 1)),
                                 wait=([("po", P.val("po"))] if diag
                                       else [("se", exp_mark["e"])])
                                 + ([("se", avsb_mark)] if kt == 0 else []),
                                 inc=["pe"])
                        exp_mark[("av", kt2 % 2)] = P.val("pe")
                    av_pe = P.val("pe")
                    P.op("vector", lambda e: e.reciprocal(
                        rowf(2, 512), PSAP[64:65, 3072:3584]),
                        wait=[("pe", av_pe)])
                    P.op("scalar", lambda e: e.copy(
                        arb(A_O + 20480, 512, parts=1), rowf(2, 512)),
                        wait=["ve"], inc=["se"])
                    P.op("tensor", lambda e: e.matmul(
                        psn(7, 1, 0, 64), C_("ones_row_bf")[:, 0:64],
                        arb(A_O + 20480, 512, parts=1), start=True, stop=True),
                        wait=["se", ("ve", bcfree_mark)], inc=["pe"])
                    P.op("scalar", lambda e: e.copy(
                        av_sb, PSAP[0:64, 3072:3584]), wait=["pe"], inc=["se"])
                    avsb_mark = P.val("se")
                    if hh == 0:
                        P.op("vector", lambda e, pp=pp, qt=qt: e.tensor_mul(
                            ao[pp][0:64, 512 * qt: 512 * (qt + 1)], av_sb,
                            PSAP[0:64, 3584:4096]), wait=["se"], inc=["ve"])
                    else:
                        P.op("vector", lambda e: e.tensor_mul(
                            scr[2][0:64, 0:512], av_sb,
                            PSAP[0:64, 3584:4096]),
                            wait=["se", ("po", P.val("po"))], inc=["ve"])
                        P.op("gpsimd", lambda e, pp=pp, qt=qt: e.dma_start(
                            out=ao[pp][64:128, 512 * qt: 512 * (qt + 1)],
                            in_=scr[2][0:64, 0:512]), wait=["ve"],
                            inc=[("dgp", 16)])
                    bcfree_mark = P.val("ve")
            pair_done_pe[pp] = P.val("pe")

        qpe = qproj(0)
        rope_q(0, qpe)
        bounds(0)
        for pp in range(8):
            if pp + 1 < 8:
                qpe = qproj(pp + 1)
                rope_q(pp + 1, qpe)
            st_av(pp)
            if pp + 1 < 8:
                bounds(pp + 1)

        wo = WS(WSO, 1024, WSO + 4096, 1024)
        wo.load(wsrc(woT, slice(0, 128)), 0)
        stt_mark = {}
        for m in range(NCH):
            buf = m % 2
            if m + 1 < NCH:
                wo.load(wsrc(woT, slice(128 * (m + 1), 128 * (m + 2))),
                        (m + 1) % 2)
            pb = (m % 2) * 4
            for tt in range(NTT):
                for kk in range(NCH):
                    P.op("tensor", lambda e, kk=kk, tt=tt, buf=buf, pb=pb:
                         e.matmul(psn(pb + tt),
                                  wo.bf(buf)[:, 128 * kk: 128 * (kk + 1)],
                                  ao[kk][:, 512 * tt: 512 * (tt + 1)],
                                  start=(kk == 0), stop=(kk == NCH - 1)),
                         wait=((wo.wait_use(buf)
                                + [("ve", P.val("ve")),
                                   ("po", P.val("po")),
                                   ("ve", stt_mark.get(m - 2, 0))])
                               if kk == 0 and tt == 0 else []),
                         inc=(["pe"] if kk == NCH - 1 and tt == NTT - 1 else []))
            wo.mark_done(buf)
            P.op("vector", lambda e, m=m, pb=pb: e.scalar_tensor_tensor(
                X(m), PSAP[:, 2048 * (m % 2): 2048 * (m % 2) + 2048],
                CNs(CN_ONE), X(m), ALU.mult, ALU.add),
                wait=["pe"], inc=["ve"])
            stt_mark[m] = P.val("ve")
        attn._chunk_marks = dict(stt_mark)

    def ffn():
        sh = [arb(C_O + 4096 * m, T) for m in range(NCH)]
        W1_O = 32768
        w2q = arb(C_O + 49280, NCH * 1024)
        w2stage = arf(A_O + 16384, T)
        gf8 = arb(A_O, T, parts=8)
        grow = arb(A_O + 4096, T, parts=1)
        gate_bc = arb(A_O + 8192, T)
        gsc = [arf(A_O + 12288 + i * 64, 8) for i in range(4)]
        gcol = [arf(A_O + 12544 + i * 16, 1) for i in range(4)]
        gwb = arb(A_O + 12672, NCH * NE)
        gstage = arf(A_O + 12928, NCH * NE)

        P.op("sync", lambda e: e.dma_start(out=gstage, in_=gateT),
             wait=P.all_waits(), inc=[("dma", 16)])
        P.op("vector", lambda e: e.tensor_copy(gwb, gstage), wait=["ys"],
             inc=["ve"])
        for tc in range(16):
            for kk in range(NCH):
                P.op("tensor", lambda e, tc=tc, kk=kk: e.matmul(
                    PSAP[:, 8 * tc: 8 * tc + 8],
                    xn_t(kk)[:, 128 * tc: 128 * (tc + 1)],
                    gwb[:, 8 * kk: 8 * (kk + 1)],
                    start=(kk == 0), stop=(kk == NCH - 1)),
                    wait=(["ve"] + P.all_waits() if tc == 0 and kk == 0 else []),
                    inc=(["pe"] if kk == NCH - 1 else []))
        tr_mark = 0
        for tc in range(16):
            lg = PSAP[:, 8 * tc: 8 * tc + 8]
            P.op("vector", lambda e, lg=lg: e.reduce_max(
                gcol[0], lg, axis=AX.X), wait=(["pe"] if tc == 0 else []))
            P.op("vector", lambda e, lg=lg: e.tensor_scalar(
                gsc[0], lg, gcol[0], None, ALU.is_equal))
            P.op("vector", lambda e, lg=lg: e.scalar_tensor_tensor(
                gsc[1], gsc[0], CNs(CN_NINF), lg, ALU.mult, ALU.add))
            P.op("vector", lambda e: e.reduce_max(gcol[1], gsc[1], axis=AX.X))
            P.op("vector", lambda e: e.tensor_sub(gcol[2], gcol[0], gcol[1]),
                 inc=["ve"])
            P.op("scalar", lambda e: e.activation(
                gcol[3], gcol[2], AF.Sigmoid), wait=["ve"], inc=["se"])
            P.op("vector", lambda e, lg=lg: e.tensor_scalar(
                gsc[2], lg, gcol[1], None, ALU.is_equal), wait=["se"])
            P.op("vector", lambda e: e.tensor_scalar(
                gsc[3], gsc[2], gcol[3], CNs(CN_NEG1), ALU.mult, ALU.mult))
            P.op("vector", lambda e: e.tensor_scalar(
                gsc[0], gsc[0], gcol[3], None, ALU.mult))
            P.op("vector", lambda e: e.tensor_add(gsc[0], gsc[0], gsc[2]))
            P.op("vector", lambda e: e.tensor_add(gsc[0], gsc[0], gsc[3]))
            P.op("vector", lambda e: e.tensor_copy(
                gsc[1].bitcast(BF16)[:, 0:8], gsc[0]), inc=["ve"])
            P.op("tensor", lambda e: e.transpose(
                psbf(4, 128, 0, 8), gsc[1].bitcast(BF16)[:, 0:8],
                C_("ident_bf")), wait=["ve", ("ve", tr_mark)], inc=["pe"])
            P.op("vector", lambda e, tc=tc: e.tensor_copy(
                gf8[:, 128 * tc: 128 * (tc + 1)], psbf(4, 128, 0, 8)),
                wait=["pe"], inc=["ve"])
            tr_mark = P.val("ve")

        w1 = WS(W1_O, 2048, W1_O + 8192, 2048)
        silu_mark = {}
        stt_mark = {}
        w2cast = [P.val("ve")]
        for ei in range(NEX):
            if ei >= 1:
                P.op("gpsimd", lambda e, ei=ei: e.dma_start(
                    out=grow, in_=gf8[ei - 1:ei, :]),
                    wait=["ve", "pe", "po"], inc=[("dgp", 16)])
                for tt in range(NTT):
                    P.op("tensor", lambda e, tt=tt: e.matmul(
                        psn(4 + tt), C_("ones_row_bf"),
                        grow[:, 512 * tt: 512 * (tt + 1)], start=True,
                        stop=True),
                        wait=(["po", ("se", silu_mark.get("last", 0)),
                               ("ve", stt_mark.get("last", 0))]
                              if tt == 0 else []),
                        inc=(["pe"] if tt == NTT - 1 else []))
                P.op("vector", lambda e: e.tensor_copy(
                    gate_bc, PSAP[:, 2048:4096]), wait=["pe"], inc=["ve"])
            gbc_ve = P.val("ve")
            for q in range(4):
                w2pe = stt_mark.get("w2pe", 0)

                def w1load(g, q=q, ei=ei):
                    w1.load(wsrc(ffn_w1[ei],
                                 slice(1024 * q + 256 * g,
                                       1024 * q + 256 * (g + 1))), g % 2)

                def w2load(g2, q=q, ei=ei):
                    P.op("sync", lambda e, g2=g2, ei=ei, q=q: e.dma_start(
                        out=w2stage.rearrange("p (a b) -> p a b", a=2),
                        in_=ffn_w2[ei][1024 * q + 256 * g2:
                                       1024 * q + 256 * (g2 + 1), :]
                        .rearrange("(k p) c -> p k c", p=128)),
                        wait=[("ve", w2cast[0]), ("pe", w2pe)],
                        inc=[("dma", 16)])
                    P.op("vector", lambda e, g2=g2: e.tensor_copy(
                        w2q[:, 2048 * g2: 2048 * (g2 + 1)], w2stage),
                        wait=["ys"], inc=["ve"])
                    w2cast[0] = P.val("ve")

                w1load(0)
                w1load(1)
                w2load(0)
                w2load(1)
                for m in range(NCH):
                    g = m // 2
                    buf = g % 2
                    loc = m % 2
                    if loc == 1:
                        if g + 1 < 4:
                            w1load(g + 1)
                        if g < 2:
                            w2load(g + 2)
                    pb = (m % 2) * 4
                    for tt in range(NTT):
                        for kk in range(NCH):
                            P.op("tensor", lambda e, kk=kk, tt=tt, buf=buf,
                                 loc=loc, pb=pb: e.matmul(
                                     psn(pb + tt),
                                     w1.bf(buf)[:, 256 * kk + 128 * loc:
                                                256 * kk + 128 * (loc + 1)],
                                     xn_t(kk)[:, 512 * tt: 512 * (tt + 1)],
                                     start=(kk == 0), stop=(kk == NCH - 1)),
                                 wait=((w1.wait_use(buf)
                                        + [("se", silu_mark.get((q, m - 2), 0)),
                                           ("ve", gbc_ve),
                                           ("ve", stt_mark.get("last", 0))])
                                       if kk == 0 and tt == 0 else []),
                                 inc=(["pe"] if kk == NCH - 1 and tt == NTT - 1
                                      else []))
                    if loc == 1:
                        w1.mark_done(buf)
                    P.op("scalar", lambda e, m=m, pb=pb: e.activation(
                        sh[m], PSAP[:, 2048 * (m % 2): 2048 * (m % 2) + 2048],
                        AF.Silu), wait=["pe"], inc=["se"])
                    silu_mark[(q, m)] = P.val("se")
                    if ei >= 1:
                        P.op("vector", lambda e, m=m: e.tensor_mul(
                            sh[m], sh[m], gate_bc), wait=["se"], inc=["ve"])
                silu_mark["last"] = P.val("se")
                pomark = P.val("ve")
                w2rdy = w2cast[0]
                for m2 in range(NCH):
                    pb = (m2 % 2) * 4
                    for tt in range(NTT):
                        for k2 in range(NCH):
                            P.op("tensor", lambda e, m2=m2, k2=k2, tt=tt,
                                 pb=pb: e.matmul(
                                     psn(pb + tt),
                                     w2q[:, 1024 * k2 + 128 * m2:
                                         1024 * k2 + 128 * (m2 + 1)],
                                     sh[k2][:, 512 * tt: 512 * (tt + 1)],
                                     start=(k2 == 0), stop=(k2 == NCH - 1)),
                                 wait=(([("ve", w2rdy), ("ve", pomark),
                                         ("se", silu_mark["last"]),
                                         ("ve", stt_mark.get(m2 - 2, 0))])
                                       if k2 == 0 and tt == 0 else []),
                                 inc=(["pe"] if k2 == NCH - 1 and tt == NTT - 1
                                      else []))
                    P.op("vector", lambda e, m2=m2, pb=pb:
                         e.scalar_tensor_tensor(
                             X(m2),
                             PSAP[:, 2048 * (m2 % 2): 2048 * (m2 % 2) + 2048],
                             CNs(CN_ONE), X(m2), ALU.mult, ALU.add),
                         wait=["pe"], inc=["ve"])
                    stt_mark[m2] = P.val("ve")
                stt_mark["last"] = P.val("ve")
                stt_mark["w2pe"] = P.val("pe")
        ffn._chunk_marks = dict((m2, stt_mark[m2]) for m2 in range(NCH))

    def ffn_sparse(last=False):
        ph0 = P.phase
        entry = P.all_waits()
        xstage = [arb(A_O + 2112 * i, D + 8) for i in range(2)]
        gsc = [arf(A_O + 4352 + i * 64, 8) for i in range(4)]
        gcol = [arf(A_O + 4608 + i * 16, 1) for i in range(4)]
        gwb = arb(A_O + 4672, NCH * NE)
        gstage = arf(A_O + 4864, NCH * NE)
        mstg = arf(C_O + 65728, 128)
        ctok = arf(C_O + 66240, 8)
        tok1 = arf(C_O + 66272, 8)
        tokm = arf(C_O + 66304, 8)
        tok4 = arf(C_O + 66336, 8)
        tok5 = arf(C_O + 66368, 8)
        p2c = arf(C_O + 66400, 1)
        pmc = arf(C_O + 66416, 1)
        itok = arf(C_O + 67264, 8).bitcast(I32)
        imt = arf(C_O + 67296, 8).bitcast(I32)
        pcf = arf(C_O + 66432, 32)
        pcolsi = arf(C_O + 66560, 32).bitcast(I32)
        icols = arf(C_O + 66688, 16).bitcast(I32)
        idxc = arf(C_O + 66752, 64).bitcast(I32)
        ggatf = arf(C_O + 67008, 64)
        zrow = arb(A_O + 8192, D + 8, parts=1, p0=0)
        m8 = arf(A_O + 8192, T, parts=8, p0=32)
        cum8 = arf(A_O + 0, T, parts=8, p0=32)
        sh = [arb(C_O + 4096 * m, T) for m in range(NCH)]
        W1_O = 32832
        w2q = arb(C_O + 49344, NCH * 1024)
        w2stage = arf(A_O + 16384, T)
        xgT = [arb(C_O + 8192 * i, 4096) for i in range(2)]
        xg = arb(C_O + 16384, 4 * (D + 8))
        yT = arb(C_O + 24640, 4096)
        shq = arb(B_O, 8192)
        ysb = arb(B_O + 16384, 8192)

        P.op("vector", lambda e: e.memset(zrow, 0.0), wait=entry, inc=["ve"])
        P.op("sync", lambda e: e.dma_start(
            out=ysd[NSLOT:NSLOT + 1, :], in_=zrow[:, 0:D]),
            wait=["ve"] + entry, inc=[("dma", 16)])
        P.op("sync", lambda e: e.dma_start(
            out=xnd[T:T + 1, :], in_=zrow), inc=[("dma", 16)])
        P.op("sync", lambda e: e.dma_start(out=gstage, in_=gateT),
             inc=[("dma", 16)])
        P.op("vector", lambda e: e.tensor_copy(gwb, gstage), wait=["ys"],
             inc=["ve"])
        xev, xys = {}, {}
        tr_mark = 0
        for tc in range(16):
            pbx = psbf(4 + tc % 2, 1024)
            for c in range(NCH):
                P.op("tensor", lambda e, c=c, tc=tc, pbx=pbx: e.transpose(
                    pbx[:, 128 * c: 128 * (c + 1)],
                    xn_t(c)[:, 128 * tc: 128 * (tc + 1)], C_("ident_bf")),
                    wait=((entry + ["ve"]) if tc == 0 and c == 0 else
                          ([("ve", xev.get(tc - 2, 0))] if c == 0 else [])),
                    inc=(["pe"] if c == NCH - 1 else []))
            P.op("vector", lambda e, tc=tc, pbx=pbx: e.tensor_copy(
                xstage[tc % 2][:, 0:D], pbx),
                wait=["pe", ("ys", xys.get(tc - 2, 0))], inc=["ve"])
            xev[tc] = P.val("ve")
            for kk in range(NCH):
                P.op("tensor", lambda e, tc=tc, kk=kk: e.matmul(
                    PSAP[:, 8 * tc: 8 * tc + 8],
                    xn_t(kk)[:, 128 * tc: 128 * (tc + 1)],
                    gwb[:, 8 * kk: 8 * (kk + 1)],
                    start=(kk == 0), stop=(kk == NCH - 1)),
                    wait=(["ve"] if tc == 0 and kk == 0 else []),
                    inc=(["pe"] if kk == NCH - 1 else []))
            lg = PSAP[:, 8 * tc: 8 * tc + 8]
            P.op("vector", lambda e, lg=lg: e.reduce_max(
                gcol[0], lg, axis=AX.X), wait=["pe"])
            P.op("vector", lambda e, lg=lg: e.tensor_scalar(
                gsc[0], lg, gcol[0], None, ALU.is_equal))
            P.op("vector", lambda e, lg=lg: e.scalar_tensor_tensor(
                gsc[1], gsc[0], CNs(CN_NINF), lg, ALU.mult, ALU.add))
            P.op("vector", lambda e: e.reduce_max(gcol[1], gsc[1], axis=AX.X))
            P.op("vector", lambda e: e.tensor_sub(gcol[2], gcol[0], gcol[1]),
                 inc=["ve"])
            P.op("scalar", lambda e: e.activation(
                gcol[3], gcol[2], AF.Sigmoid), wait=["ve"], inc=["se"])
            P.op("vector", lambda e, lg=lg: e.tensor_scalar(
                gsc[2], lg, gcol[1], None, ALU.is_equal), wait=["se"])
            P.op("vector", lambda e: e.tensor_scalar(
                gsc[3], gsc[2], gcol[3], CNs(CN_NEG1), ALU.mult, ALU.mult))
            P.op("vector", lambda e: e.tensor_scalar(
                gsc[0], gsc[0], gcol[3], None, ALU.mult))
            P.op("vector", lambda e: e.tensor_add(gsc[0], gsc[0], gsc[2]))
            P.op("vector", lambda e: e.tensor_add(gsc[0], gsc[0], gsc[3]))
            P.op("vector", lambda e, tc=tc: e.tensor_copy(
                xstage[tc % 2][:, D:D + 8], gsc[0]))
            P.op("vector", lambda e, tc=tc: e.tensor_scalar(
                mstg[:, 8 * tc: 8 * tc + 8], gsc[0], 0.0, None, ALU.is_gt),
                inc=["ve"])
            P.op("sync", lambda e, tc=tc: e.dma_start(
                out=xnd[128 * tc: 128 * (tc + 1), :], in_=xstage[tc % 2]),
                wait=["ve"], inc=[("dma", 16)])
            xys[tc] = P.val("ys")
            P.op("tensor", lambda e, tc=tc: e.transpose(
                PSAP[0:8, 3072:3200], mstg[:, 8 * tc: 8 * tc + 8],
                C_("ident_f")),
                wait=["ve", ("ve", tr_mark)], inc=["pe"])
            P.op("vector", lambda e, tc=tc: e.tensor_copy(
                m8[:, 128 * tc: 128 * (tc + 1)], PSAP[0:8, 3072:3200]),
                wait=["pe"], inc=["ve"])
            tr_mark = P.val("ve")
        xnd_ys = P.val("ys")

        P.op("vector", lambda e: e.tensor_tensor_scan(
            cum8, m8, m8, 0.0, ALU.add, ALU.bypass),
            wait=[("ys", xnd_ys)], inc=["ve"])
        tr2 = {}
        for tc in range(16):
            bk6 = 3072 + 512 * (tc % 2)
            P.op("tensor", lambda e, tc=tc, bk6=bk6: e.transpose(
                PSAP[:, bk6: bk6 + 8],
                cum8[:, 128 * tc: 128 * (tc + 1)],
                C_("ident_f")[32:40, 32:40]),
                wait=(["ve"] if tc == 0 else [("ve", tr2.get(tc - 2, 0))]),
                inc=["pe"])
            P.op("vector", lambda e, bk6=bk6: e.tensor_copy(
                ctok, PSAP[:, bk6: bk6 + 8]), wait=["pe"])
            P.op("vector", lambda e: e.tensor_scalar(
                tok1, ctok, float(CAP), None, ALU.is_le))
            P.op("vector", lambda e: e.tensor_scalar(
                ctok, ctok, -1.0, None, ALU.add))
            P.op("vector", lambda e: e.tensor_copy(itok, ctok))
            P.op("vector", lambda e: e.tensor_scalar(
                imt, itok, 7, None, ALU.bitwise_and))
            P.op("vector", lambda e: e.tensor_copy(tokm, imt))
            P.op("vector", lambda e: e.tensor_sub(ctok, ctok, tokm))
            P.op("vector", lambda e: e.tensor_scalar(
                ctok, ctok, 8.0, None, ALU.mult))
            P.op("vector", lambda e: e.tensor_add(ctok, ctok, tokm))
            P.op("vector", lambda e: e.tensor_add(ctok, ctok, C_("ecol8")))
            P.op("vector", lambda e: e.tensor_scalar(
                ctok, ctok, -float(NSLOT), None, ALU.add))
            P.op("vector", lambda e: e.tensor_mul(ctok, ctok, tok1))
            P.op("vector", lambda e: e.tensor_scalar(
                ctok, ctok, float(NSLOT), None, ALU.add))
            P.op("vector", lambda e, tc=tc: e.tensor_tensor_scan(
                tok4, mstg[:, 8 * tc: 8 * tc + 8],
                mstg[:, 8 * tc: 8 * tc + 8], 0.0, ALU.add, ALU.bypass))
            P.op("vector", lambda e: e.tensor_scalar(
                tok5, tok4, 1.0, None, ALU.is_equal))
            P.op("vector", lambda e, tc=tc: e.tensor_mul(
                tok5, tok5, mstg[:, 8 * tc: 8 * tc + 8]))
            P.op("vector", lambda e: e.tensor_scalar(
                tok4, tok4, 2.0, None, ALU.is_equal))
            P.op("vector", lambda e, tc=tc: e.tensor_mul(
                tok4, tok4, mstg[:, 8 * tc: 8 * tc + 8]))
            P.op("vector", lambda e: e.reduce_sum(pmc, tok4, axis=AX.X))
            P.op("vector", lambda e: e.tensor_mul(tok5, tok5, ctok))
            P.op("vector", lambda e, tc=tc: e.reduce_sum(
                pcf[:, tc: tc + 1], tok5, axis=AX.X))
            P.op("vector", lambda e: e.tensor_mul(tok4, tok4, ctok))
            P.op("vector", lambda e: e.reduce_sum(p2c, tok4, axis=AX.X))
            P.op("vector", lambda e: e.tensor_scalar(
                p2c, p2c, -float(NSLOT), None, ALU.add))
            P.op("vector", lambda e: e.tensor_mul(p2c, p2c, pmc))
            P.op("vector", lambda e, tc=tc: e.tensor_scalar(
                pcf[:, 16 + tc: 17 + tc], p2c, float(NSLOT), None, ALU.add),
                inc=["ve"])
            tr2[tc] = P.val("ve")
        P.op("vector", lambda e: e.tensor_copy(pcolsi, pcf), inc=["ve"])
        vposm = P.val("ve")
        P.op("gpsimd", lambda e: e.iota(
            icols, pattern=[[128, 16]], base=0, channel_multiplier=1),
            wait=entry, inc=["po"])
        P.op("gpsimd", lambda e: e.memset(idxc, T), inc=["po"])
        P.op("gpsimd", lambda e: e.dma_start(
            out=idxd[0:NSLOT].rearrange("(p f) o -> p (f o)", p=128),
            in_=idxc), wait=["po"], inc=[("dgp", 16)])
        for j in range(32):
            P.op("gpsimd", lambda e, j=j: e.indirect_dma_start(
                idxd, IndirectOffsetOnAxis(pcolsi[:, j:j + 1], 0),
                icols[:, j % 16: j % 16 + 1], None),
                wait=([("ve", vposm), "po"] if j == 0 else []),
                inc=[("dgp", 16)])
        P.op("gpsimd", lambda e: e.dma_start(
            out=idxc, in_=idxd[0:NSLOT].rearrange("(p f) o -> p (f o)",
                                                  p=128)),
            inc=[("dgp", 16)])

        P.phase = ph0 + "s"
        w1 = WS(W1_O, 2048, W1_O + 8192, 2048)
        silu_mark = {}
        stt_mark = {}
        w2cast = [P.val("ve")]

        def w1load(g, ei, q, eng="sync"):
            w1.load(wsrc(ffn_w1[ei], slice(1024 * q + 256 * g,
                                           1024 * q + 256 * (g + 1))),
                    g % 2, eng=eng)

        def w2load(g2, ei, q):
            P.op("sync", lambda e, g2=g2, ei=ei, q=q: e.dma_start(
                out=w2stage.rearrange("p (a b) -> p a b", a=2),
                in_=ffn_w2[ei][1024 * q + 256 * g2:
                               1024 * q + 256 * (g2 + 1), :]
                .rearrange("(k p) c -> p k c", p=128)),
                wait=[("ve", w2cast[0]), ("pe", stt_mark.get("w2pe", 0))],
                inc=[("dma", 16)])
            P.op("vector", lambda e, g2=g2: e.tensor_copy(
                w2q[:, 2048 * g2: 2048 * (g2 + 1)], w2stage),
                wait=["ys"], inc=["ve"])
            w2cast[0] = P.val("ve")

        for q in range(4):
            w1load(0, 0, q)
            w1load(1, 0, q)
            w2load(0, 0, q)
            w2load(1, 0, q)
            for m in range(NCH):
                g = m // 2
                buf = g % 2
                loc = m % 2
                if loc == 1 and g < 2:
                    w2load(g + 2, 0, q)
                pb = (m % 2) * 4
                for tt in range(NTT):
                    for kk in range(NCH):
                        P.op("tensor", lambda e, kk=kk, tt=tt, buf=buf,
                             loc=loc, pb=pb: e.matmul(
                                 psn(pb + tt),
                                 w1.bf(buf)[:, 256 * kk + 128 * loc:
                                            256 * kk + 128 * (loc + 1)],
                                 xn_t(kk)[:, 512 * tt: 512 * (tt + 1)],
                                 start=(kk == 0), stop=(kk == NCH - 1)),
                             wait=((w1.wait_use(buf)
                                    + [("se", silu_mark.get(("s", q, m - 2), 0)),
                                       ("ve", vposm),
                                       ("ve", stt_mark.get("last", 0))])
                                   if kk == 0 and tt == 0 else []),
                             inc=(["pe"] if kk == NCH - 1 and tt == NTT - 1
                                  else []))
                if loc == 1:
                    w1.mark_done(buf)
                    if g + 2 < 4:
                        w1load(g + 2, 0, q)
                P.op("scalar", lambda e, m=m, pb=pb: e.activation(
                    sh[m], PSAP[:, 2048 * (m % 2): 2048 * (m % 2) + 2048],
                    AF.Silu), wait=["pe"], inc=["se"])
                silu_mark[("s", q, m)] = P.val("se")
            silu_mark["last"] = P.val("se")
            w2rdy = w2cast[0]
            for m2 in range(NCH):
                pb = (m2 % 2) * 4
                for tt in range(NTT):
                    for k2 in range(NCH):
                        P.op("tensor", lambda e, m2=m2, k2=k2, tt=tt,
                             pb=pb: e.matmul(
                                 psn(pb + tt),
                                 w2q[:, 1024 * k2 + 128 * m2:
                                     1024 * k2 + 128 * (m2 + 1)],
                                 sh[k2][:, 512 * tt: 512 * (tt + 1)],
                                 start=(k2 == 0), stop=(k2 == NCH - 1)),
                             wait=(([("ve", w2rdy),
                                     ("se", silu_mark["last"]),
                                     ("ve", stt_mark.get(m2 - 2, 0))])
                                   if k2 == 0 and tt == 0 else []),
                             inc=(["pe"] if k2 == NCH - 1 and tt == NTT - 1
                                  else []))
                P.op("vector", lambda e, m2=m2, pb=pb:
                     e.scalar_tensor_tensor(
                         X(m2),
                         PSAP[:, 2048 * (m2 % 2): 2048 * (m2 % 2) + 2048],
                         CNs(CN_ONE), X(m2), ALU.mult, ALU.add),
                     wait=["pe"], inc=["ve"])
                stt_mark[m2] = P.val("ve")
            stt_mark["last"] = P.val("ve")
            stt_mark["w2pe"] = P.val("pe")
        shared_pe = P.val("pe")

        P.phase = ph0 + "r"
        bank_ev = {bk: P.val("ve") for bk in range(8)}
        xgT_ev = {0: 0, 1: 0}
        ysb_mark = {}
        ytr_pe = {0: 0}
        ytr_ve = {0: 0}
        ytr_ve_g0 = {0: 0}
        yst_ys = {}

        def gathers(ei, grp, wv):
            e8 = ei - 1
            for j in range(4):
                c = 8 * e8 + 4 * grp + j
                P.op("gpsimd", lambda e, c=c, j=j: e.indirect_dma_start(
                    xg[:, 1032 * j: 1032 * (j + 1)], None, xnd,
                    IndirectOffsetOnAxis(idxc[:, c:c + 1], 0)),
                    wait=(wv if j == 0 else []), inc=[("dgp", 16)])

        P.op("gpsimd", lambda e: e.memset(xg[:], 0.0),
             wait=[("pe", shared_pe)], inc=["po"])
        gathers(1, 0, [("pe", shared_pe), ("ys", xnd_ys), "po"])
        g0po = {1: P.val("po")}

        for ei in range(1, NEX):
            e8 = ei - 1
            for grp in range(2):
                if grp == 1:
                    w1load(0, ei, 0, "gpsimd")
                    w1load(1, ei, 0, "gpsimd")
                    gathers(ei, 1, [("ve", xgT_ev[0])])
                    g1po = P.val("po")
                for j in range(4):
                    rot = 4 * grp + j
                    P.op("vector", lambda e, j=j, rot=rot: e.tensor_copy(
                        ggatf[:, 8 * rot: 8 * rot + 8],
                        xg[:, 1032 * j + D: 1032 * j + D + 8]),
                        wait=([("po", g0po[ei] if grp == 0 else g1po)]
                              if j == 0 else []))
                    for f in range(NCH):
                        bk = (j * NCH + f) % 4
                        P.op("tensor", lambda e, j=j, f=f, bk=bk, grp=grp:
                             e.transpose(
                                 psbf(bk, 128),
                                 xg[:, 1032 * j + 128 * f:
                                    1032 * j + 128 * (f + 1)],
                                 C_("ident_bf")),
                             wait=(([("po", g0po[ei] if grp == 0 else g1po)]
                                    if j == 0 and f == 0 else [])
                                   + [("ve", bank_ev.get(bk, 0))]),
                             inc=["pe"])
                        P.op("vector", lambda e, j=j, f=f, bk=bk, grp=grp:
                             e.tensor_copy(
                                 xgT[grp][:, 512 * f + 128 * j:
                                          512 * f + 128 * (j + 1)],
                                 psbf(bk, 128)),
                             wait=["pe"], inc=["ve"])
                        bank_ev[bk] = P.val("ve")
                xgT_ev[grp] = P.val("ve")
            for q in range(4):
                if q > 0:
                    w1load(0, ei, q, "gpsimd")
                    w1load(1, ei, q, "gpsimd")
                w2load(0, ei, q)
                w2load(1, ei, q)
                for m in range(NCH):
                    g = m // 2
                    buf = g % 2
                    loc = m % 2
                    mc = 8 * q + m
                    if loc == 1 and g < 2:
                        w2load(g + 2, ei, q)
                    for grp in range(2):
                        bk = (m % 2) * 2 + grp
                        for kk in range(NCH):
                            P.op("tensor", lambda e, kk=kk, buf=buf, loc=loc,
                                 bk=bk, grp=grp: e.matmul(
                                     psn(bk),
                                     w1.bf(buf)[:, 256 * kk + 128 * loc:
                                                256 * kk + 128 * (loc + 1)],
                                     xgT[grp][:, 512 * kk: 512 * (kk + 1)],
                                     start=(kk == 0), stop=(kk == NCH - 1)),
                                 wait=((w1.wait_use(buf)
                                        + [("se", silu_mark.get(
                                            ("r", ei, mc - 2, grp), 0)),
                                           ("ve", xgT_ev[grp]),
                                           ("ve", bank_ev.get(bk, 0))])
                                       if kk == 0 else []),
                                 inc=(["pe"] if kk == NCH - 1 else []))
                        P.op("scalar", lambda e, m=m, grp=grp, bk=bk:
                             e.activation(
                                 shq[:, 1024 * m + 512 * grp:
                                     1024 * m + 512 * (grp + 1)],
                                 psn(bk), AF.Silu),
                             wait=(["pe"]
                                   + ([("pe", silu_mark.get(
                                       ("rw2pe", ei, q - 1), 0))]
                                      if m == 0 and grp == 0 else [])),
                             inc=["se"])
                        silu_mark[("r", ei, mc, grp)] = P.val("se")
                    if loc == 1:
                        w1.mark_done(buf)
                        if g + 2 < 4:
                            w1load(g + 2, ei, q, "gpsimd")
                slast = P.val("se")
                w2rdy = w2cast[0]
                for f in range(NCH):
                    for grp in range(2):
                        bk = 4 + (f % 2) * 2 + grp
                        for k2 in range(NCH):
                            P.op("tensor", lambda e, k2=k2, f=f, grp=grp,
                                 bk=bk: e.matmul(
                                     psn(bk),
                                     w2q[:, 1024 * k2 + 128 * f:
                                         1024 * k2 + 128 * (f + 1)],
                                     shq[:, 1024 * k2 + 512 * grp:
                                         1024 * k2 + 512 * (grp + 1)],
                                     start=(k2 == 0), stop=(k2 == NCH - 1)),
                                 wait=(([("ve", w2rdy), ("se", slast),
                                         ("ve", ysb_mark.get((f - 2, grp), 0)),
                                         ("ve", bank_ev.get(bk, 0))])
                                       if k2 == 0 else []),
                                 inc=(["pe"] if k2 == NCH - 1 else []))
                        P.op("vector", lambda e, f=f, grp=grp, bk=bk, q=q:
                             (e.tensor_copy(
                                 ysb[:, 1024 * f + 512 * grp:
                                     1024 * f + 512 * (grp + 1)], psn(bk))
                              if q == 0 else
                              e.tensor_add(
                                  ysb[:, 1024 * f + 512 * grp:
                                      1024 * f + 512 * (grp + 1)],
                                  ysb[:, 1024 * f + 512 * grp:
                                      1024 * f + 512 * (grp + 1)],
                                  psn(bk))),
                             wait=(["pe"]
                                   + ([("pe", ytr_pe[ei - 1])]
                                      if q == 0 else [])),
                             inc=["ve"])
                        ysb_mark[(f, grp)] = P.val("ve")
                        bank_ev[bk] = P.val("ve")
                silu_mark[("rw2pe", ei, q)] = P.val("pe")
                stt_mark["w2pe"] = P.val("pe")
            for grp in range(2):
                for j in range(4):
                    c = 8 * e8 + 4 * grp + j
                    rot = 4 * grp + j
                    for f in range(NCH):
                        bk = 4 + (j * NCH + f) % 4
                        P.op("tensor", lambda e, j=j, f=f, grp=grp, bk=bk:
                             e.transpose(
                                 psbf(bk, 128),
                                 ysb[:, 1024 * f + 512 * grp + 128 * j:
                                     1024 * f + 512 * grp + 128 * (j + 1)],
                                 C_("ident_bf")),
                             wait=([("ve", bank_ev.get(bk, 0))]
                                   + ([("ve", P.val("ve"))]
                                      if j == 0 and f == 0 and grp == 0
                                      else [])),
                             inc=["pe"])
                        P.op("vector", lambda e, j=j, f=f, bk=bk, rot=rot,
                             e8=e8: e.tensor_scalar(
                                 yT[:, 1024 * j + 128 * f:
                                    1024 * j + 128 * (f + 1)],
                                 psbf(bk, 128),
                                 ggatf[:, 8 * rot + e8: 8 * rot + e8 + 1],
                                 None, ALU.mult),
                             wait=(["pe"]
                                   + ([("ys", yst_ys.get((1 - grp, j), 0))]
                                      if f == 0 else [])
                                   + ([("po", P.val("po"))]
                                      if f == 0 and j == 0 else [])),
                             inc=["ve"])
                        bank_ev[bk] = P.val("ve")
                    P.op("sync", lambda e, j=j, c=c: e.dma_start(
                        out=ysd[0:NSLOT]
                        .rearrange("(p f) d -> p (f d)", p=128)
                        [:, 1024 * c: 1024 * (c + 1)],
                        in_=yT[:, 1024 * j: 1024 * (j + 1)]),
                        wait=["ve"], inc=[("dma", 16)])
                    yst_ys[(grp, j)] = P.val("ys")
                if grp == 0:
                    ytr_ve_g0[ei] = P.val("ve")
            ytr_pe[ei] = P.val("pe")
            ytr_ve[ei] = P.val("ve")
            if ei + 1 < NEX:
                gathers(ei + 1, 0, [("ve", xgT_ev[1])])
                g0po[ei + 1] = P.val("po")
        ysd_done = P.val("ys")

        P.phase = ph0 + "c"
        gAB = [[arb(C_O + 16384 + 2048 * (2 * b + h), 1024) for h in range(2)]
               for b in range(2)]
        Sb = [arb(C_O + 24576 + 2048 * b, 1024) for b in range(2)]
        ostg = [arf(C_O + i * 4096, 1024) for i in range(2)]
        comb_ve = {}
        comb_ys = {}
        for tc in range(16):
            b = tc % 2
            P.op("gpsimd", lambda e, tc=tc, b=b: e.indirect_dma_start(
                gAB[b][0], None, ysd,
                IndirectOffsetOnAxis(pcolsi[:, tc:tc + 1], 0)),
                wait=(([("ys", ysd_done)] + P.all_waits()) if tc == 0 else
                      [("ve", comb_ve.get(tc - 2, 0))]),
                inc=[("dgp", 16)], nofence=True)
            P.op("gpsimd", lambda e, tc=tc, b=b: e.indirect_dma_start(
                gAB[b][1], None, ysd,
                IndirectOffsetOnAxis(pcolsi[:, 16 + tc: 17 + tc], 0)),
                inc=[("dgp", 16)], nofence=True)
            P.op("vector", lambda e, b=b: e.tensor_add(
                Sb[b], gAB[b][0], gAB[b][1]),
                wait=[("dgp", P.val("dgp"))], inc=["ve"])
            sve = P.val("ve")
            if last:
                for c in range(NCH):
                    P.op("tensor", lambda e, c=c, tc=tc: e.transpose(
                        psn(c).bitcast(F32)[:, 0:128],
                        X(c)[:, 128 * tc: 128 * (tc + 1)], C_("ident_f")),
                        wait=(([("ve", comb_ve.get(tc - 1, sve))]
                               + ([("ve", sve)] if tc == 0 else []))
                              if c == 0 else []),
                        inc=(["pe"] if c == NCH - 1 else []))
                for c in range(NCH):
                    P.op("vector", lambda e, c=c, b=b: e.tensor_add(
                        ostg[b][:, 128 * c: 128 * (c + 1)],
                        psn(c).bitcast(F32)[:, 0:128],
                        Sb[b][:, 128 * c: 128 * (c + 1)]),
                        wait=(["pe", ("ve", sve),
                               ("ys", comb_ys.get(tc - 2, 0))]
                              if c == 0 else []),
                        inc=(["ve"] if c == NCH - 1 else []))
                P.op("sync", lambda e, b=b, tc=tc: e.dma_start(
                    out=out[128 * tc: 128 * (tc + 1), :], in_=ostg[b]),
                    wait=["ve"], inc=[("dma", 16)])
                comb_ys[tc] = P.val("ys")
            else:
                for c in range(NCH):
                    P.op("tensor", lambda e, c=c, b=b: e.transpose(
                        psbf(c, 128), Sb[b][:, 128 * c: 128 * (c + 1)],
                        C_("ident_bf")),
                        wait=(([("ve", sve),
                                ("ve", comb_ve.get(tc - 1, 0))])
                              if c == 0 else []),
                        inc=(["pe"] if c == NCH - 1 else []))
                for c in range(NCH):
                    P.op("vector", lambda e, c=c, tc=tc:
                         e.scalar_tensor_tensor(
                             X(c)[:, 128 * tc: 128 * (tc + 1)],
                             psbf(c, 128), CNs(CN_ONE),
                             X(c)[:, 128 * tc: 128 * (tc + 1)],
                             ALU.mult, ALU.add),
                         wait=(["pe"] if c == 0 else []),
                         inc=(["ve"] if c == NCH - 1 else []))
            comb_ve[tc] = P.val("ve")
        if last:
            P.op("sync", lambda e: e.nop(), wait=["dma"])

    def mamba(p, src_t):
        mp = mbp[p]
        ug = [arb(C_O + 4096 * c, T) for c in range(NCH)]
        WI_O = 32768
        xs = arf(C_O + 49152, 2051)
        cacc = arf(C_O + 57360, 2048)
        cacc2 = [cacc, arf(C_O + 49152, 2048)]
        gs_b = arb(C_O + 65552, T)
        mpar = arf(A_O + 0, 176)
        sAB = arf(A_O + 1024, SSD, parts=SSD)
        sATB = arf(A_O + 1216, SSD, parts=SSD)
        sBC = arf(A_O + 1344, 4, parts=SSD)
        wtb = arf(A_O + 1408, 132, parts=SSD)
        pw = [arf(A_O + 1952 + i * 128, SSD, parts=SSD) for i in range(8)]
        pwT = [arf(A_O + 2976 + i * 128, SSD, parts=SSD) for i in range(8)]
        rsm4 = [arf(A_O + 4000 + i * 128, SSD, parts=SSD) for i in range(4)]
        rsm4b = [arb(A_O + 4512 + i * 64, SSD, parts=SSD) for i in range(4)]
        ssmw_bf = arb(A_O + 4768, 256)
        ssb = arf(A_O + 5280, 4 * SSD)
        hx = arf(A_O + 5792, 16, parts=SSD)
        hb = arf(A_O + 5856, 16, parts=SSD)
        msbf = arf(A_O + 5920, 128)
        shsb = arf(A_O + 6432, 128)
        wtbf = arb(A_O + 6944, 132, parts=SSD)
        u_b2 = [arb(A_O + 7232, T), arb(A_O + 16384 + 4096, T)]
        usb = arb(A_O + 11328, T, parts=SSD)
        crow = arb(A_O + 15424, T, parts=1)

        ew = P.all_waits()
        loads = [("inb", mpar[:, 0:16]), ("convb", mpar[:, 16:24]),
                 ("convw", mpar[:, 24:56]), ("D", mpar[:, 56:64]),
                 ("outb", mpar[:, 64:72]), ("A", sAB), ("AT", sATB),
                 ("B", sBC[0:SSD, 0:1]), ("C", sBC[0:SSD, 1:2]),
                 ("ssmb", sBC[0:SSD, 2:3])]
        for i, (nm, dst) in enumerate(loads):
            P.op("sync", lambda e, nm=nm, dst=dst: e.dma_start(
                out=dst, in_=mp[nm]), wait=(ew if i == 0 else []),
                inc=[("dma", 16)])
        P.op("sync", lambda e: e.dma_start(
            out=xs[:, 0:256].rearrange("p (a b) -> p a b", a=8),
            in_=mp["ssmw"].rearrange("(k p) c -> p k c", p=128)),
            inc=[("dma", 16)])
        P.op("vector", lambda e: e.tensor_copy(ssmw_bf, xs[:, 0:256]),
             wait=["ys"], inc=["ve"])
        P.op("vector", lambda e: e.tensor_mul(
            sBC[0:SSD, 3:4], sBC[0:SSD, 2:3], sBC[0:SSD, 0:1]), inc=["ve"])
        P.op("scalar", lambda e: e.copy(pw[0], sATB), wait=["ys"])
        P.op("scalar", lambda e: e.copy(pwT[0], sAB))
        P.op("scalar", lambda e: e.copy(wtb[:, 0:1], sBC[0:SSD, 1:2]),
             inc=["se"])
        P.op("tensor", lambda e: e.matmul(
            psn(0, 1, 0, SSD)[:, 0:1], sAB, wtb[:, 0:1], start=True,
            stop=True), wait=["se", "ve", "pe"], inc=["pe"])
        P.op("scalar", lambda e: e.copy(
            wtb[:, 1:2], psn(0, 1, 0, SSD)[:, 0:1]), wait=["pe"], inc=["se"])
        for lvl in range(1, 8):
            cols = 1 << lvl
            P.op("tensor", lambda e, lvl=lvl: e.matmul(
                psn(1, 1, 0, SSD)[:, 0:SSD], pwT[lvl - 1], pw[lvl - 1],
                start=True, stop=True), wait=["se"], inc=["pe"])
            P.op("tensor", lambda e, lvl=lvl: e.matmul(
                psn(2, 1, 0, SSD)[:, 0:SSD], pw[lvl - 1], pwT[lvl - 1],
                start=True, stop=True), inc=["pe"])
            P.op("scalar", lambda e, lvl=lvl: e.copy(
                pw[lvl], psn(1, 1, 0, SSD)[:, 0:SSD]), wait=["pe"])
            P.op("scalar", lambda e, lvl=lvl: e.copy(
                pwT[lvl], psn(2, 1, 0, SSD)[:, 0:SSD]), inc=["se"])
            take = min(cols, 129 - cols)
            P.op("tensor", lambda e, lvl=lvl, cols=cols, take=take: e.matmul(
                psn(0, 1, 0, SSD)[:, 0:take], pwT[lvl], wtb[:, 0:take],
                start=True, stop=True), wait=["se"], inc=["pe"])
            P.op("scalar", lambda e, cols=cols, take=take: e.copy(
                wtb[:, cols:cols + take], psn(0, 1, 0, SSD)[:, 0:take]),
                wait=["pe"], inc=["se"])
        P.op("tensor", lambda e: e.matmul(
            psn(3, 1, 0, SSD)[:, 0:SSD], pwT[1], sATB, start=True, stop=True),
            wait=["se"], inc=["pe"])
        P.op("scalar", lambda e: e.copy(
            rsm4[0], psn(3, 1, 0, SSD)[:, 0:SSD]), wait=["pe"])
        P.op("scalar", lambda e: e.copy(rsm4[1], pw[1]))
        P.op("scalar", lambda e: e.copy(rsm4[2], sATB))
        P.op("scalar", lambda e: e.copy(
            rsm4[3], C_("ident_f")[0:SSD, 0:SSD]), inc=["se"])
        for ip in range(4):
            P.op("vector", lambda e, ip=ip: e.tensor_copy(
                rsm4b[ip], rsm4[ip]), wait=(["se"] if ip == 0 else []))
        P.op("vector", lambda e: e.tensor_copy(wtbf, wtb), inc=["ve"])
        P.op("vector", lambda e: e.memset(xs[:, 0:3], 0.0), inc=["ve"])
        P.op("vector", lambda e: e.memset(hx, 0.0), inc=["ve"])

        wi = WS(WI_O, 2048, WI_O + 8192, 2048)
        wi.load(wsrc(mp["inw"], slice(0, 256)), 0)
        mk = {}
        for c in range(NCH):
            buf = c % 2
            if c + 1 < NCH:
                wi.load(wsrc(mp["inw"], slice(256 * (c + 1), 256 * (c + 2))),
                        (c + 1) % 2)
            for mloc in range(2):
                for tt in range(NTT):
                    for kk in range(NCH):
                        P.op("tensor", lambda e, kk=kk, tt=tt, buf=buf,
                             mloc=mloc: e.matmul(
                                 PSAP[:, 512 * tt: 512 * (tt + 1)],
                                 wi.bf(buf)[:, 256 * kk + 128 * mloc:
                                            256 * kk + 128 * (mloc + 1)],
                                 src_t(kk)[:, 512 * tt: 512 * (tt + 1)],
                                 start=(kk == 0), stop=(kk == NCH - 1)),
                             wait=((wi.wait_use(buf)
                                    + [("se", mk.get(("ev", c, mloc - 1),
                                                     mk.get(("ev", c - 1, 1), 0)))]
                                    + ([("ve", P.val("ve"))] if c == 0 and
                                       mloc == 0 else []))
                                   if kk == 0 and tt == 0 else []),
                             inc=(["pe"] if kk == NCH - 1 and tt == NTT - 1
                                  else []))
                if mloc == 0:
                    P.op("scalar", lambda e, c=c: e.activation(
                        xs[:, 3:2051], PSAP[:, 0:2048], AF.Identity,
                        bias=mpar[:, 2 * c:2 * c + 1]),
                        wait=["pe", ("ve", mk.get(("dw", c - 1), 0))],
                        inc=["se"])
                    mk[("ev", c, 0)] = P.val("se")
                else:
                    P.op("scalar", lambda e, c=c: e.activation(
                        gs_b, PSAP[:, 0:2048], AF.Silu,
                        bias=mpar[:, 2 * c + 1:2 * c + 2]),
                        wait=["pe", ("po", mk.get(("ug", c - 1), 0))],
                        inc=["se"])
                    mk[("ev", c, 1)] = P.val("se")
            wi.mark_done(buf)
            P.op("vector", lambda e, c=c: e.tensor_scalar(
                cacc, xs[:, 0:2048], mpar[:, 24 + 4 * c: 25 + 4 * c], None,
                ALU.mult),
                wait=[("se", mk[("ev", c, 0)]),
                      ("se", mk.get(("u", c - 1), 0))])
            for tau in range(1, 4):
                P.op("vector", lambda e, c=c, tau=tau: e.scalar_tensor_tensor(
                    cacc, xs[:, tau: tau + 2048],
                    mpar[:, 24 + 4 * c + tau: 25 + 4 * c + tau], cacc,
                    ALU.mult, ALU.add),
                    inc=(["ve"] if tau == 3 else []))
            mk[("dw", c)] = P.val("ve")
            ub = u_b2[c % 2]
            P.op("scalar", lambda e, c=c, ub=ub: e.activation(
                ub, cacc, AF.Silu, bias=mpar[:, 16 + c:17 + c]),
                wait=["ve", ("po", mk.get(("ug", c - 1), 0)),
                      ("pe", mk.get(("xs", c - 2), 0))], inc=["se"])
            mk[("u", c)] = P.val("se")

            def xssm_mms(c):
                ub = u_b2[c % 2]
                for tt in range(NTT):
                    P.op("tensor", lambda e, c=c, tt=tt, ub=ub: e.matmul(
                        PSAP[0:SSD, 2048 + 512 * tt: 2048 + 512 * (tt + 1)],
                        ssmw_bf[:, 32 * c: 32 * (c + 1)],
                        ub[:, 512 * tt: 512 * (tt + 1)],
                        start=(c == 0), stop=(c == NCH - 1)),
                        wait=([("se", mk[("u", c)])] if tt == 0 else []),
                        inc=(["pe"] if tt == NTT - 1 else []))
                mk[("xs", c)] = P.val("pe")
            mk[("defer", c)] = xssm_mms
            if c >= 1:
                mk[("defer", c - 1)](c - 1)
            if c == NCH - 1:
                mk[("defer", c)](c)
            P.op("gpsimd", lambda e, c=c, ub=ub: e.tensor_mul(
                ug[c], ub, gs_b), wait=[("se", P.val("se"))], inc=["po"])
            mk[("ug", c)] = P.val("po")
        P.op("vector", lambda e: e.tensor_scalar(
            usb, PSAP[0:SSD, 2048:4096], sBC[0:SSD, 0:1], sBC[0:SSD, 3:4],
            ALU.mult, ALU.add), wait=["pe"], inc=["ve"])
        u_ve = P.val("ve")

        for g in range(4):
            for ip in range(4):
                P.op("tensor", lambda e, g=g, ip=ip: e.matmul(
                    PSAP[:, 3584 + 32 * g: 3584 + 32 * (g + 1)],
                    usb.rearrange("p (n s) -> p s n", s=4)[:, ip, :]
                    [:, 128 * g: 128 * (g + 1)],
                    rsm4b[ip], start=(ip == 0), stop=(ip == 3)),
                    wait=([("ve", u_ve)] if g == 0 and ip == 0 else []),
                    inc=(["pe"] if ip == 3 else []))
        P.op("vector", lambda e: e.tensor_copy(ssb, PSAP[:, 3584:3712]),
             wait=["pe"], inc=["ve"])
        ssb_ve = P.val("ve")

        for b in range(32):
            if b > 0:
                P.op("tensor", lambda e: e.matmul(
                    PSAP[0:SSD, 3072:3088], pw[2], hx, start=True,
                    stop=False),
                    wait=[("se", mk.get(("hx", b - 1), 0))], inc=[])
            for g in range(4):
                P.op("tensor", lambda e, b=b, g=g: e.matmul(
                    PSAP[0:SSD, 3072:3088][:, 4 * g: 4 * (g + 1)],
                    ssb[:, 32 * g: 32 * (g + 1)],
                    C_("sel")[:, 4 * b: 4 * (b + 1)],
                    start=(b == 0 and g == 0), stop=(g == 3)),
                    wait=([("ve", ssb_ve),
                           ("se", mk.get(("hx", b - 1), 0))]
                          if g == 0 else []),
                    inc=(["pe"] if g == 3 else []))
            P.op("scalar", lambda e: e.copy(hx, PSAP[0:SSD, 3072:3088]),
                 wait=["pe"], inc=["se"])
            mk[("hx", b)] = P.val("se")
        P.op("scalar", lambda e: e.copy(hb[:, 0:1], hx[:, 0:1]), inc=["se"])
        for j in range(1, 16):
            P.op("tensor", lambda e, j=j: e.matmul(
                PSAP[0:SSD, 3136:3137], pw[7], hb[:, j - 1:j], start=True,
                stop=False), wait=["se"], inc=[])
            P.op("tensor", lambda e, j=j: e.matmul(
                PSAP[0:SSD, 3136:3137], C_("ident_f")[0:SSD, 0:SSD],
                hx[:, j:j + 1], start=False, stop=True), inc=["pe"])
            P.op("scalar", lambda e, j=j: e.copy(
                hb[:, j:j + 1], PSAP[0:SSD, 3136:3137]), wait=["pe"],
                inc=["se"])
        hb_se = P.val("se")

        dshift_w = [dscan[33024 * i: 33024 * i + 33024]
                    .rearrange("(a j) -> a j", a=128, j=258)[:, 0:128]
                    for i in range(2)]
        dshift_r = [dscan[33024 * i: 33024 * i + 32896]
                    .rearrange("(a j) -> a j", a=128, j=257)[:, 0:128]
                    for i in range(2)]
        for j in range(16):
            P.op("tensor", lambda e, j=j: e.matmul(
                PSAP[:, 0:128], usb[:, 128 * j: 128 * (j + 1)],
                wtbf[:, 0:128], start=True, stop=True),
                wait=([("ve", u_ve), ("se", hb_se)] if j == 0 else
                      [("se", mk.get(("ms", j - 1), 0))]), inc=["pe"])
            P.op("scalar", lambda e, j=j: e.copy(msbf, PSAP[:, 0:128]),
                 wait=["pe", ("dgp", mk.get(("wr", j - 1), 0))], inc=["se"])
            mk[("ms", j)] = P.val("se")
            P.op("gpsimd", lambda e, j=j: e.dma_start(
                out=dshift_w[j % 2], in_=msbf),
                wait=[("se", mk[("ms", j)]), ("dgp", mk.get(("rd", j - 2), 0))],
                inc=[("dgp", 16)])
            mk[("wr", j)] = P.val("dgp")
            P.op("gpsimd", lambda e, j=j: e.dma_start(
                out=shsb, in_=dshift_r[j % 2]),
                wait=[("dgp", mk[("wr", j)]),
                      ("pe", mk.get(("cl", j - 1), 0))],
                inc=[("dgp", 16)])
            mk[("rd", j)] = P.val("dgp")
            mk[("rdf", j)] = P.val("po")
            P.op("tensor", lambda e, j=j: e.matmul(
                PSAP[0:1, 512 + 128 * j: 512 + 128 * (j + 1)],
                C_("ones_col_f"), shsb, start=True, stop=(j == 0)),
                wait=[("po", mk[("rdf", j)])], inc=(["pe"] if j == 0 else []))
            if j > 0:
                P.op("tensor", lambda e, j=j: e.matmul(
                    PSAP[0:1, 512 + 128 * j: 512 + 128 * (j + 1)],
                    hb[:, j - 1:j], wtb[:, 1:129], start=False, stop=True),
                    inc=["pe"])
            mk[("cl", j)] = P.val("pe")
        P.op("scalar", lambda e: e.copy(crow, PSAP[0:1, 512:2560]),
             wait=["pe"], inc=["se"])
        for tt in range(NTT):
            P.op("tensor", lambda e, tt=tt: e.matmul(
                psn(tt), C_("ones_row_bf"), crow[:, 512 * tt: 512 * (tt + 1)],
                start=True, stop=True), wait=(["se"] if tt == 0 else []),
                inc=(["pe"] if tt == NTT - 1 else []))
        cbc_pe = P.val("pe")
        yv = {}
        for c in range(NCH):
            P.op("scalar", lambda e, c=c: e.activation(
                cacc, PSAP[:, 0:2048], AF.Identity,
                bias=mpar[:, 56 + c:57 + c]),
                wait=[("pe", cbc_pe), ("ve", yv.get(c - 1, 0))], inc=["se"])
            P.op("vector", lambda e, c=c: e.tensor_mul(
                ug[c], cacc, ug[c]), wait=["se"], inc=["ve"])
            yv[c] = P.val("ve")
        wo2 = WS(WI_O, 2048, WI_O + 8192, 2048)
        wo2.load(wsrc(mp["outw"], slice(0, 256)), 0)
        for m in range(NCH):
            gidx, loc = divmod(m, 2)
            buf = gidx % 2
            if loc == 0 and gidx + 1 < 4:
                wo2.load(wsrc(mp["outw"],
                              slice(256 * (gidx + 1), 256 * (gidx + 2))),
                         (gidx + 1) % 2)
            ob = 2048 * (m % 2)
            for tt in range(NTT):
                for kk in range(NCH):
                    P.op("tensor", lambda e, kk=kk, tt=tt, buf=buf, loc=loc,
                         ob=ob: e.matmul(
                             PSAP[:, ob + 512 * tt: ob + 512 * (tt + 1)],
                             wo2.bf(buf)[:, 256 * kk + 128 * loc:
                                         256 * kk + 128 * (loc + 1)],
                             ug[kk][:, 512 * tt: 512 * (tt + 1)],
                             start=(kk == 0), stop=(kk == NCH - 1)),
                         wait=((wo2.wait_use(buf)
                                + [("ve", yv.get(NCH - 1, 0)),
                                   ("se", mk.get(("ow", m - 2), 0))])
                               if kk == 0 and tt == 0 else []),
                         inc=(["pe"] if kk == NCH - 1 and tt == NTT - 1
                              else []))
            if loc == 1:
                wo2.mark_done(buf)
            P.op("scalar", lambda e, m=m, ob=ob: e.activation(
                cacc2[m % 2], PSAP[:, ob: ob + 2048], AF.Identity,
                bias=mpar[:, 64 + m:65 + m]),
                wait=["pe", ("ve", mk.get(("owtt", m - 2), 0))], inc=["se"])
            mk[("ow", m)] = P.val("se")
            P.op("vector", lambda e, m=m: e.tensor_add(
                src_t(m), cacc2[m % 2], src_t(m)), wait=["se"], inc=["ve"])
            mk[("owtt", m)] = P.val("ve")

    def conv():
        oc = [arb(C_O + 4096 * m, T) for m in range(NCH)]
        WSO = 32768
        o5a = arf(C_O + 57360, 2048)
        c1bs = arf(A_O + 0, 4)
        stage3 = arf(C_O + WSO, 3072)
        bf3 = [arb(C_O + WSO + 12288 + i * 6144, 3072) for i in range(2)]
        P.op("sync", lambda e: e.dma_start(out=c1bs, in_=c1b_d),
             wait=P.all_waits(), inc=[("dma", 16)])
        P.op("vector", lambda e: e.tensor_scalar(
            c1bs, c1bs, CNs(CN_BNS), None, ALU.mult), wait=["ys"],
            inc=["ve"])

        cast_rdy = [0, 0]
        free_pe = [0, 0]
        last_cast = [P.val("po")]

        def load3(w3, m, buf, three=True):
            ntau = 3 if three else 1
            for tau in range(ntau):
                P.op("sync", lambda e, w3=w3, m=m, tau=tau, three=three:
                     e.dma_start(
                         out=stage3[:, 1024 * tau: 1024 * (tau + 1)]
                         .rearrange("p (a b) -> p a b", a=8),
                         in_=wsrc(w3[tau] if three else w3,
                                  slice(128 * m, 128 * (m + 1)))),
                     wait=([("po", last_cast[0]), ("pe", free_pe[buf])]
                           if tau == 0 else []),
                     inc=[("dma", 16)])
            P.op("gpsimd", lambda e, buf=buf, ntau=ntau: e.tensor_copy(
                bf3[buf][:, 0:1024 * ntau], stage3[:, 0:1024 * ntau]),
                wait=["ys"], inc=["po"])
            cast_rdy[buf] = P.val("po")
            last_cast[0] = P.val("po")

        def cmms(src_t_, m, buf, ntau, pb, wait_first, evac):
            first = [True]
            for tt in range(NTT):
                for tau in ([1, 0, 2] if ntau == 3 else [0]):
                    for kk in range(NCH):
                        sh_ = tau - 1 if ntau == 3 else 0
                        s0 = 512 * tt + sh_
                        o0, n = 512 * tt, 512
                        if s0 < 0:
                            o0, s0, n = 1, 0, 511
                        elif s0 + 512 > 2048:
                            n = 2048 - s0
                        fin = (tau == (2 if ntau == 3 else 0)
                               and kk == NCH - 1)
                        last = fin and tt == NTT - 1
                        st_ = (tau == (1 if ntau == 3 else 0) and kk == 0)
                        P.op("tensor", lambda e, tau=tau, kk=kk, s0=s0,
                             o0=o0, n=n, buf=buf, pb=pb, f=st_, l=fin:
                             e.matmul(
                                 PSAP[:, 2048 * pb + o0: 2048 * pb + o0 + n],
                                 bf3[buf][:, 1024 * tau + 128 * kk:
                                          1024 * tau + 128 * (kk + 1)],
                                 src_t_(kk)[:, s0: s0 + n],
                                 start=f, stop=l),
                             wait=(wait_first + [("po", cast_rdy[buf])]
                                   if first[0] else []),
                             inc=(["pe"] if last else []))
                        first[0] = False
            free_pe[buf] = P.val("pe")
            evac()

        evm = {}
        load3(c1wT, 0, 0)
        for m in range(4):
            buf = m % 2
            if m + 1 < 4:
                load3(c1wT, m + 1, (m + 1) % 2)
            pb = m % 2

            def ev1(m=m, pb=pb):
                P.op("scalar", lambda e, m=m, pb=pb: e.activation(
                    oc[m], PSAP[:, 2048 * pb: 2048 * (pb + 1)], AF.Relu,
                    bias=c1bs[:, m:m + 1], scale=CNs(CN_BNS)),
                    wait=["pe"], inc=["se"])
                evm[m] = P.val("se")
            cmms(xn_t, m, buf, 3, pb,
                 [("se", evm.get(m - 2, 0))] + (P.all_waits() if m == 0
                                                else []), ev1)
        load3(c4wT, 0, 0, three=False)
        for m in range(4):
            buf = m % 2
            if m + 1 < 4:
                load3(c4wT, m + 1, (m + 1) % 2, three=False)
            pb = m % 2

            def ev4(m=m, pb=pb):
                P.op("scalar", lambda e, m=m, pb=pb: e.activation(
                    oc[4 + m], PSAP[:, 2048 * pb: 2048 * (pb + 1)], AF.Relu),
                    wait=["pe"], inc=["se"])
                evm[4 + m] = P.val("se")
            cmms(xn_t, m, buf, 1, pb, [("se", evm.get(4 + m - 2, 0))], ev4)
        oc_se = P.val("se")
        sttm = {}
        load3(c5wT, 0, 0)
        for m in range(NCH):
            buf = m % 2
            if m + 1 < NCH:
                load3(c5wT, m + 1, (m + 1) % 2)
            pb = m % 2

            def ev5(m=m, pb=pb):
                P.op("scalar", lambda e, m=m, pb=pb: e.activation(
                    o5a, PSAP[:, 2048 * pb: 2048 * (pb + 1)], AF.Relu,
                    scale=CNs(CN_BNS)),
                    wait=["pe", ("ve", sttm.get(m - 1, 0))], inc=["se"])
                P.op("vector", lambda e, m=m: e.scalar_tensor_tensor(
                    X(m), o5a, CNs(CN_ONE), X(m), ALU.mult, ALU.add),
                    wait=["se"])
                P.op("vector", lambda e, m=m: e.scalar_tensor_tensor(
                    X(m), xn_t(m), CNs(CN_ONE), X(m), ALU.mult, ALU.add),
                    inc=["ve"])
                sttm[m] = P.val("ve")
                evm[("o5", m)] = P.val("se")
            cmms(lambda kk: oc[kk], m, buf, 3, pb,
                 [("se", oc_se), ("se", evm.get(("o5", m - 2), 0))], ev5)
        conv._chunk_marks = dict(sttm)

    def pout():
        ostg = [arf(C_O + i * 4096, 1024) for i in range(2)]
        vml = {}
        for tc in range(16):
            b = tc % 2
            for c in range(NCH):
                P.op("tensor", lambda e, c=c, tc=tc: e.transpose(
                    psn(c).bitcast(F32)[:, 0:128],
                    X(c)[:, 128 * tc: 128 * (tc + 1)], C_("ident_f")),
                    wait=(P.all_waits() if tc == 0 and c == 0 else
                          ([("ys", P.val("ys") - 1),
                            ("ve", vml.get(tc - 1, 0))] if c == 0 else [])),
                    inc=(["pe"] if c == NCH - 1 else []))
            for c in range(NCH):
                P.op("vector", lambda e, c=c, b=b: e.tensor_copy(
                    ostg[b][:, 128 * c: 128 * (c + 1)],
                    psn(c).bitcast(F32)[:, 0:128]),
                    wait=(["pe"] if c == 0 else []),
                    inc=(["ve"] if c == NCH - 1 else []))
            vml[tc] = P.val("ve")
            P.op("sync", lambda e, b=b, tc=tc: e.dma_start(
                out=out[128 * tc: 128 * (tc + 1), :], in_=ostg[b]),
                wait=["ve"], inc=[("dma", 16)])
        P.op("sync", lambda e: e.nop(), wait=["dma"])

    def tap(name):
        if name not in tap_outs:
            return
        P.op("sync", lambda e, name=name: e.dma_start(
            out=tap_outs[name][:, 0:AR_BYTES // 4], in_=ARH.ap()),
            wait=P.all_waits(), inc=[("dma", 16)])
        P.op("sync", lambda e, name=name: e.dma_start(
            out=tap_outs[name][:, 32256: 32256 + NCH * T], in_=XB.ap()),
            inc=[("dma", 16)])

    P.phase = "p0"
    p0()
    tap("x0")
    if "attn" in stages:
        P.phase = "ln1"
        ln(X)
        tap("ln1")
        P.phase = "attn"
        attn()
        tap("attn")
    if "ffn1" in stages:
        P.phase = "ln2"
        ln(X)
        P.phase = "ffn1"
        if SPARSE:
            ffn_sparse(False)
        else:
            ffn()
        tap("ffn1")
    if "mamba1" in stages:
        P.phase = "ln3"
        ln(X)
        P.phase = "mamba1"
        mamba("m1", xn_t)
        tap("mamba1")
    if "mamba2" in stages:
        P.phase = "mamba2"
        mamba("m2", xn_t)
        tap("mamba2")
    if "conv" in stages:
        P.phase = "conv"
        conv()
        tap("conv")
    if "ffn2" in stages:
        P.phase = "ln4"
        ln(X)
        P.phase = "ffn2"
        if SPARSE:
            ffn_sparse(True)
        else:
            ffn()
        tap("ffn2")
    if not (SPARSE and "ffn2" in stages):
        P.phase = "pout"
        pout()

    semh = {s: nc.alloc_semaphore(s) for s in Prog.SEMS}

    def run(eng_name, e):
        import contextlib
        cur = [None]
        stack = [None]

        def set_scope(ph):
            if ph == cur[0]:
                return
            if stack[0] is not None:
                stack[0].__exit__(None, None, None)
                stack[0] = None
            cur[0] = ph
            if ph is not None:
                stack[0] = nc.named_scope(ph)
                stack[0].__enter__()

        for opi, (eng, waits, fn, incs, phase) in enumerate(P.ops):
            if eng != eng_name:
                continue
            set_scope(phase)
            seen = set()
            for s, v in waits:
                if (s, v) in seen:
                    continue
                seen.add((s, v))
                e.wait_ge(semh[s], v)
            try:
                ins = fn(e)
            except Exception:
                print("FAILED op", opi, "eng", eng, "phase", phase)
                raise
            for s, n in incs:
                ins.then_inc(semh[s], n)
        set_scope(None)

    with nc.Block() as block:
        @block.sync
        def _(e):
            run("sync", e)

        @block.tensor
        def _(e):
            run("tensor", e)

        @block.vector
        def _(e):
            run("vector", e)

        @block.scalar
        def _(e):
            run("scalar", e)

        @block.gpsimd
        def _(e):
            run("gpsimd", e)

    return nc


def kernel(**inputs):
    g = host_prep(inputs)
    x = np.ascontiguousarray(inputs["L0_attn_a_x"], dtype=np.float32)
    nc = build_program()
    shared = {n: g[n] for n in SHARED_NAMES}
    shared.update({"c_" + n: g[n] for n in CONST_NAMES})
    in_maps = []
    for i in range(8):
        m = dict(shared)
        m["x"] = np.ascontiguousarray(x[i])
        in_maps.append(m)
    res = run_bass_kernel_spmd(nc, in_maps, core_ids=list(range(8)))
    return np.stack([r["out"] for r in res.results])
